# version 21
# baseline (speedup 1.0000x reference)
import contextlib
import numpy as np

import concourse.bass as bass
import concourse.mybir as mybir
from concourse.bass_utils import run_bass_kernel_spmd

F32 = mybir.dt.float32
I32 = mybir.dt.int32
BF16 = mybir.dt.bfloat16
OP = mybir.AluOpType
AF = mybir.ActivationFunctionType
AX = mybir.AxisListType

MAGIC = 12582912.0
D = 2048
KC = 16
H = 16
T = 1024
NS = 2
IC = 64
N_AU = 64
N_FU = 192
AU_PC = N_AU // 8
FU_PC = N_FU // 8


def build_nc():
    nc = bass.Bass("TRN2", target_bir_lowering=False)

    x_t = nc.dram_tensor("x_t", [D, T], F32, kind="ExternalInput").ap()
    w_attn = nc.dram_tensor("w_attn", [AU_PC, 128, 16, 128], F32, kind="ExternalInput").ap()
    w_ffn = nc.dram_tensor("w_ffn", [FU_PC, 128, 16, 128], F32, kind="ExternalInput").ap()
    bq_d = nc.dram_tensor("bq", [128, 16], F32, kind="ExternalInput").ap()
    bk_d = nc.dram_tensor("bk", [128, 16], F32, kind="ExternalInput").ap()
    bv_d = nc.dram_tensor("bv", [128, 16], F32, kind="ExternalInput").ap()
    bo_d = nc.dram_tensor("bo", [128, 16], F32, kind="ExternalInput").ap()
    wmq_d = nc.dram_tensor("wmq", [128, 64], F32, kind="ExternalInput").ap()
    wmk_d = nc.dram_tensor("wmk", [128, 64], F32, kind="ExternalInput").ap()
    y_t = nc.dram_tensor("y_t", [D, T], F32, kind="ExternalOutput").ap()

    ag_a_in = nc.dram_tensor("ag_a_in", [AU_PC, 128, 16, 128], BF16).ap()
    ag_a_out = nc.dram_tensor("ag_a_out", [N_AU, 128, 16, 128], BF16).ap()
    ag_f_in = nc.dram_tensor("ag_f_in", [FU_PC, 128, 16, 128], BF16).ap()
    ag_f_out1 = nc.dram_tensor("ag_f_out1", [64, 128, 16, 128], BF16).ap()
    ag_f_out2 = nc.dram_tensor("ag_f_out2", [64, 128, 16, 128], BF16).ap()
    ag_f_out3 = nc.dram_tensor("ag_f_out3", [64, 128, 16, 128], BF16).ap()
    cc_in = nc.dram_tensor("cc_in", [128, 1032], F32).ap()
    cc_out = nc.dram_tensor("cc_out", [128, 1032], F32).ap()

    es = contextlib.ExitStack()

    def sb(name, shape, dt):
        return es.enter_context(nc.sbuf_tensor(name, shape, dt))

    XF = sb("XF", [128, KC * T], F32)
    WK = sb("WK", [128, KC * T], BF16)
    ARENA = sb("ARENA", [128, 32768], BF16)
    QMF_O, KMT_O = 0, 8192
    WQB_O, WKB_O, WVB_O, WOB_O = 16384, 18432, 20480, 22528
    KF_O, QF_O, VF_O, VT_O, KVB_O, WST_O = 24576, 25088, 25600, 26112, 26368, 27400
    FUST = sb("FUST", [128, 5120], F32)
    FBB = FUST[:].bitcast(BF16)
    GW0, GW1, UW0, UW1, DW0 = 0, 2048, 4096, 6144, 8192
    YST = sb("YST", [128, 512], F32)
    XLN = sb("XLN", [128, 512], F32)
    HSB = sb("HSB", [128, 512], F32)
    KVS = sb("KVS", [128, 1032], F32)
    BC1 = sb("BC1", [128, 512], F32)
    BC2 = sb("BC2", [128, 512], F32)
    QA = sb("QA", [128, 512], F32)
    QB = sb("QB", [128, 512], F32)
    QE = sb("QE", [128, 512], F32)
    QC = sb("QC", [128, 512], I32)
    QD = sb("QD", [128, 512], I32)
    QG = sb("QG", [128, 512], BF16)
    QM = sb("QM", [128, 16], F32)
    QS = sb("QS", [128, 16], F32)
    QR = sb("QR", [128, 16], F32)
    BQ = sb("BQ", [128, 16], F32)
    BK = sb("BK", [128, 16], F32)
    BV = sb("BV", [128, 16], F32)
    BO = sb("BO", [128, 16], F32)
    WMQ = sb("WMQ", [128, 64], BF16)
    WMK = sb("WMK", [128, 64], BF16)
    ONESF = sb("ONESF", [128, 1], F32)
    ONESR = sb("ONESR", [1, 128], F32)
    ONESB = sb("ONESB", [128, 1], BF16)
    EPSB = sb("EPSB", [128, 1], F32)
    MBIAS = sb("MBIAS", [128, 3], F32)
    KMH = sb("KMH", [128, 128], F32)
    IDF = sb("IDF", [128, 128], F32)
    VFF = YST

    P1 = es.enter_context(nc.psum_tensor("P1", [128, 512], F32))
    P2 = es.enter_context(nc.psum_tensor("P2", [128, 512], F32))
    PG = es.enter_context(nc.psum_tensor("PG", [128, 512], F32))
    PU = es.enter_context(nc.psum_tensor("PU", [128, 512], F32))
    KVP = es.enter_context(nc.psum_tensor("KVP", [128, 1024], F32))
    S = es.enter_context(nc.psum_tensor("S", [128, 512], F32))
    PT = es.enter_context(nc.psum_tensor("PT", [128, 256], F32))

    semh = {}
    for s_ in ["a", "v", "t", "gc", "cc",
               "x", "bl", "wm", "aga", "agf0", "agf1", "agf2", "ccd",
               "ust0", "ust1",
               "lwq", "lwk", "lwv", "lwo", "gw0", "gw1", "uw0", "uw1",
               "dw", "yo"]:
        semh[s_] = es.enter_context(nc.semaphore("sem_" + s_))
    cnt = {k: 0 for k in semh}
    prog = []

    def emit(eng, fn, waits=(), inc=None, step=1):
        waits = [w for w in waits if w is not None]
        tok = None
        if inc is not None:
            cnt[inc] += step
            tok = (inc, cnt[inc])
        prog.append((eng, tuple(waits), fn, (inc, step)))
        return tok

    def vop(fn, waits=()):
        return emit("vector", fn, waits, "v")

    def aop(fn, waits=()):
        return emit("scalar", fn, waits, "a")

    def gop(fn, waits=()):
        return emit("gpsimd", fn, waits, "gc")

    def top(fn, waits=(), inc=None):
        return emit("tensor", fn, waits, inc)

    def dma(eng, sem, out, in_, waits=()):
        return emit(eng, lambda e, out=out, in_=in_: e.dma_start(out=out, in_=in_),
                    waits, sem, 16)

    res = {}

    def use(name, tok):
        res.setdefault(name, []).append(tok)

    def take(name):
        return res.pop(name, [])

    qprev = [None]

    def quant512(src, dst, extra_waits=(), pre_t=False):
        qa3 = (src if pre_t else QA[:]).rearrange("p (b k) -> p b k", k=32)
        qb3 = QB[:].rearrange("p (b k) -> p b k", k=32)
        qg3 = QG[:].rearrange("p (b k) -> p b k", k=32)
        s_b = QS[:].unsqueeze(-1).broadcast_to((128, 16, 32))
        r_b = QR[:].unsqueeze(-1).broadcast_to((128, 16, 32))
        if pre_t:
            first_w = list(extra_waits) + ([qprev[0]] if qprev[0] else [])
        else:
            w0 = vop(lambda e, src=src: e.transpose(out=QA[:], in_=src),
                     list(extra_waits) + ([qprev[0]] if qprev[0] else []))
            first_w = [w0]
        w1 = vop(lambda e, qa3=qa3: e.tensor_reduce(out=QM[:], in_=qa3, axis=AX.X,
                                                    op=OP.max, apply_absolute_value=True),
                 first_w)
        w2 = vop(lambda e: e.tensor_scalar(out=QS[:], in0=QM[:], scalar1=float(1.0 / 448.0),
                                           scalar2=1e-30, op0=OP.mult, op1=OP.max), [w1])
        w3 = vop(lambda e: e.reciprocal(out=QR[:], in_=QS[:]), [w2])
        w4 = vop(lambda e, qa3=qa3, qb3=qb3, r_b=r_b: e.tensor_tensor(
            out=qb3, in0=qa3, in1=r_b, op=OP.mult), [w3])
        w5 = vop(lambda e: e.tensor_scalar(out=QC[:], in0=QB[:].bitcast(I32), scalar1=0x7F800000,
                                           scalar2=0, op0=OP.bitwise_and, op1=OP.bypass), [w4])
        w6 = vop(lambda e: e.tensor_scalar(out=QD[:], in0=QC[:], scalar1=-25165824.0,
                                           scalar2=989855744.0, op0=OP.add, op1=OP.max), [w5])
        w7 = vop(lambda e: e.tensor_scalar(out=QE[:], in0=QC[:], scalar1=-1,
                                           scalar2=2155872256.0, op0=OP.mult, op1=OP.add), [w5])
        w8 = vop(lambda e: e.tensor_scalar(out=QC[:], in0=QE[:], scalar1=1140850688.0,
                                           scalar2=0, op0=OP.min, op1=OP.bypass), [w7, w6])
        w9 = vop(lambda e: e.tensor_tensor(out=QA[:], in0=QB[:], in1=QC[:].bitcast(F32),
                                           op=OP.mult), [w8])
        w13 = aop(lambda e: e.activation(out=QA[:], in_=QA[:], func=AF.Identity,
                                         bias=MBIAS[:, 0:1], scale=1.0), [w9, mb1, mb2, mb2b])
        w14 = aop(lambda e: e.activation(out=QA[:], in_=QA[:], func=AF.Identity,
                                         bias=MBIAS[:, 2:3], scale=1.0), [w13])
        wop = vop if pre_t else gop
        w15 = wop(lambda e: e.tensor_tensor(out=QB[:], in0=QA[:], in1=QD[:].bitcast(F32),
                                            op=OP.mult), [w14, w6])
        w16 = wop(lambda e, qb3=qb3, qg3=qg3, s_b=s_b: e.tensor_tensor(
            out=qg3, in0=qb3, in1=s_b, op=OP.mult), [w15])
        w17 = vop(lambda e, dst=dst: e.transpose(out=dst, in_=QG[:]), [w16])
        qprev[0] = w17
        return w17

    for k in range(KC):
        dma("scalar", "x", XF[:, k * T:(k + 1) * T], x_t[k * 128:(k + 1) * 128, :])
    x_ld = ("x", cnt["x"])
    dma("scalar", "bl", BQ[:, 0:16], bq_d)
    dma("scalar", "bl", BK[:, 0:16], bk_d)
    dma("scalar", "bl", BV[:, 0:16], bv_d)
    dma("scalar", "bl", BO[:, 0:16], bo_d)
    b_ld = ("bl", cnt["bl"])
    dma("gpsimd", "wm", WMQ[:], wmq_d)
    dma("gpsimd", "wm", WMK[:], wmk_d)
    wm_ld = ("wm", cnt["wm"])
    on1 = gop(lambda e: e.memset(ONESF[:], 1.0))
    on1b = gop(lambda e: e.memset(ONESR[:], 1.0))
    on2 = gop(lambda e: e.memset(ONESB[:], 1.0))
    mb1 = gop(lambda e: e.memset(MBIAS[:, 0:1], MAGIC))
    mb2 = gop(lambda e: e.memset(MBIAS[:, 1:2], -MAGIC / 512.0))
    mb2b = gop(lambda e: e.memset(MBIAS[:, 2:3], -MAGIC))
    eps1 = gop(lambda e: e.memset(EPSB[:], 1e-5))
    io1 = gop(lambda e: e.iota(HSB[:, 0:128], [[1, 128]], channel_multiplier=-1,
                               allow_small_or_imprecise_dtypes=True))
    id1 = vop(lambda e: e.tensor_scalar(out=IDF[:], in0=HSB[:, 0:128], scalar1=0.0,
                                        scalar2=0, op0=OP.is_equal, op1=OP.bypass), [io1])
    use("HSB", id1)

    st_prev = None
    for u in range(AU_PC):
        ld = dma("sync", "ust0", FUST[:, 0:2048].rearrange("p (k m) -> p k m", k=16),
                 w_attn[u], waits=take("UST0"))
        cv = vop(lambda e: e.tensor_copy(out=ARENA[:, WST_O:WST_O + 2048],
                                         in_=FUST[:, 0:2048]),
                 [ld] + take("WST"))
        use("UST0", cv)
        st = dma("sync", "aga", ag_a_in[u],
                 ARENA[:, WST_O:WST_O + 2048].rearrange("p (k m) -> p k m", k=16),
                 waits=[cv])
        use("WST", st)
        st_prev = st
    cc_a = emit("gpsimd", lambda e: e.collective_compute(
        "AllGather", OP.bypass, replica_groups=[list(range(8))],
        ins=[ag_a_in.opt()], outs=[ag_a_out.opt()]), [st_prev], "cc")

    ffn_lds = {}
    cc_fs = []

    def load_ffn_unit(u):
        o = 2048 * (u % 2)
        ffn_lds[u] = dma("sync", "ust%d" % (u % 2),
                         FUST[:, o:o + 2048].rearrange("p (k m) -> p k m", k=16),
                         w_ffn[u], waits=take("UST%d" % (u % 2)))

    def prep_ffn_unit(u):
        if u not in ffn_lds:
            load_ffn_unit(u)
        ld = ffn_lds[u]
        o = 2048 * (u % 2)
        wq = None
        for j in range(4):
            wq = quant512(FUST[:, o + j * 512:o + (j + 1) * 512],
                          ARENA[:, WST_O + j * 512:WST_O + (j + 1) * 512],
                          extra_waits=([ld] + take("WST")) if j == 0 else (),
                          pre_t=True)
            use("UST%d" % (u % 2), wq)
        st = dma("sync", "agf%d" % (u // 8), ag_f_in[u],
                 ARENA[:, WST_O:WST_O + 2048].rearrange("p (k m) -> p k m", k=16),
                 waits=[wq])
        use("WST", st)
        if u % 8 == 7 and u // 8 < 2:
            cc_fs.append(emit("gpsimd", lambda e, u=u: e.collective_compute(
                "AllGather", OP.bypass, replica_groups=[list(range(8))],
                ins=[ag_f_in[u - 7:u + 1].opt()],
                outs=[[ag_f_out1, ag_f_out2][u // 8].opt()]),
                [st], "cc"))
        return st

    def ln_setup(s, first_waits, bc_ps=None):
        pa, pan, pb, pbn = bc_ps or (P1, "P1", P2, "P2")
        sl = slice(s * 512, (s + 1) * 512)
        for k in range(KC):
            top(lambda e, k=k, sl=sl: e.matmul(
                out=S[0:1, 0:512], lhsT=ONESF[:],
                rhs=XF[:, k * T:k * T + T][:, sl],
                start=(k == 0), stop=(k == KC - 1)),
                waits=(list(first_waits) + [x_ld, on1, on1b] + take("S_stat")) if k == 0 else ())
        for k in range(KC):
            sq = aop(lambda e, k=k, sl=sl: e.activation(
                out=HSB[:], in_=XF[:, k * T:k * T + T][:, sl], func=AF.Square),
                take("HSB") + ((take("PU") + list(first_waits) + [x_ld]) if k == 0 else []))
            tk = top(lambda e, k=k: e.matmul(
                out=PU[0:1, 0:512], lhsT=ONESF[:], rhs=HSB[:],
                start=(k == 0), stop=(k == KC - 1)), waits=[sq], inc="t")
            use("HSB", tk)
        ts1 = aop(lambda e: e.copy(out=XLN[0:1, 0:512], in_=S[0:1, 0:512]),
                  [tk] + take("XLN_q"))
        ts2 = aop(lambda e: e.copy(out=BC2[0:1, 0:512], in_=PU[0:1, 0:512]),
                  [tk] + take("BC2"))
        use("S_stat", ts1)
        use("PU", ts2)
        b1 = top(lambda e, pa=pa: e.matmul(out=pa[0:128, 0:512], lhsT=ONESR[0:1, :],
                                           rhs=XLN[0:1, 0:512],
                                    start=True, stop=True), waits=[ts1] + take(pan), inc="t")
        use("XLN_q", b1)
        b2 = top(lambda e, pb=pb: e.matmul(out=pb[0:128, 0:512], lhsT=ONESR[0:1, :],
                                           rhs=BC2[0:1, 0:512],
                                    start=True, stop=True), waits=[ts2] + take(pbn), inc="t")
        use("BC2_row", b2)
        m1 = vop(lambda e, pa=pa: e.tensor_scalar(out=BC1[:], in0=pa[0:128, 0:512],
                                           scalar1=float(1.0 / D),
                                           scalar2=0.0, op0=OP.mult, op1=OP.add),
                 [b1] + take("BC1"))
        use(pan, m1)
        use("S_stat", m1)
        m2 = vop(lambda e, pb=pb: e.tensor_scalar(out=BC2[:], in0=pb[0:128, 0:512],
                                           scalar1=float(1.0 / D),
                                           scalar2=0.0, op0=OP.mult, op1=OP.add),
                 [b2] + take("BC2") + take("BC2_row"))
        use(pbn, m2)
        m3 = vop(lambda e: e.tensor_tensor(out=HSB[:], in0=BC1[:], in1=BC1[:], op=OP.mult),
                 [m1] + take("HSB"))
        m4 = vop(lambda e: e.tensor_tensor(out=XLN[:], in0=BC2[:], in1=HSB[:], op=OP.subtract),
                 [m3, m2] + take("XLN_q"))
        use("HSB", m4)
        m6 = aop(lambda e: e.activation(out=BC2[:], in_=XLN[:], func=AF.Sqrt,
                                        bias=EPSB[:, 0:1], scale=1.0), [m4, eps1])
        m7 = vop(lambda e: e.reciprocal(out=XLN[:], in_=BC2[:]), [m6])
        m8 = vop(lambda e: e.tensor_tensor(out=BC2[:], in0=BC1[:], in1=XLN[:], op=OP.mult), [m7])
        m9 = aop(lambda e: e.copy(out=BC1[:], in_=XLN[:]), [m7, m8])
        use("XLN_q", m9)
        use("XLN_q", m8)
        m10 = vop(lambda e: e.engine_nop(), [m8, m9])
        return m10

    def ln_apply(s, k, dst, ln_tok, dst_waits=()):
        sl = slice(s * 512, (s + 1) * 512)
        h1 = vop(lambda e, k=k, sl=sl: e.tensor_tensor(
            out=HSB[:], in0=XF[:, k * T:k * T + T][:, sl], in1=BC1[:], op=OP.mult),
            [ln_tok] + take("HSB") + list(dst_waits))
        h2 = vop(lambda e, dst=dst: e.tensor_tensor(out=dst, in0=HSB[:], in1=BC2[:],
                                                    op=OP.subtract), [h1])
        use("HSB", h2)
        use("BC1", h1)
        use("BC2", h2)
        return h2

    ln1_last = None
    for s in range(NS):
        tok = ln_setup(s, [])
        use("BC", tok)
        for k in range(KC):
            ln1_last = ln_apply(s, k, WK[:, k * T + s * 512:k * T + (s + 1) * 512], tok)
    ln1_all = ln1_last

    def au(mat, m):
        return {"wq": 0, "wk": 16, "wv": 32, "wo": 48}[mat] + m

    blocks = [(m, s) for m in range(H) for s in range(NS)]
    ablk = {}

    def emit_A(m, s):
        d = {}
        lq, lk, lv = ablk[(m, "loads")]
        qfo = 29448 if s == 0 else 29960
        kfo = KF_O if s == 0 else KF_O + 512
        vff = YST if s == 0 else XLN
        sl = slice(s * 512, (s + 1) * 512)
        for k in range(KC):
            tq = top(lambda e, k=k, sl=sl: e.matmul(
                out=P1[:], lhsT=ARENA[:, WQB_O + k * 128:WQB_O + (k + 1) * 128],
                rhs=WK[:, k * T:k * T + T][:, sl], start=(k == 0), stop=(k == KC - 1)),
                waits=([lq, ln1_all, b_ld] + take("P1")) if k == 0 else (),
                inc="t" if k == KC - 1 else None)
        if s == NS - 1:
            use("WQB", tq)
        eq = aop(lambda e, m=m, qfo=qfo: e.activation(out=ARENA[:, qfo:qfo + 512], in_=P1[:],
                                             func=AF.Identity, bias=BQ[:, m:m + 1],
                                             scale=1.0), [tq] + take("QF%d" % s))
        use("P1", eq)
        for k in range(KC):
            tk = top(lambda e, k=k, sl=sl: e.matmul(
                out=P2[:], lhsT=ARENA[:, WKB_O + k * 128:WKB_O + (k + 1) * 128],
                rhs=WK[:, k * T:k * T + T][:, sl], start=(k == 0), stop=(k == KC - 1)),
                waits=([lk] + take("P2")) if k == 0 else (),
                inc="t" if k == KC - 1 else None)
        if s == NS - 1:
            use("WKB", tk)
        ek = aop(lambda e, m=m, kfo=kfo: e.activation(out=ARENA[:, kfo:kfo + 512], in_=P2[:],
                                             func=AF.Identity, bias=BK[:, m:m + 1],
                                             scale=1.0), [tk] + take("KF%d" % s))
        use("P2", ek)
        for k in range(KC):
            tv = top(lambda e, k=k, sl=sl: e.matmul(
                out=PG[:], lhsT=ARENA[:, WVB_O + k * 128:WVB_O + (k + 1) * 128],
                rhs=WK[:, k * T:k * T + T][:, sl], start=(k == 0), stop=(k == KC - 1)),
                waits=([lv] + take("PG")) if k == 0 else (),
                inc="t" if k == KC - 1 else None)
        if s == NS - 1:
            use("WVB", tv)
        ev = aop(lambda e, m=m, vff=vff: e.activation(out=vff[0:128, 0:512], in_=PG[:],
                                             func=AF.Identity, bias=BV[:, m:m + 1],
                                             scale=1.0),
                 [tv] + take("VF%d" % s) + (take("XLN_q") if s == 1 else []))
        use("PG", ev)
        d.update(eq=eq, ek=ek, ev=ev)
        return d

    def emit_B(m, s, d):
        eq, ek, ev = d["eq"], d["ek"], d["ev"]
        qfo = 29448 if s == 0 else 29960
        kfo = KF_O if s == 0 else KF_O + 512
        vff = YST if s == 0 else XLN
        qm = top(lambda e, qfo=qfo: e.matmul(out=PU[0:64, 0:512], lhsT=WMQ[:],
                                    rhs=ARENA[:, qfo:qfo + 512], start=True, stop=True),
                 waits=[eq, wm_ld] + take("PU"), inc="t")
        use("QF%d" % s, qm)
        e1 = aop(lambda e: e.activation(out=HSB[0:64, :], in_=PU[0:64, 0:512],
                                        func=AF.Relu, scale=-1.0),
                 [qm] + take("HSB"))
        e2 = aop(lambda e: e.activation(out=HSB[0:64, :], in_=HSB[0:64, :], func=AF.Exp,
                                        scale=-1.0), [e1])
        qmf_dst = ARENA[64 * (m % 2):64 * (m % 2) + 64,
                        QMF_O + (m // 2) * 1024 + s * 512:QMF_O + (m // 2) * 1024 + (s + 1) * 512]
        e3 = vop(lambda e, qmf_dst=qmf_dst: e.scalar_tensor_tensor(
            out=qmf_dst, in0=PU[0:64, 0:512], scalar=0.0, in1=HSB[0:64, :],
            op0=OP.max, op1=OP.add), [e2])
        use("HSB", e3)
        use("PU", e3)
        for j in range(4):
            tau = s * 4 + j
            pt_sl = slice(128 * (j % 2), 128 * (j % 2) + 128)
            vt_sl = slice(VT_O + 128 * (j % 2), VT_O + 128 * (j % 2) + 128)
            kmp = PU if (j % 2 == 0) else PG
            kmpn = "PU" if (j % 2 == 0) else "PG"
            hsl = slice(64 * (j % 2), 64 * (j % 2) + 64)
            km = top(lambda e, j=j, kmp=kmp, kfo=kfo: e.matmul(
                out=kmp[0:128, 0:64], lhsT=ARENA[:, kfo + j * 128:kfo + (j + 1) * 128],
                rhs=WMK[:], start=True, stop=True),
                waits=[ek, wm_ld] + take(kmpn) + ([e3] if j == 0 else []) +
                      ([ev] if j == 1 else []), inc="t")
            use("KF%d" % s, km)
            f1 = aop(lambda e, kmp=kmp, hsl=hsl: e.activation(
                out=KMH[:, hsl], in_=kmp[0:128, 0:64], func=AF.Relu, scale=-1.0),
                [km] + take("KMH%d" % (j % 2)))
            f2 = aop(lambda e, hsl=hsl: e.activation(out=KMH[:, hsl], in_=KMH[:, hsl],
                                                     func=AF.Exp, scale=-1.0), [f1])
            kmt_dst = ARENA[:, KMT_O + tau * 1024 + m * 64:KMT_O + tau * 1024 + (m + 1) * 64]
            f3 = vop(lambda e, kmt_dst=kmt_dst, kmp=kmp, hsl=hsl: e.scalar_tensor_tensor(
                out=kmt_dst, in0=kmp[0:128, 0:64], scalar=0.0, in1=KMH[:, hsl],
                op0=OP.max, op1=OP.add), [f2])
            use("KMH%d" % (j % 2), f3)
            use(kmpn, f3)
            vt = top(lambda e, j=j, pt_sl=pt_sl, vff=vff: e.transpose(
                out=PT[:, pt_sl], in_=vff[0:128, j * 128:(j + 1) * 128],
                identity=IDF[:]), waits=[ev, id1] + take("PT%d" % (j % 2)), inc="t")
            use("VF%d" % s, vt)
            if s == 1:
                use("XLN_q", vt)
            g1 = aop(lambda e, pt_sl=pt_sl, vt_sl=vt_sl: e.copy(
                out=ARENA[:, vt_sl], in_=PT[:, pt_sl]),
                [vt] + take("VT%d" % (j % 2)))
            use("PT%d" % (j % 2), g1)
            base = 64 * (m % 2)
            kv_start = (m % 8 in (0, 1)) and tau == 0
            kv_stop = (m % 8 in (6, 7)) and tau == NS * 4 - 1
            kv_last = top(lambda e, base=base, m=m, kmt_dst=kmt_dst, vt_sl=vt_sl,
                          kv_start=kv_start, kv_stop=kv_stop: e.matmul(
                out=KVP[base:base + 64, (m // 2) * 128:(m // 2) * 128 + 128],
                lhsT=kmt_dst, rhs=ARENA[:, vt_sl],
                start=kv_start, stop=kv_stop, skip_group_check=True),
                waits=[f3, g1] + (take("KVP") if (m == 0 and tau == 0) else []),
                inc="t")
            use("VT%d" % (j % 2), kv_last)
            ks_start = m in (0, 1) and tau == 0
            ks_stop = m in (H - 2, H - 1) and tau == NS * 4 - 1
            ks_tok = top(lambda e, base=base, m=m, kmt_dst=kmt_dst,
                         ks_start=ks_start, ks_stop=ks_stop: e.matmul(
                out=S[base:base + 64, (m // 2):(m // 2) + 1],
                lhsT=kmt_dst, rhs=ONESB[:],
                start=ks_start, stop=ks_stop, skip_group_check=True),
                waits=[on2] if (m in (0, 1) and tau == 0) else (),
                inc="t" if (m == H - 1 and tau == NS * 4 - 1) else None)
        return ks_tok

    pend = None
    for i, (m, s) in enumerate(blocks):
        if s == 0:
            if m > 0:
                us = list(range((m - 1) * FU_PC // H, m * FU_PC // H))
            else:
                us = []
            qh = [us[:(len(us) + 1) // 2], us[(len(us) + 1) // 2:]]
            ablk[(m, "qh")] = qh
            lq = dma("sync", "lwq", ARENA[:, WQB_O:WQB_O + 2048],
                     ag_a_out[au("wq", m)].rearrange("p k c -> p (k c)"),
                     waits=([cc_a] if m == 0 else []) + take("WQB"))
            lk = dma("sync", "lwk", ARENA[:, WKB_O:WKB_O + 2048],
                     ag_a_out[au("wk", m)].rearrange("p k c -> p (k c)"), waits=take("WKB"))
            lv = dma("sync", "lwv", ARENA[:, WVB_O:WVB_O + 2048],
                     ag_a_out[au("wv", m)].rearrange("p k c -> p (k c)"), waits=take("WVB"))
            ablk[(m, "loads")] = (lq, lk, lv)
        for u in ablk[(m, "qh")][s]:
            st_f = prep_ffn_unit(u)
        ablk[(m, s)] = emit_A(m, s)
        if pend is not None:
            ks_tok = emit_B(*pend, ablk[pend])
        pend = (m, s)
        if s == NS - 1:
            for u in range(m * FU_PC // H, (m + 1) * FU_PC // H):
                load_ffn_unit(u)
    ks_tok = emit_B(*pend, ablk[pend])
    for u in range((H - 1) * FU_PC // H, FU_PC):
        st_f = prep_ffn_unit(u)
    kv_done = ks_tok

    kvev = aop(lambda e: e.copy(out=KVS[:, 0:1024], in_=KVP[:]), [kv_done])
    ksev = vop(lambda e: e.tensor_scalar(out=KVS[:, 1024:1032], in0=S[:, 0:8], scalar1=1.0,
                                         scalar2=0.0, op0=OP.mult, op1=OP.add), [kv_done])
    use("KVP", kvev)
    use("S_stat", ksev)
    st1 = dma("sync", "ccd", cc_in, KVS[:], waits=[kvev, ksev])
    ar = emit("gpsimd", lambda e: e.collective_compute(
        "AllReduce", OP.add, replica_groups=[[0, 1, 2, 3], [4, 5, 6, 7]],
        ins=[cc_in.opt()], outs=[cc_out.opt()]), [st1], "cc")
    ld1 = dma("sync", "ccd", KVS[:], cc_out, waits=[ar])
    kvb = vop(lambda e: e.tensor_copy(out=ARENA[:, KVB_O:KVB_O + 1032], in_=KVS[:]), [ld1])
    cc_fs.append(emit("gpsimd", lambda e: e.collective_compute(
        "AllGather", OP.bypass, replica_groups=[list(range(8))],
        ins=[ag_f_in[16:24].opt()], outs=[ag_f_out3.opt()]), [st_f], "cc"))

    for m in range(H):
        base = 64 * (m % 2)
        for s in range(NS):
            qmf_src = ARENA[base:base + 64,
                            QMF_O + (m // 2) * 1024 + s * 512:QMF_O + (m // 2) * 1024 + (s + 1) * 512]
            nm = top(lambda e, base=base, m=m, qmf_src=qmf_src: e.matmul(
                out=PU[0:1, 0:512],
                lhsT=ARENA[base:base + 64, KVB_O + 1024 + (m // 2):KVB_O + 1024 + (m // 2) + 1],
                rhs=qmf_src, start=True, stop=True),
                waits=[kvb] + take("PU"), inc="t")
            n1 = vop(lambda e: e.tensor_scalar(out=XLN[0:1, 0:512], in0=PU[0:1, 0:512], scalar1=1e-6,
                                               scalar2=0.0, op0=OP.add, op1=OP.bypass),
                     [nm] + take("NR1") + take("XLN_q"))
            use("PU", n1)
            n2 = vop(lambda e: e.reciprocal(out=YST[0:1, 0:512], in_=XLN[0:1, 0:512]), [n1])
            use("XLN_q", n2)
            nb = top(lambda e: e.matmul(out=P2[:], lhsT=ONESR[:], rhs=YST[0:1, 0:512],
                                        start=True, stop=True),
                     waits=[n2] + take("P2"), inc="t")
            use("NR1", nb)
            n3 = aop(lambda e: e.copy(out=BC1[:], in_=P2[:]), [nb] + take("BC1") + take("BC"))
            use("P2", n3)
            oh = top(lambda e, base=base, m=m, qmf_src=qmf_src: e.matmul(
                out=P1[:],
                lhsT=ARENA[base:base + 64, KVB_O + (m // 2) * 128:KVB_O + (m // 2) * 128 + 128],
                rhs=qmf_src, start=True, stop=True),
                waits=take("P1"), inc="t")
            o1 = vop(lambda e, m=m, s=s: e.tensor_tensor(
                out=WK[:, m * T + s * 512:m * T + (s + 1) * 512],
                in0=P1[:], in1=BC1[:], op=OP.mult), [oh, n3])
            use("P1", o1)
            use("BC1", o1)
    attn_f = o1

    for m in range(H):
        lo = dma("sync", "lwo", ARENA[:, WOB_O:WOB_O + 2048],
                 ag_a_out[au("wo", m)].rearrange("p k c -> p (k c)"), waits=take("WOB"))
        for s in range(NS):
            sl = slice(s * 512, (s + 1) * 512)
            for k in range(KC):
                tw = top(lambda e, k=k, sl=sl: e.matmul(
                    out=P1[:], lhsT=ARENA[:, WOB_O + k * 128:WOB_O + (k + 1) * 128],
                    rhs=WK[:, k * T:k * T + T][:, sl], start=(k == 0), stop=(k == KC - 1)),
                    waits=([lo, attn_f] + take("P1")) if k == 0 else (),
                    inc="t" if k == KC - 1 else None)
            use("WOB", tw)
            a1 = aop(lambda e, m=m: e.activation(out=HSB[:], in_=P1[:], func=AF.Identity,
                                                 bias=BO[:, m:m + 1], scale=1.0),
                     [tw] + take("HSB"))
            use("P1", a1)
            a2 = vop(lambda e, m=m, sl=sl: e.tensor_tensor(
                out=XF[:, m * T:m * T + T][:, sl], in0=XF[:, m * T:m * T + T][:, sl],
                in1=HSB[:], op=OP.add), [a1])
            use("HSB", a2)
    attn_done = a2

    gbuf = [GW0, GW1]
    ubuf = [UW0, UW1]

    def emit_deqx(s, k, tok):
        l2 = ln_apply(s, k, XLN[:], tok, take("XLN_q"))
        dq = quant512(XLN[:], WK[:, k * T + s * 512:k * T + (s + 1) * 512],
                      extra_waits=[l2] + ([attn_done, st_f] if (s == 0 and k == 0) else []))
        use("XLN_q", dq)
        return dq

    tok0 = ln_setup(0, [attn_done])
    deqx_last = None
    for k in range(KC):
        deqx_last = emit_deqx(0, k, tok0)
    for s in range(NS):
        sl = slice(s * 512, (s + 1) * 512)
        for c in range(IC):
            gb, ub = gbuf[c % 2], ubuf[c % 2]
            lg = dma("sync", "gw%d" % (c % 2), FBB[:, gb:gb + 2048],
                     ag_f_out1[(c % 8) * 8 + c // 8].rearrange("p k c2 -> p (k c2)"),
                     waits=([cc_fs[0], deqx_last] if (s == 0 and c < 2) else []) + take("GB%d" % (c % 2)))
            lu = dma("sync", "uw%d" % (c % 2), FBB[:, ub:ub + 2048],
                     ag_f_out2[(c % 8) * 8 + c // 8].rearrange("p k c2 -> p (k c2)"),
                     waits=([cc_fs[1]] if (s == 0 and c < 2) else []) + take("UB%d" % (c % 2)))
            for k in range(KC):
                tg = top(lambda e, k=k, sl=sl, gb=gb: e.matmul(
                    out=PG[:], lhsT=FBB[:, gb + k * 128:gb + (k + 1) * 128],
                    rhs=WK[:, k * T:k * T + T][:, sl], start=(k == 0), stop=(k == KC - 1)),
                    waits=([lg, deqx_last] + take("PG")) if k == 0 else (),
                    inc="t" if k == KC - 1 else None)
            use("GB%d" % (c % 2), tg)
            for k in range(KC):
                tu = top(lambda e, k=k, sl=sl, ub=ub: e.matmul(
                    out=PU[:], lhsT=FBB[:, ub + k * 128:ub + (k + 1) * 128],
                    rhs=WK[:, k * T:k * T + T][:, sl], start=(k == 0), stop=(k == KC - 1)),
                    waits=([lu] + take("PU")) if k == 0 else (),
                    inc="t" if k == KC - 1 else None)
            use("UB%d" % (c % 2), tu)
            s1 = aop(lambda e: e.activation(out=HSB[:], in_=PG[:], func=AF.Silu),
                     [tg] + take("HSB"))
            use("PG", s1)
            s2 = vop(lambda e: e.tensor_tensor(out=XLN[:], in0=HSB[:], in1=PU[:], op=OP.mult),
                     [s1, tu] + take("XLN_q"))
            use("HSB", s2)
            use("PU", s2)
            hq = quant512(XLN[:], ARENA[:, c * 512:(c + 1) * 512],
                          extra_waits=[s2] + ([st_f] if (s == 0 and c == 0) else []) +
                          (take("DEQH") if c == 0 else []))
            use("XLN_q", hq)
        deqh_done = hq
        if s + 1 < NS:
            tok_n = ln_setup(s + 1, [], bc_ps=(PG, "PG", PU, "PU"))
        dwb_bufs = [DW0, GW0]
        for m in range(H):
            pd = P1 if (m % 2 == 0) else P2
            pname = "P1" if (m % 2 == 0) else "P2"
            for cg in range(4):
                dpar = (m * 4 + cg) % 2
                dwo = dwb_bufs[dpar]
                dsem = "dw" if dpar == 0 else "gw0"
                g_dw = m * 4 + cg
                ldw = dma("sync", dsem, FBB[:, dwo:dwo + 2048],
                          ag_f_out3[(g_dw % 8) * 8 + g_dw // 8].rearrange("p k c2 -> p (k c2)"),
                          waits=([deqh_done] if (m == 0 and cg == 0) else []) +
                                ([cc_fs[2]] if (s == 0 and m == 0 and cg < 2) else []) +
                                take("DWB%d" % dpar) + (take("GB0") if (m == 0 and cg == 1) else []))
                for cl in range(16):
                    c = cg * 16 + cl
                    td = top(lambda e, cl=cl, c=c, pd=pd, dwo=dwo: e.matmul(
                        out=pd[:], lhsT=FBB[:, dwo + cl * 128:dwo + (cl + 1) * 128],
                        rhs=ARENA[:, c * 512:(c + 1) * 512],
                        start=(c == 0), stop=(c == IC - 1)),
                        waits=([ldw] + (take(pname) if cg == 0 else [])) if cl == 0 else (),
                        inc="t" if (cl == 15) else None)
                use("DWB%d" % dpar, td)
            yv = vop(lambda e, m=m, sl=sl, pd=pd: e.tensor_tensor(
                out=YST[:], in0=pd[:], in1=XF[:, m * T:m * T + T][:, sl], op=OP.add),
                [td] + take("YST"))
            use(pname, yv)
            yst = dma("sync", "yo", y_t[m * 128:(m + 1) * 128, sl], YST[:], waits=[yv])
            use("YST", yst)
            if s + 1 < NS:
                deqx_last = emit_deqx(s + 1, m, tok_n)
        if s == 0:
            use("DEQH", td)

    final_tok = yst
    prog.append(("sync", (final_tok,), None, (None, 0)))

    eng_prog = {k: [] for k in ["sync", "scalar", "vector", "tensor", "gpsimd"]}
    for eng, waits, fn, inc in prog:
        eng_prog[eng].append((waits, fn, inc))

    with nc.Block() as block:
        def make_section(entries):
            def body(e):
                for waits, fn, (incname, step) in entries:
                    for (sname, val) in waits:
                        e.wait_ge(semh[sname], val)
                    if fn is not None:
                        inst = fn(e)
                        if incname is not None:
                            inst.then_inc(semh[incname], step)
            return body

        block.sync(make_section(eng_prog["sync"]))
        block.scalar(make_section(eng_prog["scalar"]))
        block.vector(make_section(eng_prog["vector"]))
        block.tensor(make_section(eng_prog["tensor"]))
        block.gpsimd(make_section(eng_prog["gpsimd"]))

    es.close()
    return nc


def _prep_inputs(ins):
    f32 = np.float32
    x = np.ascontiguousarray(ins["L0_attn_hidden_states"], dtype=f32).reshape(8192, D)
    wq = ins["L0_attn_wq"]
    wk = ins["L0_attn_wk"]
    wv = ins["L0_attn_wv"]
    wo = ins["L0_attn_wo"]

    def tile_w(w, rows):
        return np.ascontiguousarray(
            w.reshape(rows // 128, 128, w.shape[1] // 128, 128).transpose(0, 3, 2, 1),
            dtype=f32)

    wq_t = tile_w(wq, D)
    wk_t = tile_w(wk, D)
    wv_t = tile_w(wv, D)
    wo_t = tile_w(wo, D)
    attn_units = np.concatenate([wq_t, wk_t, wv_t, wo_t], axis=0)
    def bt32(a):
        n = a.shape[0]
        b = a.reshape(n, 4, 32, 16, 4, 32)
        return np.ascontiguousarray(b.transpose(0, 1, 5, 3, 4, 2)).reshape(a.shape)

    gw_t = tile_w(ins["L0_ffn_gate_w"], 8192)
    uw_t = tile_w(ins["L0_ffn_up_w"], 8192)
    dw_t = tile_w(ins["L0_ffn_down_w"], D)
    dw_u = np.ascontiguousarray(
        dw_t.reshape(16, 128, 4, 16, 128).transpose(0, 2, 1, 3, 4)).reshape(64, 128, 16, 128)
    ffn_units = bt32(np.concatenate([gw_t, uw_t, dw_u], axis=0))

    in_maps = []
    for c in range(8):
        xs = np.ascontiguousarray(x[c * T:(c + 1) * T].T)
        in_maps.append({
            "x_t": xs,
            "w_attn": np.ascontiguousarray(attn_units[c * AU_PC:(c + 1) * AU_PC]),
            "w_ffn": np.ascontiguousarray(ffn_units[c::8]),
            "bq": np.ascontiguousarray(ins["L0_attn_bq"], dtype=f32).reshape(16, 128).T.copy(),
            "bk": np.ascontiguousarray(ins["L0_attn_bk"], dtype=f32).reshape(16, 128).T.copy(),
            "bv": np.ascontiguousarray(ins["L0_attn_bv"], dtype=f32).reshape(16, 128).T.copy(),
            "bo": np.ascontiguousarray(ins["L0_attn_bo"], dtype=f32).reshape(16, 128).T.copy(),
            "wmq": np.ascontiguousarray(ins["L0_attn_wmap_q"].T, dtype=f32),
            "wmk": np.ascontiguousarray(ins["L0_attn_wmap_k"].T, dtype=f32),
        })
    return in_maps


def kernel(**inputs):
    in_maps = _prep_inputs(inputs)
    nc = build_nc()
    res_ = run_bass_kernel_spmd(nc, in_maps, core_ids=list(range(8)))
    outs = []
    for c in range(8):
        yt = np.asarray(res_.results[c]["y_t"])
        outs.append(np.ascontiguousarray(yt.T))
    y = np.concatenate(outs, axis=0).reshape(2, 4096, D).astype(np.float32)
    return y


if __name__ == "__main__":
    import reference as _r
    ins = _r.setup_inputs()
    out = kernel(**{k: np.asarray(v) for k, v in ins.items()})
    ref = np.asarray(_r.reference(**ins))
    err = np.abs(out - ref)
    print("max abs err:", err.max(), "rms rel:", float(np.sqrt((err ** 2).mean() / (ref ** 2).mean())))


# version 22
# speedup vs baseline: 1.0465x; 1.0465x over previous
import contextlib
import numpy as np

import concourse.bass as bass
import concourse.mybir as mybir
from concourse.bass_utils import run_bass_kernel_spmd

F32 = mybir.dt.float32
I32 = mybir.dt.int32
BF16 = mybir.dt.bfloat16
OP = mybir.AluOpType
AF = mybir.ActivationFunctionType
AX = mybir.AxisListType

MAGIC = 12582912.0
D = 2048
KC = 16
H = 16
T = 1024
NS = 2
IC = 64
N_AU = 64
N_FU = 192
AU_PC = N_AU // 8
FU_PC = N_FU // 8


def build_nc():
    nc = bass.Bass("TRN2", target_bir_lowering=False)

    x_t = nc.dram_tensor("x_t", [D, T], F32, kind="ExternalInput").ap()
    w_attn = nc.dram_tensor("w_attn", [AU_PC, 128, 16, 128], F32, kind="ExternalInput").ap()
    w_ffn = nc.dram_tensor("w_ffn", [FU_PC, 128, 16, 128], F32, kind="ExternalInput").ap()
    bq_d = nc.dram_tensor("bq", [128, 16], F32, kind="ExternalInput").ap()
    bk_d = nc.dram_tensor("bk", [128, 16], F32, kind="ExternalInput").ap()
    bv_d = nc.dram_tensor("bv", [128, 16], F32, kind="ExternalInput").ap()
    bo_d = nc.dram_tensor("bo", [128, 16], F32, kind="ExternalInput").ap()
    wmq_d = nc.dram_tensor("wmq", [128, 64], F32, kind="ExternalInput").ap()
    wmk_d = nc.dram_tensor("wmk", [128, 64], F32, kind="ExternalInput").ap()
    y_t = nc.dram_tensor("y_t", [D, T], F32, kind="ExternalOutput").ap()

    ag_a_in = nc.dram_tensor("ag_a_in", [AU_PC, 128, 16, 128], BF16).ap()
    ag_a_out = nc.dram_tensor("ag_a_out", [N_AU, 128, 16, 128], BF16).ap()
    ag_f_in = nc.dram_tensor("ag_f_in", [FU_PC, 128, 16, 128], BF16).ap()
    ag_f_out1 = nc.dram_tensor("ag_f_out1", [64, 128, 16, 128], BF16).ap()
    ag_f_out2 = nc.dram_tensor("ag_f_out2", [64, 128, 16, 128], BF16).ap()
    ag_f_out3 = nc.dram_tensor("ag_f_out3", [64, 128, 16, 128], BF16).ap()
    cc_in = nc.dram_tensor("cc_in", [128, 1032], F32).ap()
    cc_out = nc.dram_tensor("cc_out", [128, 1032], F32).ap()

    es = contextlib.ExitStack()

    def sb(name, shape, dt):
        return es.enter_context(nc.sbuf_tensor(name, shape, dt))

    XF = sb("XF", [128, KC * T], F32)
    WK = sb("WK", [128, KC * T], BF16)
    ARENA = sb("ARENA", [128, 32768], BF16)
    QMF_O, KMT_O = 0, 8192
    WQB_O, WKB_O, WVB_O, WOB_O = 16384, 18432, 20480, 22528
    KF_O, QF_O, VF_O, VT_O, KVB_O, WST_O = 24576, 25088, 25600, 26112, 26368, 27400
    FUST = sb("FUST", [128, 5120], F32)
    FBB = FUST[:].bitcast(BF16)
    GW0, GW1, UW0, UW1, DW0 = 0, 2048, 4096, 6144, 8192
    YST = sb("YST", [128, 512], F32)
    XLN = sb("XLN", [128, 512], F32)
    HSB = sb("HSB", [128, 512], F32)
    KVS = sb("KVS", [128, 1032], F32)
    BC1 = sb("BC1", [128, 512], F32)
    BC2 = sb("BC2", [128, 512], F32)
    QA = sb("QA", [128, 512], F32)
    QB = sb("QB", [128, 512], F32)
    QE = sb("QE", [128, 512], F32)
    QC = sb("QC", [128, 512], I32)
    QD = sb("QD", [128, 512], I32)
    QG = sb("QG", [128, 512], BF16)
    QM = sb("QM", [128, 16], F32)
    QS = sb("QS", [128, 16], F32)
    QR = sb("QR", [128, 16], F32)
    BQ = sb("BQ", [128, 16], F32)
    BK = sb("BK", [128, 16], F32)
    BV = sb("BV", [128, 16], F32)
    BO = sb("BO", [128, 16], F32)
    WMQ = sb("WMQ", [128, 64], BF16)
    WMK = sb("WMK", [128, 64], BF16)
    ONESF = sb("ONESF", [128, 1], F32)
    ONESR = sb("ONESR", [128, 128], F32)
    ONESB = sb("ONESB", [128, 1], BF16)
    EPSB = sb("EPSB", [128, 1], F32)
    MBIAS = sb("MBIAS", [128, 3], F32)
    KMH = sb("KMH", [128, 128], F32)
    IDF = sb("IDF", [128, 128], F32)
    VFF = YST

    P1 = es.enter_context(nc.psum_tensor("P1", [128, 512], F32))
    P2 = es.enter_context(nc.psum_tensor("P2", [128, 512], F32))
    PG = es.enter_context(nc.psum_tensor("PG", [128, 512], F32))
    PU = es.enter_context(nc.psum_tensor("PU", [128, 512], F32))
    KVP = es.enter_context(nc.psum_tensor("KVP", [128, 1024], F32))
    S = es.enter_context(nc.psum_tensor("S", [128, 512], F32))
    PT = es.enter_context(nc.psum_tensor("PT", [128, 256], F32))

    semh = {}
    for s_ in ["a", "v", "t", "gc", "cc",
               "x", "bl", "wm", "aga", "agf0", "agf1", "agf2", "ccd",
               "ust0", "ust1",
               "lwq", "lwk", "lwv", "lwo", "gw0", "gw1", "uw0", "uw1",
               "dw", "yo"]:
        semh[s_] = es.enter_context(nc.semaphore("sem_" + s_))
    cnt = {k: 0 for k in semh}
    prog = []

    def emit(eng, fn, waits=(), inc=None, step=1):
        waits = [w for w in waits if w is not None]
        tok = None
        if inc is not None:
            cnt[inc] += step
            tok = (inc, cnt[inc])
        prog.append((eng, tuple(waits), fn, (inc, step)))
        return tok

    def vop(fn, waits=()):
        return emit("vector", fn, waits, "v")

    def aop(fn, waits=()):
        return emit("scalar", fn, waits, "a")

    def gop(fn, waits=()):
        return emit("gpsimd", fn, waits, "gc")

    def top(fn, waits=(), inc=None):
        return emit("tensor", fn, waits, inc)

    def dma(eng, sem, out, in_, waits=()):
        return emit(eng, lambda e, out=out, in_=in_: e.dma_start(out=out, in_=in_),
                    waits, sem, 16)

    res = {}

    def use(name, tok):
        res.setdefault(name, []).append(tok)

    def take(name):
        return res.pop(name, [])

    qprev = [None]

    def quant512(src, dst, extra_waits=(), pre_t=False):
        qa3 = (src if pre_t else QA[:]).rearrange("p (b k) -> p b k", k=32)
        qb3 = QB[:].rearrange("p (b k) -> p b k", k=32)
        qg3 = QG[:].rearrange("p (b k) -> p b k", k=32)
        s_b = QS[:].unsqueeze(-1).broadcast_to((128, 16, 32))
        r_b = QR[:].unsqueeze(-1).broadcast_to((128, 16, 32))
        if pre_t:
            first_w = list(extra_waits) + ([qprev[0]] if qprev[0] else [])
        else:
            w0 = vop(lambda e, src=src: e.transpose(out=QA[:], in_=src),
                     list(extra_waits) + ([qprev[0]] if qprev[0] else []))
            first_w = [w0]
        w1 = vop(lambda e, qa3=qa3: e.tensor_reduce(out=QM[:], in_=qa3, axis=AX.X,
                                                    op=OP.max, apply_absolute_value=True),
                 first_w)
        w2 = vop(lambda e: e.tensor_scalar(out=QS[:], in0=QM[:], scalar1=float(1.0 / 448.0),
                                           scalar2=1e-30, op0=OP.mult, op1=OP.max), [w1])
        w3 = vop(lambda e: e.reciprocal(out=QR[:], in_=QS[:]), [w2])
        w4 = vop(lambda e, qa3=qa3, qb3=qb3, r_b=r_b: e.tensor_tensor(
            out=qb3, in0=qa3, in1=r_b, op=OP.mult), [w3])
        w5 = vop(lambda e: e.tensor_scalar(out=QC[:], in0=QB[:].bitcast(I32), scalar1=0x7F800000,
                                           scalar2=0, op0=OP.bitwise_and, op1=OP.bypass), [w4])
        w6 = vop(lambda e: e.tensor_scalar(out=QD[:], in0=QC[:], scalar1=-25165824.0,
                                           scalar2=989855744.0, op0=OP.add, op1=OP.max), [w5])
        w7 = vop(lambda e: e.tensor_scalar(out=QE[:], in0=QC[:], scalar1=-1,
                                           scalar2=2155872256.0, op0=OP.mult, op1=OP.add), [w5])
        w8 = vop(lambda e: e.tensor_scalar(out=QC[:], in0=QE[:], scalar1=1140850688.0,
                                           scalar2=0, op0=OP.min, op1=OP.bypass), [w7, w6])
        w9 = vop(lambda e: e.tensor_tensor(out=QA[:], in0=QB[:], in1=QC[:].bitcast(F32),
                                           op=OP.mult), [w8])
        w13 = aop(lambda e: e.activation(out=QA[:], in_=QA[:], func=AF.Identity,
                                         bias=MBIAS[:, 0:1], scale=1.0), [w9, mb1, mb2, mb2b])
        w14 = aop(lambda e: e.activation(out=QA[:], in_=QA[:], func=AF.Identity,
                                         bias=MBIAS[:, 2:3], scale=1.0), [w13])
        w15 = vop(lambda e: e.tensor_tensor(out=QB[:], in0=QA[:], in1=QD[:].bitcast(F32),
                                            op=OP.mult), [w14, w6])
        w16 = vop(lambda e, qb3=qb3, qg3=qg3, s_b=s_b: e.tensor_tensor(
            out=qg3, in0=qb3, in1=s_b, op=OP.mult), [w15])
        w17 = vop(lambda e, dst=dst: e.transpose(out=dst, in_=QG[:]), [w16])
        qprev[0] = w17
        return w17

    for k in range(KC):
        dma("scalar", "x", XF[:, k * T:(k + 1) * T], x_t[k * 128:(k + 1) * 128, :])
    x_ld = ("x", cnt["x"])
    dma("scalar", "bl", BQ[:, 0:16], bq_d)
    dma("scalar", "bl", BK[:, 0:16], bk_d)
    dma("scalar", "bl", BV[:, 0:16], bv_d)
    dma("scalar", "bl", BO[:, 0:16], bo_d)
    b_ld = ("bl", cnt["bl"])
    dma("gpsimd", "wm", WMQ[:], wmq_d)
    dma("gpsimd", "wm", WMK[:], wmk_d)
    wm_ld = ("wm", cnt["wm"])
    on1 = gop(lambda e: e.memset(ONESF[:], 1.0))
    on1b = gop(lambda e: e.memset(ONESR[:], 1.0))
    on2 = gop(lambda e: e.memset(ONESB[:], 1.0))
    mb1 = gop(lambda e: e.memset(MBIAS[:, 0:1], MAGIC))
    mb2 = gop(lambda e: e.memset(MBIAS[:, 1:2], -MAGIC / 512.0))
    mb2b = gop(lambda e: e.memset(MBIAS[:, 2:3], -MAGIC))
    eps1 = gop(lambda e: e.memset(EPSB[:], 1e-5))
    io1 = gop(lambda e: e.iota(HSB[:, 0:128], [[1, 128]], channel_multiplier=-1,
                               allow_small_or_imprecise_dtypes=True))
    id1 = vop(lambda e: e.tensor_scalar(out=IDF[:], in0=HSB[:, 0:128], scalar1=0.0,
                                        scalar2=0, op0=OP.is_equal, op1=OP.bypass), [io1])
    use("HSB", id1)

    st_prev = None
    for u in range(AU_PC):
        ld = dma("sync", "ust0", FUST[:, 0:2048].rearrange("p (k m) -> p k m", k=16),
                 w_attn[u], waits=take("UST0"))
        cv = vop(lambda e: e.tensor_copy(out=ARENA[:, WST_O:WST_O + 2048],
                                         in_=FUST[:, 0:2048]),
                 [ld] + take("WST"))
        use("UST0", cv)
        st = dma("sync", "aga", ag_a_in[u],
                 ARENA[:, WST_O:WST_O + 2048].rearrange("p (k m) -> p k m", k=16),
                 waits=[cv])
        use("WST", st)
        st_prev = st
    cc_a = emit("gpsimd", lambda e: e.collective_compute(
        "AllGather", OP.bypass, replica_groups=[list(range(8))],
        ins=[ag_a_in.opt()], outs=[ag_a_out.opt()]), [st_prev], "cc")

    ffn_lds = {}
    cc_fs = []

    def load_ffn_unit(u):
        o = 2048 * (u % 2)
        ffn_lds[u] = dma("sync", "ust%d" % (u % 2),
                         FUST[:, o:o + 2048].rearrange("p (k m) -> p k m", k=16),
                         w_ffn[u], waits=take("UST%d" % (u % 2)))

    def prep_ffn_unit(u):
        if u not in ffn_lds:
            load_ffn_unit(u)
        ld = ffn_lds[u]
        o = 2048 * (u % 2)
        wq = None
        for j in range(4):
            wq = quant512(FUST[:, o + j * 512:o + (j + 1) * 512],
                          ARENA[:, WST_O + j * 512:WST_O + (j + 1) * 512],
                          extra_waits=([ld] + take("WST")) if j == 0 else (),
                          pre_t=True)
            use("UST%d" % (u % 2), wq)
        st = dma("sync", "agf%d" % (u // 8), ag_f_in[u],
                 ARENA[:, WST_O:WST_O + 2048].rearrange("p (k m) -> p k m", k=16),
                 waits=[wq])
        use("WST", st)
        if u % 8 == 7 and u // 8 < 2:
            cc_fs.append(emit("gpsimd", lambda e, u=u: e.collective_compute(
                "AllGather", OP.bypass, replica_groups=[list(range(8))],
                ins=[ag_f_in[u - 7:u + 1].opt()],
                outs=[[ag_f_out1, ag_f_out2][u // 8].opt()]),
                [st], "cc"))
        return st

    def ln_setup(s, first_waits, bc_ps=None, pre_stats=None):
        pa, pan, pb, pbn = bc_ps or (P1, "P1", P2, "P2")
        sl = slice(s * 512, (s + 1) * 512)
        if pre_stats is not None:
            sum_t, sq_t, sum_src, sq_src = pre_stats
            r2 = 0 if s == 0 else 64
            ts1 = aop(lambda e, sum_src=sum_src: e.copy(out=XLN[0:1, 0:512], in_=sum_src),
                      [sum_t] + list(first_waits) + take("XLN_q"))
            if s == 0:
                ts2 = aop(lambda e, sq_src=sq_src: e.copy(out=BC2[0:1, 0:512], in_=sq_src),
                          [sq_t] + take("BC2"))
                use("S_stat", ts1)
                use("PU", ts2)
            else:
                ts2 = vop(lambda e, sq_src=sq_src: e.tensor_copy(
                    out=BC2[64:65, 0:512], in_=sq_src), [sq_t] + take("BC2"))
                use("KVP", ts1)
                use("S_stat", ts2)
            return ln_bcast(ts1, ts2, pa, pan, pb, pbn, r2)
        for k in range(KC):
            top(lambda e, k=k, sl=sl: e.matmul(
                out=S[0:1, 0:512], lhsT=ONESF[:],
                rhs=XF[:, k * T:k * T + T][:, sl],
                start=(k == 0), stop=(k == KC - 1)),
                waits=(list(first_waits) + [x_ld, on1, on1b] + take("S_stat")) if k == 0 else ())
        for k in range(KC):
            sq = aop(lambda e, k=k, sl=sl: e.activation(
                out=HSB[:], in_=XF[:, k * T:k * T + T][:, sl], func=AF.Square),
                take("HSB") + ((take("PU") + list(first_waits) + [x_ld]) if k == 0 else []))
            tk = top(lambda e, k=k: e.matmul(
                out=PU[0:1, 0:512], lhsT=ONESF[:], rhs=HSB[:],
                start=(k == 0), stop=(k == KC - 1)), waits=[sq], inc="t")
            use("HSB", tk)
        ts1 = aop(lambda e: e.copy(out=XLN[0:1, 0:512], in_=S[0:1, 0:512]),
                  [tk] + take("XLN_q"))
        ts2 = aop(lambda e: e.copy(out=BC2[0:1, 0:512], in_=PU[0:1, 0:512]),
                  [tk] + take("BC2"))
        use("S_stat", ts1)
        use("PU", ts2)
        return ln_bcast(ts1, ts2, pa, pan, pb, pbn, 0)

    def ln_bcast(ts1, ts2, pa, pan, pb, pbn, r2):
        b1 = top(lambda e, pa=pa: e.matmul(out=pa[0:128, 0:512], lhsT=ONESR[0:1, :],
                                           rhs=XLN[0:1, 0:512],
                                    start=True, stop=True), waits=[ts1] + take(pan), inc="t")
        use("XLN_q", b1)
        b2 = top(lambda e, pb=pb, r2=r2: e.matmul(out=pb[0:128, 0:512],
                                           lhsT=ONESR[r2:r2 + 1, :],
                                           rhs=BC2[r2:r2 + 1, 0:512],
                                    start=True, stop=True), waits=[ts2] + take(pbn), inc="t")
        use("BC2_row", b2)
        m1 = vop(lambda e, pa=pa: e.tensor_scalar(out=BC1[:], in0=pa[0:128, 0:512],
                                           scalar1=float(1.0 / D),
                                           scalar2=0.0, op0=OP.mult, op1=OP.add),
                 [b1] + take("BC1"))
        use(pan, m1)
        use("S_stat", m1)
        m2 = vop(lambda e, pb=pb: e.tensor_scalar(out=BC2[:], in0=pb[0:128, 0:512],
                                           scalar1=float(1.0 / D),
                                           scalar2=0.0, op0=OP.mult, op1=OP.add),
                 [b2] + take("BC2") + take("BC2_row"))
        use(pbn, m2)
        m3 = vop(lambda e: e.tensor_tensor(out=HSB[:], in0=BC1[:], in1=BC1[:], op=OP.mult),
                 [m1] + take("HSB"))
        m4 = vop(lambda e: e.tensor_tensor(out=XLN[:], in0=BC2[:], in1=HSB[:], op=OP.subtract),
                 [m3, m2] + take("XLN_q"))
        use("HSB", m4)
        m6 = aop(lambda e: e.activation(out=BC2[:], in_=XLN[:], func=AF.Sqrt,
                                        bias=EPSB[:, 0:1], scale=1.0), [m4, eps1])
        m7 = vop(lambda e: e.reciprocal(out=XLN[:], in_=BC2[:]), [m6])
        m8 = vop(lambda e: e.tensor_tensor(out=BC2[:], in0=BC1[:], in1=XLN[:], op=OP.mult), [m7])
        m9 = aop(lambda e: e.copy(out=BC1[:], in_=XLN[:]), [m7, m8])
        use("XLN_q", m9)
        use("XLN_q", m8)
        m10 = vop(lambda e: e.engine_nop(), [m8, m9])
        return m10

    def ln_apply(s, k, dst, ln_tok, dst_waits=()):
        sl = slice(s * 512, (s + 1) * 512)
        h1 = vop(lambda e, k=k, sl=sl: e.tensor_tensor(
            out=HSB[:], in0=XF[:, k * T:k * T + T][:, sl], in1=BC1[:], op=OP.mult),
            [ln_tok] + take("HSB") + list(dst_waits))
        h2 = vop(lambda e, dst=dst: e.tensor_tensor(out=dst, in0=HSB[:], in1=BC2[:],
                                                    op=OP.subtract), [h1])
        use("HSB", h2)
        use("BC1", h1)
        use("BC2", h2)
        return h2

    ln1_last = None
    for s in range(NS):
        tok = ln_setup(s, [])
        use("BC", tok)
        for k in range(KC):
            ln1_last = ln_apply(s, k, WK[:, k * T + s * 512:k * T + (s + 1) * 512], tok)
    ln1_all = ln1_last

    def au(mat, m):
        return {"wq": 0, "wk": 16, "wv": 32, "wo": 48}[mat] + m

    blocks = [(m, s) for m in range(H) for s in range(NS)]
    ablk = {}

    def emit_A(m, s):
        d = {}
        lq, lk, lv = ablk[(m, "loads")]
        qfo = 29448 if s == 0 else 29960
        kfo = KF_O if s == 0 else KF_O + 512
        vff = YST if s == 0 else XLN
        sl = slice(s * 512, (s + 1) * 512)
        for k in range(KC):
            tq = top(lambda e, k=k, sl=sl: e.matmul(
                out=P1[:], lhsT=ARENA[:, WQB_O + k * 128:WQB_O + (k + 1) * 128],
                rhs=WK[:, k * T:k * T + T][:, sl], start=(k == 0), stop=(k == KC - 1)),
                waits=([lq, ln1_all, b_ld] + take("P1")) if k == 0 else (),
                inc="t" if k == KC - 1 else None)
        if s == NS - 1:
            use("WQB", tq)
        eq = aop(lambda e, m=m, qfo=qfo: e.activation(out=ARENA[:, qfo:qfo + 512], in_=P1[:],
                                             func=AF.Identity, bias=BQ[:, m:m + 1],
                                             scale=1.0), [tq] + take("QF%d" % s))
        use("P1", eq)
        for k in range(KC):
            tk = top(lambda e, k=k, sl=sl: e.matmul(
                out=P2[:], lhsT=ARENA[:, WKB_O + k * 128:WKB_O + (k + 1) * 128],
                rhs=WK[:, k * T:k * T + T][:, sl], start=(k == 0), stop=(k == KC - 1)),
                waits=([lk] + take("P2")) if k == 0 else (),
                inc="t" if k == KC - 1 else None)
        if s == NS - 1:
            use("WKB", tk)
        ek = aop(lambda e, m=m, kfo=kfo: e.activation(out=ARENA[:, kfo:kfo + 512], in_=P2[:],
                                             func=AF.Identity, bias=BK[:, m:m + 1],
                                             scale=1.0), [tk] + take("KF%d" % s))
        use("P2", ek)
        for k in range(KC):
            tv = top(lambda e, k=k, sl=sl: e.matmul(
                out=PG[:], lhsT=ARENA[:, WVB_O + k * 128:WVB_O + (k + 1) * 128],
                rhs=WK[:, k * T:k * T + T][:, sl], start=(k == 0), stop=(k == KC - 1)),
                waits=([lv] + take("PG")) if k == 0 else (),
                inc="t" if k == KC - 1 else None)
        if s == NS - 1:
            use("WVB", tv)
        ev = aop(lambda e, m=m, vff=vff: e.activation(out=vff[0:128, 0:512], in_=PG[:],
                                             func=AF.Identity, bias=BV[:, m:m + 1],
                                             scale=1.0),
                 [tv] + take("VF%d" % s) + (take("XLN_q") if s == 1 else []))
        use("PG", ev)
        d.update(eq=eq, ek=ek, ev=ev)
        return d

    def emit_B(m, s, d):
        eq, ek, ev = d["eq"], d["ek"], d["ev"]
        qfo = 29448 if s == 0 else 29960
        kfo = KF_O if s == 0 else KF_O + 512
        vff = YST if s == 0 else XLN
        qm = top(lambda e, qfo=qfo: e.matmul(out=PU[0:64, 0:512], lhsT=WMQ[:],
                                    rhs=ARENA[:, qfo:qfo + 512], start=True, stop=True),
                 waits=[eq, wm_ld] + take("PU"), inc="t")
        use("QF%d" % s, qm)
        e1 = aop(lambda e: e.activation(out=HSB[0:64, :], in_=PU[0:64, 0:512],
                                        func=AF.Relu, scale=-1.0),
                 [qm] + take("HSB"))
        e2 = aop(lambda e: e.activation(out=HSB[0:64, :], in_=HSB[0:64, :], func=AF.Exp,
                                        scale=-1.0), [e1])
        qmf_dst = ARENA[64 * (m % 2):64 * (m % 2) + 64,
                        QMF_O + (m // 2) * 1024 + s * 512:QMF_O + (m // 2) * 1024 + (s + 1) * 512]
        e3 = vop(lambda e, qmf_dst=qmf_dst: e.scalar_tensor_tensor(
            out=qmf_dst, in0=PU[0:64, 0:512], scalar=0.0, in1=HSB[0:64, :],
            op0=OP.max, op1=OP.add), [e2])
        use("HSB", e3)
        use("PU", e3)
        for j in range(4):
            tau = s * 4 + j
            pt_sl = slice(128 * (j % 2), 128 * (j % 2) + 128)
            vt_sl = slice(VT_O + 128 * (j % 2), VT_O + 128 * (j % 2) + 128)
            kmp = PU if (j % 2 == 0) else PG
            kmpn = "PU" if (j % 2 == 0) else "PG"
            hsl = slice(64 * (j % 2), 64 * (j % 2) + 64)
            km = top(lambda e, j=j, kmp=kmp, kfo=kfo: e.matmul(
                out=kmp[0:128, 0:64], lhsT=ARENA[:, kfo + j * 128:kfo + (j + 1) * 128],
                rhs=WMK[:], start=True, stop=True),
                waits=[ek, wm_ld] + take(kmpn) + ([e3] if j == 0 else []) +
                      ([ev] if j == 1 else []), inc="t")
            use("KF%d" % s, km)
            f1 = aop(lambda e, kmp=kmp, hsl=hsl: e.activation(
                out=KMH[:, hsl], in_=kmp[0:128, 0:64], func=AF.Relu, scale=-1.0),
                [km] + take("KMH%d" % (j % 2)))
            f2 = aop(lambda e, hsl=hsl: e.activation(out=KMH[:, hsl], in_=KMH[:, hsl],
                                                     func=AF.Exp, scale=-1.0), [f1])
            kmt_dst = ARENA[:, KMT_O + tau * 1024 + m * 64:KMT_O + tau * 1024 + (m + 1) * 64]
            f3 = vop(lambda e, kmt_dst=kmt_dst, kmp=kmp, hsl=hsl: e.scalar_tensor_tensor(
                out=kmt_dst, in0=kmp[0:128, 0:64], scalar=0.0, in1=KMH[:, hsl],
                op0=OP.max, op1=OP.add), [f2])
            use("KMH%d" % (j % 2), f3)
            use(kmpn, f3)
            vt = top(lambda e, j=j, pt_sl=pt_sl, vff=vff: e.transpose(
                out=PT[:, pt_sl], in_=vff[0:128, j * 128:(j + 1) * 128],
                identity=IDF[:]), waits=[ev, id1] + take("PT%d" % (j % 2)), inc="t")
            use("VF%d" % s, vt)
            if s == 1:
                use("XLN_q", vt)
            g1 = aop(lambda e, pt_sl=pt_sl, vt_sl=vt_sl: e.copy(
                out=ARENA[:, vt_sl], in_=PT[:, pt_sl]),
                [vt] + take("VT%d" % (j % 2)))
            use("PT%d" % (j % 2), g1)
            base = 64 * (m % 2)
            kv_start = (m % 8 in (0, 1)) and tau == 0
            kv_stop = (m % 8 in (6, 7)) and tau == NS * 4 - 1
            kv_last = top(lambda e, base=base, m=m, kmt_dst=kmt_dst, vt_sl=vt_sl,
                          kv_start=kv_start, kv_stop=kv_stop: e.matmul(
                out=KVP[base:base + 64, (m // 2) * 128:(m // 2) * 128 + 128],
                lhsT=kmt_dst, rhs=ARENA[:, vt_sl],
                start=kv_start, stop=kv_stop, skip_group_check=True),
                waits=[f3, g1] + (take("KVP") if (m == 0 and tau == 0) else []),
                inc="t")
            use("VT%d" % (j % 2), kv_last)
            ks_start = m in (0, 1) and tau == 0
            ks_stop = m in (H - 2, H - 1) and tau == NS * 4 - 1
            ks_tok = top(lambda e, base=base, m=m, kmt_dst=kmt_dst,
                         ks_start=ks_start, ks_stop=ks_stop: e.matmul(
                out=S[base:base + 64, (m // 2):(m // 2) + 1],
                lhsT=kmt_dst, rhs=ONESB[:],
                start=ks_start, stop=ks_stop, skip_group_check=True),
                waits=[on2] if (m in (0, 1) and tau == 0) else (),
                inc="t" if (m == H - 1 and tau == NS * 4 - 1) else None)
        return ks_tok

    pend = None
    for i, (m, s) in enumerate(blocks):
        if s == 0:
            if m > 0:
                us = list(range((m - 1) * FU_PC // H, m * FU_PC // H))
            else:
                us = []
            qh = [us[:(len(us) + 1) // 2], us[(len(us) + 1) // 2:]]
            ablk[(m, "qh")] = qh
            lq = dma("sync", "lwq", ARENA[:, WQB_O:WQB_O + 2048],
                     ag_a_out[au("wq", m)].rearrange("p k c -> p (k c)"),
                     waits=([cc_a] if m == 0 else []) + take("WQB"))
            lk = dma("sync", "lwk", ARENA[:, WKB_O:WKB_O + 2048],
                     ag_a_out[au("wk", m)].rearrange("p k c -> p (k c)"), waits=take("WKB"))
            lv = dma("sync", "lwv", ARENA[:, WVB_O:WVB_O + 2048],
                     ag_a_out[au("wv", m)].rearrange("p k c -> p (k c)"), waits=take("WVB"))
            ablk[(m, "loads")] = (lq, lk, lv)
        for u in ablk[(m, "qh")][s]:
            st_f = prep_ffn_unit(u)
        ablk[(m, s)] = emit_A(m, s)
        if pend is not None:
            ks_tok = emit_B(*pend, ablk[pend])
        pend = (m, s)
        if s == NS - 1:
            for u in range(m * FU_PC // H, (m + 1) * FU_PC // H):
                load_ffn_unit(u)
    ks_tok = emit_B(*pend, ablk[pend])
    for u in range((H - 1) * FU_PC // H, FU_PC):
        st_f = prep_ffn_unit(u)
    kv_done = ks_tok

    kvev = aop(lambda e: e.copy(out=KVS[:, 0:1024], in_=KVP[:]), [kv_done])
    ksev = vop(lambda e: e.tensor_scalar(out=KVS[:, 1024:1032], in0=S[:, 0:8], scalar1=1.0,
                                         scalar2=0.0, op0=OP.mult, op1=OP.add), [kv_done])
    use("KVP", kvev)
    use("S_stat", ksev)
    st1 = dma("sync", "ccd", cc_in, KVS[:], waits=[kvev, ksev])
    ar = emit("gpsimd", lambda e: e.collective_compute(
        "AllReduce", OP.add, replica_groups=[[0, 1, 2, 3], [4, 5, 6, 7]],
        ins=[cc_in.opt()], outs=[cc_out.opt()]), [st1], "cc")
    ld1 = dma("sync", "ccd", KVS[:], cc_out, waits=[ar])
    kvb = vop(lambda e: e.tensor_copy(out=ARENA[:, KVB_O:KVB_O + 1032], in_=KVS[:]), [ld1])
    cc_fs.append(emit("gpsimd", lambda e: e.collective_compute(
        "AllGather", OP.bypass, replica_groups=[list(range(8))],
        ins=[ag_f_in[16:24].opt()], outs=[ag_f_out3.opt()]), [st_f], "cc"))

    for m in range(H):
        base = 64 * (m % 2)
        for s in range(NS):
            qmf_src = ARENA[base:base + 64,
                            QMF_O + (m // 2) * 1024 + s * 512:QMF_O + (m // 2) * 1024 + (s + 1) * 512]
            nm = top(lambda e, base=base, m=m, qmf_src=qmf_src: e.matmul(
                out=PU[0:1, 0:512],
                lhsT=ARENA[base:base + 64, KVB_O + 1024 + (m // 2):KVB_O + 1024 + (m // 2) + 1],
                rhs=qmf_src, start=True, stop=True),
                waits=[kvb] + take("PU"), inc="t")
            n1 = vop(lambda e: e.tensor_scalar(out=XLN[0:1, 0:512], in0=PU[0:1, 0:512], scalar1=1e-6,
                                               scalar2=0.0, op0=OP.add, op1=OP.bypass),
                     [nm] + take("NR1") + take("XLN_q"))
            use("PU", n1)
            n2 = vop(lambda e: e.reciprocal(out=YST[0:1, 0:512], in_=XLN[0:1, 0:512]), [n1])
            use("XLN_q", n2)
            nb = top(lambda e: e.matmul(out=P2[:], lhsT=ONESR[0:1, :], rhs=YST[0:1, 0:512],
                                        start=True, stop=True),
                     waits=[n2] + take("P2"), inc="t")
            use("NR1", nb)
            n3 = aop(lambda e: e.copy(out=BC1[:], in_=P2[:]), [nb] + take("BC1") + take("BC"))
            use("P2", n3)
            oh = top(lambda e, base=base, m=m, qmf_src=qmf_src: e.matmul(
                out=P1[:],
                lhsT=ARENA[base:base + 64, KVB_O + (m // 2) * 128:KVB_O + (m // 2) * 128 + 128],
                rhs=qmf_src, start=True, stop=True),
                waits=take("P1"), inc="t")
            o1 = vop(lambda e, m=m, s=s: e.tensor_tensor(
                out=WK[:, m * T + s * 512:m * T + (s + 1) * 512],
                in0=P1[:], in1=BC1[:], op=OP.mult), [oh, n3])
            use("P1", o1)
            use("BC1", o1)
    attn_f = o1
    stats_done = {}

    for m in range(H):
        lo = dma("sync", "lwo", ARENA[:, WOB_O:WOB_O + 2048],
                 ag_a_out[au("wo", m)].rearrange("p k c -> p (k c)"), waits=take("WOB"))
        for s in range(NS):
            sl = slice(s * 512, (s + 1) * 512)
            for k in range(KC):
                tw = top(lambda e, k=k, sl=sl: e.matmul(
                    out=P1[:], lhsT=ARENA[:, WOB_O + k * 128:WOB_O + (k + 1) * 128],
                    rhs=WK[:, k * T:k * T + T][:, sl], start=(k == 0), stop=(k == KC - 1)),
                    waits=([lo, attn_f] + take("P1")) if k == 0 else (),
                    inc="t" if k == KC - 1 else None)
            use("WOB", tw)
            a1 = aop(lambda e, m=m: e.activation(out=HSB[:], in_=P1[:], func=AF.Identity,
                                                 bias=BO[:, m:m + 1], scale=1.0),
                     [tw] + take("HSB"))
            use("P1", a1)
            a2 = vop(lambda e, m=m, sl=sl: e.tensor_tensor(
                out=XF[:, m * T:m * T + T][:, sl], in0=XF[:, m * T:m * T + T][:, sl],
                in1=HSB[:], op=OP.add), [a1])
            use("HSB", a2)
            if s == 0:
                sum_out, sq_out = S[0:1, 0:512], PU[0:1, 0:512]
            else:
                sum_out, sq_out = KVP[0:1, 0:512], S[64:65, 0:512]
            sumw = top(lambda e, m=m, sl=sl, sum_out=sum_out: e.matmul(
                out=sum_out, lhsT=ONESF[:], rhs=XF[:, m * T:m * T + T][:, sl],
                start=(m == 0), stop=(m == H - 1), skip_group_check=True),
                waits=[a2] + (take("S_stat") if (m == 0 and s == 0) else []) +
                      (take("KVP") if (m == 0 and s == 1) else []),
                inc="t" if m == H - 1 else None)
            sq = aop(lambda e, m=m, sl=sl: e.activation(
                out=QB[:], in_=XF[:, m * T:m * T + T][:, sl], func=AF.Square),
                [a2] + take("QBD"))
            sq2 = top(lambda e, m=m, sq_out=sq_out: e.matmul(
                out=sq_out, lhsT=ONESF[:], rhs=QB[:],
                start=(m == 0), stop=(m == H - 1), skip_group_check=True),
                waits=[sq] + (take("PU") if (m == 0 and s == 0) else []), inc="t")
            use("QBD", sq2)
            if m == H - 1:
                stats_done[s] = (sumw, sq2)
    attn_done = a2

    gbuf = [GW0, GW1]
    ubuf = [UW0, UW1]

    def emit_deqx(s, k, tok):
        l2 = ln_apply(s, k, XLN[:], tok, take("XLN_q"))
        dq = quant512(XLN[:], WK[:, k * T + s * 512:k * T + (s + 1) * 512],
                      extra_waits=[l2] + ([attn_done, st_f] + take("QBD")
                                          if (s == 0 and k == 0) else []))
        use("XLN_q", dq)
        return dq

    tok0 = ln_setup(0, [attn_done],
                    pre_stats=stats_done[0] + (S[0:1, 0:512], PU[0:1, 0:512]))
    deqx_last = None
    for k in range(KC):
        deqx_last = emit_deqx(0, k, tok0)
    for s in range(NS):
        sl = slice(s * 512, (s + 1) * 512)
        for c in range(IC):
            gb, ub = gbuf[c % 2], ubuf[c % 2]
            lg = dma("sync", "gw%d" % (c % 2), FBB[:, gb:gb + 2048],
                     ag_f_out1[(c % 8) * 8 + c // 8].rearrange("p k c2 -> p (k c2)"),
                     waits=([cc_fs[0], deqx_last] if (s == 0 and c < 2) else []) + take("GB%d" % (c % 2)))
            lu = dma("sync", "uw%d" % (c % 2), FBB[:, ub:ub + 2048],
                     ag_f_out2[(c % 8) * 8 + c // 8].rearrange("p k c2 -> p (k c2)"),
                     waits=([cc_fs[1]] if (s == 0 and c < 2) else []) + take("UB%d" % (c % 2)))
            for k in range(KC):
                tg = top(lambda e, k=k, sl=sl, gb=gb: e.matmul(
                    out=PG[:], lhsT=FBB[:, gb + k * 128:gb + (k + 1) * 128],
                    rhs=WK[:, k * T:k * T + T][:, sl], start=(k == 0), stop=(k == KC - 1)),
                    waits=([lg, deqx_last] + take("PG")) if k == 0 else (),
                    inc="t" if k == KC - 1 else None)
            use("GB%d" % (c % 2), tg)
            for k in range(KC):
                tu = top(lambda e, k=k, sl=sl, ub=ub: e.matmul(
                    out=PU[:], lhsT=FBB[:, ub + k * 128:ub + (k + 1) * 128],
                    rhs=WK[:, k * T:k * T + T][:, sl], start=(k == 0), stop=(k == KC - 1)),
                    waits=([lu] + take("PU")) if k == 0 else (),
                    inc="t" if k == KC - 1 else None)
            use("UB%d" % (c % 2), tu)
            s1 = aop(lambda e: e.activation(out=HSB[:], in_=PG[:], func=AF.Silu),
                     [tg] + take("HSB"))
            use("PG", s1)
            s2 = vop(lambda e: e.tensor_tensor(out=XLN[:], in0=HSB[:], in1=PU[:], op=OP.mult),
                     [s1, tu] + take("XLN_q"))
            use("HSB", s2)
            use("PU", s2)
            hq = quant512(XLN[:], ARENA[:, c * 512:(c + 1) * 512],
                          extra_waits=[s2] + ([st_f] if (s == 0 and c == 0) else []) +
                          (take("DEQH") if c == 0 else []))
            use("XLN_q", hq)
        deqh_done = hq
        if s + 1 < NS:
            tok_n = ln_setup(s + 1, [], bc_ps=(PG, "PG", PU, "PU"),
                             pre_stats=stats_done[1] + (KVP[0:1, 0:512],
                                                        S[64:65, 0:512]))
        dwb_bufs = [DW0, GW0]
        for m in range(H):
            pd = P1 if (m % 2 == 0) else P2
            pname = "P1" if (m % 2 == 0) else "P2"
            for cg in range(4):
                dpar = (m * 4 + cg) % 2
                dwo = dwb_bufs[dpar]
                dsem = "dw" if dpar == 0 else "gw0"
                g_dw = m * 4 + cg
                ldw = dma("sync", dsem, FBB[:, dwo:dwo + 2048],
                          ag_f_out3[(g_dw % 8) * 8 + g_dw // 8].rearrange("p k c2 -> p (k c2)"),
                          waits=([deqh_done] if (m == 0 and cg == 0) else []) +
                                ([cc_fs[2]] if (s == 0 and m == 0 and cg < 2) else []) +
                                take("DWB%d" % dpar) + (take("GB0") if (m == 0 and cg == 1) else []))
                for cl in range(16):
                    c = cg * 16 + cl
                    td = top(lambda e, cl=cl, c=c, pd=pd, dwo=dwo: e.matmul(
                        out=pd[:], lhsT=FBB[:, dwo + cl * 128:dwo + (cl + 1) * 128],
                        rhs=ARENA[:, c * 512:(c + 1) * 512],
                        start=(c == 0), stop=(c == IC - 1)),
                        waits=([ldw] + (take(pname) if cg == 0 else [])) if cl == 0 else (),
                        inc="t" if (cl == 15) else None)
                use("DWB%d" % dpar, td)
            yv = vop(lambda e, m=m, sl=sl, pd=pd: e.tensor_tensor(
                out=YST[:], in0=pd[:], in1=XF[:, m * T:m * T + T][:, sl], op=OP.add),
                [td] + take("YST"))
            use(pname, yv)
            yst = dma("sync", "yo", y_t[m * 128:(m + 1) * 128, sl], YST[:], waits=[yv])
            use("YST", yst)
            if s + 1 < NS:
                deqx_last = emit_deqx(s + 1, m, tok_n)
        if s == 0:
            use("DEQH", td)

    final_tok = yst
    prog.append(("sync", (final_tok,), None, (None, 0)))

    eng_prog = {k: [] for k in ["sync", "scalar", "vector", "tensor", "gpsimd"]}
    for eng, waits, fn, inc in prog:
        eng_prog[eng].append((waits, fn, inc))

    with nc.Block() as block:
        def make_section(entries):
            def body(e):
                for waits, fn, (incname, step) in entries:
                    for (sname, val) in waits:
                        e.wait_ge(semh[sname], val)
                    if fn is not None:
                        inst = fn(e)
                        if incname is not None:
                            inst.then_inc(semh[incname], step)
            return body

        block.sync(make_section(eng_prog["sync"]))
        block.scalar(make_section(eng_prog["scalar"]))
        block.vector(make_section(eng_prog["vector"]))
        block.tensor(make_section(eng_prog["tensor"]))
        block.gpsimd(make_section(eng_prog["gpsimd"]))

    es.close()
    return nc


def _prep_inputs(ins):
    f32 = np.float32
    x = np.ascontiguousarray(ins["L0_attn_hidden_states"], dtype=f32).reshape(8192, D)
    wq = ins["L0_attn_wq"]
    wk = ins["L0_attn_wk"]
    wv = ins["L0_attn_wv"]
    wo = ins["L0_attn_wo"]

    def tile_w(w, rows):
        return np.ascontiguousarray(
            w.reshape(rows // 128, 128, w.shape[1] // 128, 128).transpose(0, 3, 2, 1),
            dtype=f32)

    wq_t = tile_w(wq, D)
    wk_t = tile_w(wk, D)
    wv_t = tile_w(wv, D)
    wo_t = tile_w(wo, D)
    attn_units = np.concatenate([wq_t, wk_t, wv_t, wo_t], axis=0)
    def bt32(a):
        n = a.shape[0]
        b = a.reshape(n, 4, 32, 16, 4, 32)
        return np.ascontiguousarray(b.transpose(0, 1, 5, 3, 4, 2)).reshape(a.shape)

    gw_t = tile_w(ins["L0_ffn_gate_w"], 8192)
    uw_t = tile_w(ins["L0_ffn_up_w"], 8192)
    dw_t = tile_w(ins["L0_ffn_down_w"], D)
    dw_u = np.ascontiguousarray(
        dw_t.reshape(16, 128, 4, 16, 128).transpose(0, 2, 1, 3, 4)).reshape(64, 128, 16, 128)
    ffn_units = bt32(np.concatenate([gw_t, uw_t, dw_u], axis=0))

    in_maps = []
    for c in range(8):
        xs = np.ascontiguousarray(x[c * T:(c + 1) * T].T)
        in_maps.append({
            "x_t": xs,
            "w_attn": np.ascontiguousarray(attn_units[c * AU_PC:(c + 1) * AU_PC]),
            "w_ffn": np.ascontiguousarray(ffn_units[c::8]),
            "bq": np.ascontiguousarray(ins["L0_attn_bq"], dtype=f32).reshape(16, 128).T.copy(),
            "bk": np.ascontiguousarray(ins["L0_attn_bk"], dtype=f32).reshape(16, 128).T.copy(),
            "bv": np.ascontiguousarray(ins["L0_attn_bv"], dtype=f32).reshape(16, 128).T.copy(),
            "bo": np.ascontiguousarray(ins["L0_attn_bo"], dtype=f32).reshape(16, 128).T.copy(),
            "wmq": np.ascontiguousarray(ins["L0_attn_wmap_q"].T, dtype=f32),
            "wmk": np.ascontiguousarray(ins["L0_attn_wmap_k"].T, dtype=f32),
        })
    return in_maps


def kernel(**inputs):
    in_maps = _prep_inputs(inputs)
    nc = build_nc()
    res_ = run_bass_kernel_spmd(nc, in_maps, core_ids=list(range(8)))
    outs = []
    for c in range(8):
        yt = np.asarray(res_.results[c]["y_t"])
        outs.append(np.ascontiguousarray(yt.T))
    y = np.concatenate(outs, axis=0).reshape(2, 4096, D).astype(np.float32)
    return y


if __name__ == "__main__":
    import reference as _r
    ins = _r.setup_inputs()
    out = kernel(**{k: np.asarray(v) for k, v in ins.items()})
    ref = np.asarray(_r.reference(**ins))
    err = np.abs(out - ref)
    print("max abs err:", err.max(), "rms rel:", float(np.sqrt((err ** 2).mean() / (ref ** 2).mean())))


# version 23
# speedup vs baseline: 1.0550x; 1.0082x over previous
import contextlib
import numpy as np

import concourse.bass as bass
import concourse.mybir as mybir
from concourse.bass_utils import run_bass_kernel_spmd

F32 = mybir.dt.float32
I32 = mybir.dt.int32
BF16 = mybir.dt.bfloat16
OP = mybir.AluOpType
AF = mybir.ActivationFunctionType
AX = mybir.AxisListType

MAGIC = 12582912.0
D = 2048
KC = 16
H = 16
T = 1024
NS = 2
IC = 64
N_AU = 64
N_FU = 192
AU_PC = N_AU // 8
FU_PC = N_FU // 8


def build_nc():
    nc = bass.Bass("TRN2", target_bir_lowering=False)

    x_t = nc.dram_tensor("x_t", [D, T], F32, kind="ExternalInput").ap()
    w_attn = nc.dram_tensor("w_attn", [AU_PC, 128, 16, 128], F32, kind="ExternalInput").ap()
    w_ffn = nc.dram_tensor("w_ffn", [FU_PC, 128, 16, 128], F32, kind="ExternalInput").ap()
    bq_d = nc.dram_tensor("bq", [128, 16], F32, kind="ExternalInput").ap()
    bk_d = nc.dram_tensor("bk", [128, 16], F32, kind="ExternalInput").ap()
    bv_d = nc.dram_tensor("bv", [128, 16], F32, kind="ExternalInput").ap()
    bo_d = nc.dram_tensor("bo", [128, 16], F32, kind="ExternalInput").ap()
    wmq_d = nc.dram_tensor("wmq", [128, 64], F32, kind="ExternalInput").ap()
    wmk_d = nc.dram_tensor("wmk", [128, 64], F32, kind="ExternalInput").ap()
    y_t = nc.dram_tensor("y_t", [D, T], F32, kind="ExternalOutput").ap()

    ag_a_in = nc.dram_tensor("ag_a_in", [AU_PC, 128, 16, 128], BF16).ap()
    ag_a_out = nc.dram_tensor("ag_a_out", [N_AU, 128, 16, 128], BF16).ap()
    ag_f_in = nc.dram_tensor("ag_f_in", [FU_PC, 128, 16, 128], BF16).ap()
    ag_f_out1 = nc.dram_tensor("ag_f_out1", [64, 128, 16, 128], BF16).ap()
    ag_f_out2 = nc.dram_tensor("ag_f_out2", [64, 128, 16, 128], BF16).ap()
    ag_f_out3 = nc.dram_tensor("ag_f_out3", [64, 128, 16, 128], BF16).ap()
    cc_in = nc.dram_tensor("cc_in", [128, 1032], F32).ap()
    cc_out = nc.dram_tensor("cc_out", [128, 1032], F32).ap()

    es = contextlib.ExitStack()

    def sb(name, shape, dt):
        return es.enter_context(nc.sbuf_tensor(name, shape, dt))

    XF = sb("XF", [128, KC * T], F32)
    WK = sb("WK", [128, KC * T], BF16)
    ARENA = sb("ARENA", [128, 32768], BF16)
    QMF_O, KMT_O = 0, 8192
    WQB_O, WKB_O, WVB_O, WOB_O = 16384, 18432, 20480, 22528
    KF_O, QF_O, VF_O, VT_O, KVB_O, WST_O = 24576, 25088, 25600, 26112, 26368, 27400
    FUST = sb("FUST", [128, 5120], F32)
    FBB = FUST[:].bitcast(BF16)
    GW0, GW1, UW0, UW1, DW0 = 0, 2048, 4096, 6144, 8192
    YST = sb("YST", [128, 512], F32)
    XLN = sb("XLN", [128, 512], F32)
    HSB = sb("HSB", [128, 512], F32)
    KVS = sb("KVS", [128, 1032], F32)
    BC1 = sb("BC1", [128, 512], F32)
    BC2 = sb("BC2", [128, 512], F32)
    QA = sb("QA", [128, 512], F32)
    QB = sb("QB", [128, 512], F32)
    QE = sb("QE", [128, 512], F32)
    QC = sb("QC", [128, 512], I32)
    QD = sb("QD", [128, 512], I32)
    QG = sb("QG", [128, 512], BF16)
    QM = sb("QM", [128, 16], F32)
    QS = sb("QS", [128, 16], F32)
    QR = sb("QR", [128, 16], F32)
    BQ = sb("BQ", [128, 16], F32)
    BK = sb("BK", [128, 16], F32)
    BV = sb("BV", [128, 16], F32)
    BO = sb("BO", [128, 16], F32)
    WMQ = sb("WMQ", [128, 64], BF16)
    WMK = sb("WMK", [128, 64], BF16)
    ONESF = sb("ONESF", [128, 1], F32)
    ONESR = sb("ONESR", [128, 128], F32)
    ONESB = sb("ONESB", [128, 1], BF16)
    EPSB = sb("EPSB", [128, 1], F32)
    MBIAS = sb("MBIAS", [128, 3], F32)
    KMH = sb("KMH", [128, 128], F32)
    IDF = sb("IDF", [128, 128], F32)
    VFF = YST

    P1 = es.enter_context(nc.psum_tensor("P1", [128, 512], F32))
    P2 = es.enter_context(nc.psum_tensor("P2", [128, 512], F32))
    PG = es.enter_context(nc.psum_tensor("PG", [128, 512], F32))
    PU = es.enter_context(nc.psum_tensor("PU", [128, 512], F32))
    KVP = es.enter_context(nc.psum_tensor("KVP", [128, 1024], F32))
    S = es.enter_context(nc.psum_tensor("S", [128, 512], F32))
    PT = es.enter_context(nc.psum_tensor("PT", [128, 256], F32))

    semh = {}
    for s_ in ["a", "v", "t", "gc", "cc",
               "x", "bl", "wm", "aga", "agf0", "agf1", "agf2", "ccd",
               "ust0", "ust1",
               "lwq", "lwk", "lwv", "lwo", "gw0", "gw1", "uw0", "uw1",
               "dw", "yo"]:
        semh[s_] = es.enter_context(nc.semaphore("sem_" + s_))
    cnt = {k: 0 for k in semh}
    prog = []

    def emit(eng, fn, waits=(), inc=None, step=1):
        waits = [w for w in waits if w is not None]
        tok = None
        if inc is not None:
            cnt[inc] += step
            tok = (inc, cnt[inc])
        prog.append((eng, tuple(waits), fn, (inc, step)))
        return tok

    def vop(fn, waits=()):
        return emit("vector", fn, waits, "v")

    def aop(fn, waits=()):
        return emit("scalar", fn, waits, "a")

    def gop(fn, waits=()):
        return emit("gpsimd", fn, waits, "gc")

    def top(fn, waits=(), inc=None):
        return emit("tensor", fn, waits, inc)

    def dma(eng, sem, out, in_, waits=()):
        return emit(eng, lambda e, out=out, in_=in_: e.dma_start(out=out, in_=in_),
                    waits, sem, 16)

    res = {}

    def use(name, tok):
        res.setdefault(name, []).append(tok)

    def take(name):
        return res.pop(name, [])

    qprev = [None]

    def quant512(src, dst, extra_waits=(), pre_t=False):
        qa3 = (src if pre_t else QA[:]).rearrange("p (b k) -> p b k", k=32)
        qb3 = QB[:].rearrange("p (b k) -> p b k", k=32)
        qg3 = QG[:].rearrange("p (b k) -> p b k", k=32)
        s_b = QS[:].unsqueeze(-1).broadcast_to((128, 16, 32))
        r_b = QR[:].unsqueeze(-1).broadcast_to((128, 16, 32))
        if pre_t:
            first_w = list(extra_waits) + ([qprev[0]] if qprev[0] else [])
        else:
            w0 = vop(lambda e, src=src: e.transpose(out=QA[:], in_=src),
                     list(extra_waits) + ([qprev[0]] if qprev[0] else []))
            first_w = [w0]
        w1 = vop(lambda e, qa3=qa3: e.tensor_reduce(out=QM[:], in_=qa3, axis=AX.X,
                                                    op=OP.max, apply_absolute_value=True),
                 first_w)
        w2 = vop(lambda e: e.tensor_scalar(out=QS[:], in0=QM[:], scalar1=float(1.0 / 448.0),
                                           scalar2=1e-30, op0=OP.mult, op1=OP.max), [w1])
        w3 = vop(lambda e: e.reciprocal(out=QR[:], in_=QS[:]), [w2])
        w4 = vop(lambda e, qa3=qa3, qb3=qb3, r_b=r_b: e.tensor_tensor(
            out=qb3, in0=qa3, in1=r_b, op=OP.mult), [w3])
        w5 = vop(lambda e: e.tensor_scalar(out=QC[:], in0=QB[:].bitcast(I32), scalar1=0x7F800000,
                                           scalar2=0, op0=OP.bitwise_and, op1=OP.bypass), [w4])
        w6 = vop(lambda e: e.tensor_scalar(out=QD[:], in0=QC[:], scalar1=-25165824.0,
                                           scalar2=989855744.0, op0=OP.add, op1=OP.max), [w5])
        w7 = vop(lambda e: e.tensor_scalar(out=QE[:], in0=QC[:], scalar1=-1,
                                           scalar2=2155872256.0, op0=OP.mult, op1=OP.add), [w5])
        w8 = vop(lambda e: e.tensor_scalar(out=QC[:], in0=QE[:], scalar1=1140850688.0,
                                           scalar2=0, op0=OP.min, op1=OP.bypass), [w7, w6])
        w9 = vop(lambda e: e.tensor_tensor(out=QA[:], in0=QB[:], in1=QC[:].bitcast(F32),
                                           op=OP.mult), [w8])
        w13 = aop(lambda e: e.activation(out=QA[:], in_=QA[:], func=AF.Identity,
                                         bias=MBIAS[:, 0:1], scale=1.0), [w9, mb1, mb2, mb2b])
        w14 = aop(lambda e: e.activation(out=QA[:], in_=QA[:], func=AF.Identity,
                                         bias=MBIAS[:, 2:3], scale=1.0), [w13])
        w15 = vop(lambda e: e.tensor_tensor(out=QB[:], in0=QA[:], in1=QD[:].bitcast(F32),
                                            op=OP.mult), [w14, w6])
        w16 = vop(lambda e, qb3=qb3, qg3=qg3, s_b=s_b: e.tensor_tensor(
            out=qg3, in0=qb3, in1=s_b, op=OP.mult), [w15])
        w17 = vop(lambda e, dst=dst: e.transpose(out=dst, in_=QG[:]), [w16])
        qprev[0] = w17
        return w17

    for k in range(KC):
        dma("scalar", "x", XF[:, k * T:(k + 1) * T], x_t[k * 128:(k + 1) * 128, :])
    x_ld = ("x", cnt["x"])
    dma("scalar", "bl", BQ[:, 0:16], bq_d)
    dma("scalar", "bl", BK[:, 0:16], bk_d)
    dma("scalar", "bl", BV[:, 0:16], bv_d)
    dma("scalar", "bl", BO[:, 0:16], bo_d)
    b_ld = ("bl", cnt["bl"])
    dma("gpsimd", "wm", WMQ[:], wmq_d)
    dma("gpsimd", "wm", WMK[:], wmk_d)
    wm_ld = ("wm", cnt["wm"])
    on1 = gop(lambda e: e.memset(ONESF[:], 1.0))
    on1b = gop(lambda e: e.memset(ONESR[:], 1.0))
    on2 = gop(lambda e: e.memset(ONESB[:], 1.0))
    mb1 = gop(lambda e: e.memset(MBIAS[:, 0:1], MAGIC))
    mb2 = gop(lambda e: e.memset(MBIAS[:, 1:2], -MAGIC / 512.0))
    mb2b = gop(lambda e: e.memset(MBIAS[:, 2:3], -MAGIC))
    eps1 = gop(lambda e: e.memset(EPSB[:], 1e-5))
    io1 = gop(lambda e: e.iota(HSB[:, 0:128], [[1, 128]], channel_multiplier=-1,
                               allow_small_or_imprecise_dtypes=True))
    id1 = vop(lambda e: e.tensor_scalar(out=IDF[:], in0=HSB[:, 0:128], scalar1=0.0,
                                        scalar2=0, op0=OP.is_equal, op1=OP.bypass), [io1])
    use("HSB", id1)

    st_prev = None
    for u in range(AU_PC):
        ld = dma("sync", "ust0", FUST[:, 0:2048].rearrange("p (k m) -> p k m", k=16),
                 w_attn[u], waits=take("UST0"))
        cv = vop(lambda e: e.tensor_copy(out=ARENA[:, WST_O:WST_O + 2048],
                                         in_=FUST[:, 0:2048]),
                 [ld] + take("WST"))
        use("UST0", cv)
        st = dma("sync", "aga", ag_a_in[u],
                 ARENA[:, WST_O:WST_O + 2048].rearrange("p (k m) -> p k m", k=16),
                 waits=[cv])
        use("WST", st)
        st_prev = st
    cc_a = emit("gpsimd", lambda e: e.collective_compute(
        "AllGather", OP.bypass, replica_groups=[list(range(8))],
        ins=[ag_a_in.opt()], outs=[ag_a_out.opt()]), [st_prev], "cc")

    ffn_lds = {}
    cc_fs = []

    def load_ffn_unit(u):
        o = 2048 * (u % 2)
        ffn_lds[u] = dma("sync", "ust%d" % (u % 2),
                         FUST[:, o:o + 2048].rearrange("p (k m) -> p k m", k=16),
                         w_ffn[u], waits=take("UST%d" % (u % 2)))

    def prep_ffn_unit(u):
        if u not in ffn_lds:
            load_ffn_unit(u)
        ld = ffn_lds[u]
        o = 2048 * (u % 2)
        wq = None
        for j in range(4):
            wq = quant512(FUST[:, o + j * 512:o + (j + 1) * 512],
                          ARENA[:, WST_O + j * 512:WST_O + (j + 1) * 512],
                          extra_waits=([ld] + take("WST")) if j == 0 else (),
                          pre_t=True)
            use("UST%d" % (u % 2), wq)
        st = dma("sync", "agf%d" % (u // 8), ag_f_in[u],
                 ARENA[:, WST_O:WST_O + 2048].rearrange("p (k m) -> p k m", k=16),
                 waits=[wq])
        use("WST", st)
        if u % 8 == 7 and u // 8 < 2:
            cc_fs.append(emit("gpsimd", lambda e, u=u: e.collective_compute(
                "AllGather", OP.bypass, replica_groups=[list(range(8))],
                ins=[ag_f_in[u - 7:u + 1].opt()],
                outs=[[ag_f_out1, ag_f_out2][u // 8].opt()]),
                [st], "cc"))
        return st

    def ln_setup(s, first_waits, bc_ps=None, pre_stats=None):
        pa, pan, pb, pbn = bc_ps or (P1, "P1", P2, "P2")
        sl = slice(s * 512, (s + 1) * 512)
        if pre_stats is not None:
            sum_t, sq_t, sum_src, sq_src = pre_stats
            r2 = 0 if s == 0 else 64
            ts1 = aop(lambda e, sum_src=sum_src: e.copy(out=XLN[0:1, 0:512], in_=sum_src),
                      [sum_t] + list(first_waits) + take("XLN_q"))
            if s == 0:
                ts2 = aop(lambda e, sq_src=sq_src: e.copy(out=BC2[0:1, 0:512], in_=sq_src),
                          [sq_t] + take("BC2"))
                use("S_stat", ts1)
                use("PU", ts2)
            else:
                ts2 = vop(lambda e, sq_src=sq_src: e.tensor_copy(
                    out=BC2[64:65, 0:512], in_=sq_src), [sq_t] + take("BC2"))
                use("KVP", ts1)
                use("S_stat", ts2)
            return ln_bcast(ts1, ts2, pa, pan, pb, pbn, r2)
        for k in range(KC):
            top(lambda e, k=k, sl=sl: e.matmul(
                out=S[0:1, 0:512], lhsT=ONESF[:],
                rhs=XF[:, k * T:k * T + T][:, sl],
                start=(k == 0), stop=(k == KC - 1)),
                waits=(list(first_waits) + [x_ld, on1, on1b] + take("S_stat")) if k == 0 else ())
        for k in range(KC):
            sq = aop(lambda e, k=k, sl=sl: e.activation(
                out=HSB[:], in_=XF[:, k * T:k * T + T][:, sl], func=AF.Square),
                take("HSB") + ((take("PU") + list(first_waits) + [x_ld]) if k == 0 else []))
            tk = top(lambda e, k=k: e.matmul(
                out=PU[0:1, 0:512], lhsT=ONESF[:], rhs=HSB[:],
                start=(k == 0), stop=(k == KC - 1)), waits=[sq], inc="t")
            use("HSB", tk)
        ts1 = aop(lambda e: e.copy(out=XLN[0:1, 0:512], in_=S[0:1, 0:512]),
                  [tk] + take("XLN_q"))
        ts2 = aop(lambda e: e.copy(out=BC2[0:1, 0:512], in_=PU[0:1, 0:512]),
                  [tk] + take("BC2"))
        use("S_stat", ts1)
        use("PU", ts2)
        return ln_bcast(ts1, ts2, pa, pan, pb, pbn, 0)

    def ln_bcast(ts1, ts2, pa, pan, pb, pbn, r2):
        b1 = top(lambda e, pa=pa: e.matmul(out=pa[0:128, 0:512], lhsT=ONESR[0:1, :],
                                           rhs=XLN[0:1, 0:512],
                                    start=True, stop=True), waits=[ts1] + take(pan), inc="t")
        use("XLN_q", b1)
        b2 = top(lambda e, pb=pb, r2=r2: e.matmul(out=pb[0:128, 0:512],
                                           lhsT=ONESR[r2:r2 + 1, :],
                                           rhs=BC2[r2:r2 + 1, 0:512],
                                    start=True, stop=True), waits=[ts2] + take(pbn), inc="t")
        use("BC2_row", b2)
        m1 = vop(lambda e, pa=pa: e.tensor_scalar(out=BC1[:], in0=pa[0:128, 0:512],
                                           scalar1=float(1.0 / D),
                                           scalar2=0.0, op0=OP.mult, op1=OP.add),
                 [b1] + take("BC1"))
        use(pan, m1)
        use("S_stat", m1)
        m2 = vop(lambda e, pb=pb: e.tensor_scalar(out=BC2[:], in0=pb[0:128, 0:512],
                                           scalar1=float(1.0 / D),
                                           scalar2=0.0, op0=OP.mult, op1=OP.add),
                 [b2] + take("BC2") + take("BC2_row"))
        use(pbn, m2)
        m3 = vop(lambda e: e.tensor_tensor(out=HSB[:], in0=BC1[:], in1=BC1[:], op=OP.mult),
                 [m1] + take("HSB"))
        m4 = vop(lambda e: e.tensor_tensor(out=XLN[:], in0=BC2[:], in1=HSB[:], op=OP.subtract),
                 [m3, m2] + take("XLN_q"))
        use("HSB", m4)
        m6 = aop(lambda e: e.activation(out=BC2[:], in_=XLN[:], func=AF.Sqrt,
                                        bias=EPSB[:, 0:1], scale=1.0), [m4, eps1])
        m7 = vop(lambda e: e.reciprocal(out=XLN[:], in_=BC2[:]), [m6])
        m8 = vop(lambda e: e.tensor_tensor(out=BC2[:], in0=BC1[:], in1=XLN[:], op=OP.mult), [m7])
        m9 = aop(lambda e: e.copy(out=BC1[:], in_=XLN[:]), [m7, m8])
        use("XLN_q", m9)
        use("XLN_q", m8)
        m10 = vop(lambda e: e.engine_nop(), [m8, m9])
        return m10

    def ln_apply(s, k, dst, ln_tok, dst_waits=()):
        sl = slice(s * 512, (s + 1) * 512)
        h1 = vop(lambda e, k=k, sl=sl: e.tensor_tensor(
            out=HSB[:], in0=XF[:, k * T:k * T + T][:, sl], in1=BC1[:], op=OP.mult),
            [ln_tok] + take("HSB") + list(dst_waits))
        h2 = vop(lambda e, dst=dst: e.tensor_tensor(out=dst, in0=HSB[:], in1=BC2[:],
                                                    op=OP.subtract), [h1])
        use("HSB", h2)
        use("BC1", h1)
        use("BC2", h2)
        return h2

    ln1_last = None
    for s in range(NS):
        tok = ln_setup(s, [])
        use("BC", tok)
        for k in range(KC):
            ln1_last = ln_apply(s, k, WK[:, k * T + s * 512:k * T + (s + 1) * 512], tok)
    ln1_all = ln1_last

    def au(mat, m):
        return {"wq": 0, "wk": 16, "wv": 32, "wo": 48}[mat] + m

    blocks = [(m, s) for m in range(H) for s in range(NS)]
    ablk = {}

    def emit_A(m, s):
        d = {}
        lq, lk, lv = ablk[(m, "loads")]
        qfo = 29448 if s == 0 else 29960
        kfo = KF_O if s == 0 else KF_O + 512
        vff = YST if s == 0 else XLN
        sl = slice(s * 512, (s + 1) * 512)
        for k in range(KC):
            tq = top(lambda e, k=k, sl=sl: e.matmul(
                out=P1[:], lhsT=ARENA[:, WQB_O + k * 128:WQB_O + (k + 1) * 128],
                rhs=WK[:, k * T:k * T + T][:, sl], start=(k == 0), stop=(k == KC - 1)),
                waits=([lq, ln1_all, b_ld] + take("P1")) if k == 0 else (),
                inc="t" if k == KC - 1 else None)
        if s == NS - 1:
            use("WQB", tq)
        eq = aop(lambda e, m=m, qfo=qfo: e.activation(out=ARENA[:, qfo:qfo + 512], in_=P1[:],
                                             func=AF.Identity, bias=BQ[:, m:m + 1],
                                             scale=1.0), [tq] + take("QF%d" % s))
        use("P1", eq)
        for k in range(KC):
            tk = top(lambda e, k=k, sl=sl: e.matmul(
                out=P2[:], lhsT=ARENA[:, WKB_O + k * 128:WKB_O + (k + 1) * 128],
                rhs=WK[:, k * T:k * T + T][:, sl], start=(k == 0), stop=(k == KC - 1)),
                waits=([lk] + take("P2")) if k == 0 else (),
                inc="t" if k == KC - 1 else None)
        if s == NS - 1:
            use("WKB", tk)
        ek = aop(lambda e, m=m, kfo=kfo: e.activation(out=ARENA[:, kfo:kfo + 512], in_=P2[:],
                                             func=AF.Identity, bias=BK[:, m:m + 1],
                                             scale=1.0), [tk] + take("KF%d" % s))
        use("P2", ek)
        for k in range(KC):
            tv = top(lambda e, k=k, sl=sl: e.matmul(
                out=PG[:], lhsT=ARENA[:, WVB_O + k * 128:WVB_O + (k + 1) * 128],
                rhs=WK[:, k * T:k * T + T][:, sl], start=(k == 0), stop=(k == KC - 1)),
                waits=([lv] + take("PG")) if k == 0 else (),
                inc="t" if k == KC - 1 else None)
        if s == NS - 1:
            use("WVB", tv)
        ev = aop(lambda e, m=m, vff=vff: e.activation(out=vff[0:128, 0:512], in_=PG[:],
                                             func=AF.Identity, bias=BV[:, m:m + 1],
                                             scale=1.0),
                 [tv] + take("VF%d" % s) + (take("XLN_q") if s == 1 else []))
        use("PG", ev)
        d.update(eq=eq, ek=ek, ev=ev)
        return d

    def emit_B(m, s, d):
        eq, ek, ev = d["eq"], d["ek"], d["ev"]
        qfo = 29448 if s == 0 else 29960
        kfo = KF_O if s == 0 else KF_O + 512
        vff = YST if s == 0 else XLN
        qm = top(lambda e, qfo=qfo: e.matmul(out=PU[0:64, 0:512], lhsT=WMQ[:],
                                    rhs=ARENA[:, qfo:qfo + 512], start=True, stop=True),
                 waits=[eq, wm_ld] + take("PU"), inc="t")
        use("QF%d" % s, qm)
        e1 = aop(lambda e: e.activation(out=HSB[0:64, :], in_=PU[0:64, 0:512],
                                        func=AF.Relu, scale=-1.0),
                 [qm] + take("HSB"))
        e2 = aop(lambda e: e.activation(out=HSB[0:64, :], in_=HSB[0:64, :], func=AF.Exp,
                                        scale=-1.0), [e1])
        qmf_dst = ARENA[64 * (m % 2):64 * (m % 2) + 64,
                        QMF_O + (m // 2) * 1024 + s * 512:QMF_O + (m // 2) * 1024 + (s + 1) * 512]
        e3 = vop(lambda e, qmf_dst=qmf_dst: e.scalar_tensor_tensor(
            out=qmf_dst, in0=PU[0:64, 0:512], scalar=0.0, in1=HSB[0:64, :],
            op0=OP.max, op1=OP.add), [e2])
        use("HSB", e3)
        use("PU", e3)
        for j in range(4):
            tau = s * 4 + j
            pt_sl = slice(128 * (j % 2), 128 * (j % 2) + 128)
            vt_sl = slice(VT_O + 128 * (j % 2), VT_O + 128 * (j % 2) + 128)
            kmp = PU if (j % 2 == 0) else PG
            kmpn = "PU" if (j % 2 == 0) else "PG"
            hsl = slice(64 * (j % 2), 64 * (j % 2) + 64)
            km = top(lambda e, j=j, kmp=kmp, kfo=kfo: e.matmul(
                out=kmp[0:128, 0:64], lhsT=ARENA[:, kfo + j * 128:kfo + (j + 1) * 128],
                rhs=WMK[:], start=True, stop=True),
                waits=[ek, wm_ld] + take(kmpn) + ([e3] if j == 0 else []) +
                      ([ev] if j == 1 else []), inc="t")
            use("KF%d" % s, km)
            f1 = aop(lambda e, kmp=kmp, hsl=hsl: e.activation(
                out=KMH[:, hsl], in_=kmp[0:128, 0:64], func=AF.Relu, scale=-1.0),
                [km] + take("KMH%d" % (j % 2)))
            f2 = aop(lambda e, hsl=hsl: e.activation(out=KMH[:, hsl], in_=KMH[:, hsl],
                                                     func=AF.Exp, scale=-1.0), [f1])
            kmt_dst = ARENA[:, KMT_O + tau * 1024 + m * 64:KMT_O + tau * 1024 + (m + 1) * 64]
            f3 = vop(lambda e, kmt_dst=kmt_dst, kmp=kmp, hsl=hsl: e.scalar_tensor_tensor(
                out=kmt_dst, in0=kmp[0:128, 0:64], scalar=0.0, in1=KMH[:, hsl],
                op0=OP.max, op1=OP.add), [f2])
            use("KMH%d" % (j % 2), f3)
            use(kmpn, f3)
            vt = top(lambda e, j=j, pt_sl=pt_sl, vff=vff: e.transpose(
                out=PT[:, pt_sl], in_=vff[0:128, j * 128:(j + 1) * 128],
                identity=IDF[:]), waits=[ev, id1] + take("PT%d" % (j % 2)), inc="t")
            use("VF%d" % s, vt)
            if s == 1:
                use("XLN_q", vt)
            g1 = aop(lambda e, pt_sl=pt_sl, vt_sl=vt_sl: e.copy(
                out=ARENA[:, vt_sl], in_=PT[:, pt_sl]),
                [vt] + take("VT%d" % (j % 2)))
            use("PT%d" % (j % 2), g1)
            base = 64 * (m % 2)
            kv_start = (m % 8 in (0, 1)) and tau == 0
            kv_stop = (m % 8 in (6, 7)) and tau == NS * 4 - 1
            kv_last = top(lambda e, base=base, m=m, kmt_dst=kmt_dst, vt_sl=vt_sl,
                          kv_start=kv_start, kv_stop=kv_stop: e.matmul(
                out=KVP[base:base + 64, (m // 2) * 128:(m // 2) * 128 + 128],
                lhsT=kmt_dst, rhs=ARENA[:, vt_sl],
                start=kv_start, stop=kv_stop, skip_group_check=True),
                waits=[f3, g1] + (take("KVP") if (m == 0 and tau == 0) else []),
                inc="t")
            use("VT%d" % (j % 2), kv_last)
            ks_start = m in (0, 1) and tau == 0
            ks_stop = m in (H - 2, H - 1) and tau == NS * 4 - 1
            ks_tok = top(lambda e, base=base, m=m, kmt_dst=kmt_dst,
                         ks_start=ks_start, ks_stop=ks_stop: e.matmul(
                out=S[base:base + 64, (m // 2):(m // 2) + 1],
                lhsT=kmt_dst, rhs=ONESB[:],
                start=ks_start, stop=ks_stop, skip_group_check=True),
                waits=[on2] if (m in (0, 1) and tau == 0) else (),
                inc="t" if (m == H - 1 and tau == NS * 4 - 1) else None)
        return ks_tok

    pend = None
    for i, (m, s) in enumerate(blocks):
        if s == 0:
            if m > 0:
                us = list(range((m - 1) * FU_PC // H, m * FU_PC // H))
            else:
                us = []
            qh = [us[:(len(us) + 1) // 2], us[(len(us) + 1) // 2:]]
            ablk[(m, "qh")] = qh
            lq = dma("sync", "lwq", ARENA[:, WQB_O:WQB_O + 2048],
                     ag_a_out[au("wq", m)].rearrange("p k c -> p (k c)"),
                     waits=([cc_a] if m == 0 else []) + take("WQB"))
            lk = dma("sync", "lwk", ARENA[:, WKB_O:WKB_O + 2048],
                     ag_a_out[au("wk", m)].rearrange("p k c -> p (k c)"), waits=take("WKB"))
            lv = dma("sync", "lwv", ARENA[:, WVB_O:WVB_O + 2048],
                     ag_a_out[au("wv", m)].rearrange("p k c -> p (k c)"), waits=take("WVB"))
            ablk[(m, "loads")] = (lq, lk, lv)
        for u in ablk[(m, "qh")][s]:
            st_f = prep_ffn_unit(u)
        ablk[(m, s)] = emit_A(m, s)
        if pend is not None:
            ks_tok = emit_B(*pend, ablk[pend])
        pend = (m, s)
        if s == NS - 1:
            for u in range(m * FU_PC // H, (m + 1) * FU_PC // H):
                load_ffn_unit(u)
    ks_tok = emit_B(*pend, ablk[pend])
    for u in range((H - 1) * FU_PC // H, FU_PC):
        st_f = prep_ffn_unit(u)
    kv_done = ks_tok

    kvev = aop(lambda e: e.copy(out=KVS[:, 0:1024], in_=KVP[:]), [kv_done])
    ksev = vop(lambda e: e.tensor_scalar(out=KVS[:, 1024:1032], in0=S[:, 0:8], scalar1=1.0,
                                         scalar2=0.0, op0=OP.mult, op1=OP.add), [kv_done])
    use("KVP", kvev)
    use("S_stat", ksev)
    st1 = dma("sync", "ccd", cc_in, KVS[:], waits=[kvev, ksev])
    ar = emit("gpsimd", lambda e: e.collective_compute(
        "AllReduce", OP.add, replica_groups=[[0, 1, 2, 3], [4, 5, 6, 7]],
        ins=[cc_in.opt()], outs=[cc_out.opt()]), [st1], "cc")
    ld1 = dma("sync", "ccd", KVS[:], cc_out, waits=[ar])
    kvb = vop(lambda e: e.tensor_copy(out=ARENA[:, KVB_O:KVB_O + 1032], in_=KVS[:]), [ld1])
    cc_fs.append(emit("gpsimd", lambda e: e.collective_compute(
        "AllGather", OP.bypass, replica_groups=[list(range(8))],
        ins=[ag_f_in[16:24].opt()], outs=[ag_f_out3.opt()]), [st_f], "cc"))

    for m in range(H):
        base = 64 * (m % 2)
        for s in range(NS):
            qmf_src = ARENA[base:base + 64,
                            QMF_O + (m // 2) * 1024 + s * 512:QMF_O + (m // 2) * 1024 + (s + 1) * 512]
            nm = top(lambda e, base=base, m=m, qmf_src=qmf_src: e.matmul(
                out=PU[0:1, 0:512],
                lhsT=ARENA[base:base + 64, KVB_O + 1024 + (m // 2):KVB_O + 1024 + (m // 2) + 1],
                rhs=qmf_src, start=True, stop=True),
                waits=[kvb] + take("PU"), inc="t")
            n1 = vop(lambda e: e.tensor_scalar(out=XLN[0:1, 0:512], in0=PU[0:1, 0:512], scalar1=1e-6,
                                               scalar2=0.0, op0=OP.add, op1=OP.bypass),
                     [nm] + take("NR1") + take("XLN_q"))
            use("PU", n1)
            n2 = vop(lambda e: e.reciprocal(out=YST[0:1, 0:512], in_=XLN[0:1, 0:512]), [n1])
            use("XLN_q", n2)
            nb = top(lambda e: e.matmul(out=P2[:], lhsT=ONESR[0:1, :], rhs=YST[0:1, 0:512],
                                        start=True, stop=True),
                     waits=[n2] + take("P2"), inc="t")
            use("NR1", nb)
            n3 = aop(lambda e: e.copy(out=BC1[:], in_=P2[:]), [nb] + take("BC1") + take("BC"))
            use("P2", n3)
            oh = top(lambda e, base=base, m=m, qmf_src=qmf_src: e.matmul(
                out=P1[:],
                lhsT=ARENA[base:base + 64, KVB_O + (m // 2) * 128:KVB_O + (m // 2) * 128 + 128],
                rhs=qmf_src, start=True, stop=True),
                waits=take("P1"), inc="t")
            o1 = vop(lambda e, m=m, s=s: e.tensor_tensor(
                out=WK[:, m * T + s * 512:m * T + (s + 1) * 512],
                in0=P1[:], in1=BC1[:], op=OP.mult), [oh, n3])
            use("P1", o1)
            use("BC1", o1)
    attn_f = o1
    stats_done = {}

    def emit_stats(m, s, sl, a2):
        if s == 0:
            sum_out, sq_out = S[0:1, 0:512], PU[0:1, 0:512]
        else:
            sum_out, sq_out = KVP[0:1, 0:512], S[64:65, 0:512]
        sumw = top(lambda e, m=m, sl=sl, sum_out=sum_out: e.matmul(
            out=sum_out, lhsT=ONESF[:], rhs=XF[:, m * T:m * T + T][:, sl],
            start=(m == 0), stop=(m == H - 1), skip_group_check=True),
            waits=[a2] + (take("S_stat") if (m == 0 and s == 0) else []) +
                  (take("KVP") if (m == 0 and s == 1) else []),
            inc="t" if m == H - 1 else None)
        sq = aop(lambda e, m=m, sl=sl: e.activation(
            out=QB[:], in_=XF[:, m * T:m * T + T][:, sl], func=AF.Square),
            [a2] + take("QBD"))
        sq2 = top(lambda e, m=m, sq_out=sq_out: e.matmul(
            out=sq_out, lhsT=ONESF[:], rhs=QB[:],
            start=(m == 0), stop=(m == H - 1), skip_group_check=True),
            waits=[sq] + (take("PU") if (m == 0 and s == 0) else []), inc="t")
        use("QBD", sq2)
        if m == H - 1:
            stats_done[s] = (sumw, sq2)

    pend_stats = None
    for m in range(H):
        lo = dma("sync", "lwo", ARENA[:, WOB_O:WOB_O + 2048],
                 ag_a_out[au("wo", m)].rearrange("p k c -> p (k c)"), waits=take("WOB"))
        for s in range(NS):
            sl = slice(s * 512, (s + 1) * 512)
            pd, pname = (P1, "P1") if (m * NS + s) % 2 == 0 else (P2, "P2")
            for k in range(KC):
                tw = top(lambda e, k=k, sl=sl, pd=pd: e.matmul(
                    out=pd[:], lhsT=ARENA[:, WOB_O + k * 128:WOB_O + (k + 1) * 128],
                    rhs=WK[:, k * T:k * T + T][:, sl], start=(k == 0), stop=(k == KC - 1)),
                    waits=([lo, attn_f] + take(pname)) if k == 0 else (),
                    inc="t" if k == KC - 1 else None)
            use("WOB", tw)
            if pend_stats is not None:
                emit_stats(*pend_stats)
            a1 = aop(lambda e, m=m, pd=pd: e.activation(out=HSB[:], in_=pd[:],
                                                 func=AF.Identity,
                                                 bias=BO[:, m:m + 1], scale=1.0),
                     [tw] + take("HSB"))
            use(pname, a1)
            a2 = vop(lambda e, m=m, sl=sl: e.tensor_tensor(
                out=XF[:, m * T:m * T + T][:, sl], in0=XF[:, m * T:m * T + T][:, sl],
                in1=HSB[:], op=OP.add), [a1])
            use("HSB", a2)
            pend_stats = (m, s, sl, a2)
    attn_done = a2
    emit_stats(*pend_stats)

    gbuf = [GW0, GW1]
    ubuf = [UW0, UW1]

    def emit_deqx(s, k, tok):
        l2 = ln_apply(s, k, XLN[:], tok, take("XLN_q"))
        dq = quant512(XLN[:], WK[:, k * T + s * 512:k * T + (s + 1) * 512],
                      extra_waits=[l2] + ([attn_done, st_f] + take("QBD")
                                          if (s == 0 and k == 0) else []))
        use("XLN_q", dq)
        return dq

    tok0 = ln_setup(0, [attn_done],
                    pre_stats=stats_done[0] + (S[0:1, 0:512], PU[0:1, 0:512]))
    deqx_last = None
    for k in range(KC):
        deqx_last = emit_deqx(0, k, tok0)
    for s in range(NS):
        sl = slice(s * 512, (s + 1) * 512)
        for c in range(IC):
            gb, ub = gbuf[c % 2], ubuf[c % 2]
            lg = dma("sync", "gw%d" % (c % 2), FBB[:, gb:gb + 2048],
                     ag_f_out1[(c % 8) * 8 + c // 8].rearrange("p k c2 -> p (k c2)"),
                     waits=([cc_fs[0], deqx_last] if (s == 0 and c < 2) else []) + take("GB%d" % (c % 2)))
            lu = dma("sync", "uw%d" % (c % 2), FBB[:, ub:ub + 2048],
                     ag_f_out2[(c % 8) * 8 + c // 8].rearrange("p k c2 -> p (k c2)"),
                     waits=([cc_fs[1]] if (s == 0 and c < 2) else []) + take("UB%d" % (c % 2)))
            for k in range(KC):
                tg = top(lambda e, k=k, sl=sl, gb=gb: e.matmul(
                    out=PG[:], lhsT=FBB[:, gb + k * 128:gb + (k + 1) * 128],
                    rhs=WK[:, k * T:k * T + T][:, sl], start=(k == 0), stop=(k == KC - 1)),
                    waits=([lg, deqx_last] + take("PG")) if k == 0 else (),
                    inc="t" if k == KC - 1 else None)
            use("GB%d" % (c % 2), tg)
            for k in range(KC):
                tu = top(lambda e, k=k, sl=sl, ub=ub: e.matmul(
                    out=PU[:], lhsT=FBB[:, ub + k * 128:ub + (k + 1) * 128],
                    rhs=WK[:, k * T:k * T + T][:, sl], start=(k == 0), stop=(k == KC - 1)),
                    waits=([lu] + take("PU")) if k == 0 else (),
                    inc="t" if k == KC - 1 else None)
            use("UB%d" % (c % 2), tu)
            s1 = aop(lambda e: e.activation(out=HSB[:], in_=PG[:], func=AF.Silu),
                     [tg] + take("HSB"))
            use("PG", s1)
            s2 = vop(lambda e: e.tensor_tensor(out=XLN[:], in0=HSB[:], in1=PU[:], op=OP.mult),
                     [s1, tu] + take("XLN_q"))
            use("HSB", s2)
            use("PU", s2)
            hq = quant512(XLN[:], ARENA[:, c * 512:(c + 1) * 512],
                          extra_waits=[s2] + ([st_f] if (s == 0 and c == 0) else []) +
                          (take("DEQH") if c == 0 else []))
            use("XLN_q", hq)
        deqh_done = hq
        if s + 1 < NS:
            tok_n = ln_setup(s + 1, [], bc_ps=(PG, "PG", PU, "PU"),
                             pre_stats=stats_done[1] + (KVP[0:1, 0:512],
                                                        S[64:65, 0:512]))
        dwb_bufs = [DW0, GW0]
        for m in range(H):
            pd = P1 if (m % 2 == 0) else P2
            pname = "P1" if (m % 2 == 0) else "P2"
            for cg in range(4):
                dpar = (m * 4 + cg) % 2
                dwo = dwb_bufs[dpar]
                dsem = "dw" if dpar == 0 else "gw0"
                g_dw = m * 4 + cg
                ldw = dma("sync", dsem, FBB[:, dwo:dwo + 2048],
                          ag_f_out3[(g_dw % 8) * 8 + g_dw // 8].rearrange("p k c2 -> p (k c2)"),
                          waits=([deqh_done] if (m == 0 and cg == 0) else []) +
                                ([cc_fs[2]] if (s == 0 and m == 0 and cg < 2) else []) +
                                take("DWB%d" % dpar) + (take("GB0") if (m == 0 and cg == 1) else []))
                for cl in range(16):
                    c = cg * 16 + cl
                    td = top(lambda e, cl=cl, c=c, pd=pd, dwo=dwo: e.matmul(
                        out=pd[:], lhsT=FBB[:, dwo + cl * 128:dwo + (cl + 1) * 128],
                        rhs=ARENA[:, c * 512:(c + 1) * 512],
                        start=(c == 0), stop=(c == IC - 1)),
                        waits=([ldw] + (take(pname) if cg == 0 else [])) if cl == 0 else (),
                        inc="t" if (cl == 15) else None)
                use("DWB%d" % dpar, td)
            yv = vop(lambda e, m=m, sl=sl, pd=pd: e.tensor_tensor(
                out=YST[:], in0=pd[:], in1=XF[:, m * T:m * T + T][:, sl], op=OP.add),
                [td] + take("YST"))
            use(pname, yv)
            yst = dma("sync", "yo", y_t[m * 128:(m + 1) * 128, sl], YST[:], waits=[yv])
            use("YST", yst)
            if s + 1 < NS:
                deqx_last = emit_deqx(s + 1, m, tok_n)
        if s == 0:
            use("DEQH", td)

    final_tok = yst
    prog.append(("sync", (final_tok,), None, (None, 0)))

    eng_prog = {k: [] for k in ["sync", "scalar", "vector", "tensor", "gpsimd"]}
    for eng, waits, fn, inc in prog:
        eng_prog[eng].append((waits, fn, inc))

    with nc.Block() as block:
        def make_section(entries):
            def body(e):
                for waits, fn, (incname, step) in entries:
                    for (sname, val) in waits:
                        e.wait_ge(semh[sname], val)
                    if fn is not None:
                        inst = fn(e)
                        if incname is not None:
                            inst.then_inc(semh[incname], step)
            return body

        block.sync(make_section(eng_prog["sync"]))
        block.scalar(make_section(eng_prog["scalar"]))
        block.vector(make_section(eng_prog["vector"]))
        block.tensor(make_section(eng_prog["tensor"]))
        block.gpsimd(make_section(eng_prog["gpsimd"]))

    es.close()
    return nc


def _prep_inputs(ins):
    f32 = np.float32
    x = np.ascontiguousarray(ins["L0_attn_hidden_states"], dtype=f32).reshape(8192, D)
    wq = ins["L0_attn_wq"]
    wk = ins["L0_attn_wk"]
    wv = ins["L0_attn_wv"]
    wo = ins["L0_attn_wo"]

    def tile_w(w, rows):
        return np.ascontiguousarray(
            w.reshape(rows // 128, 128, w.shape[1] // 128, 128).transpose(0, 3, 2, 1),
            dtype=f32)

    wq_t = tile_w(wq, D)
    wk_t = tile_w(wk, D)
    wv_t = tile_w(wv, D)
    wo_t = tile_w(wo, D)
    attn_units = np.concatenate([wq_t, wk_t, wv_t, wo_t], axis=0)
    def bt32(a):
        n = a.shape[0]
        b = a.reshape(n, 4, 32, 16, 4, 32)
        return np.ascontiguousarray(b.transpose(0, 1, 5, 3, 4, 2)).reshape(a.shape)

    gw_t = tile_w(ins["L0_ffn_gate_w"], 8192)
    uw_t = tile_w(ins["L0_ffn_up_w"], 8192)
    dw_t = tile_w(ins["L0_ffn_down_w"], D)
    dw_u = np.ascontiguousarray(
        dw_t.reshape(16, 128, 4, 16, 128).transpose(0, 2, 1, 3, 4)).reshape(64, 128, 16, 128)
    ffn_units = bt32(np.concatenate([gw_t, uw_t, dw_u], axis=0))

    in_maps = []
    for c in range(8):
        xs = np.ascontiguousarray(x[c * T:(c + 1) * T].T)
        in_maps.append({
            "x_t": xs,
            "w_attn": np.ascontiguousarray(attn_units[c * AU_PC:(c + 1) * AU_PC]),
            "w_ffn": np.ascontiguousarray(ffn_units[c::8]),
            "bq": np.ascontiguousarray(ins["L0_attn_bq"], dtype=f32).reshape(16, 128).T.copy(),
            "bk": np.ascontiguousarray(ins["L0_attn_bk"], dtype=f32).reshape(16, 128).T.copy(),
            "bv": np.ascontiguousarray(ins["L0_attn_bv"], dtype=f32).reshape(16, 128).T.copy(),
            "bo": np.ascontiguousarray(ins["L0_attn_bo"], dtype=f32).reshape(16, 128).T.copy(),
            "wmq": np.ascontiguousarray(ins["L0_attn_wmap_q"].T, dtype=f32),
            "wmk": np.ascontiguousarray(ins["L0_attn_wmap_k"].T, dtype=f32),
        })
    return in_maps


def kernel(**inputs):
    in_maps = _prep_inputs(inputs)
    nc = build_nc()
    res_ = run_bass_kernel_spmd(nc, in_maps, core_ids=list(range(8)))
    outs = []
    for c in range(8):
        yt = np.asarray(res_.results[c]["y_t"])
        outs.append(np.ascontiguousarray(yt.T))
    y = np.concatenate(outs, axis=0).reshape(2, 4096, D).astype(np.float32)
    return y


if __name__ == "__main__":
    import reference as _r
    ins = _r.setup_inputs()
    out = kernel(**{k: np.asarray(v) for k, v in ins.items()})
    ref = np.asarray(_r.reference(**ins))
    err = np.abs(out - ref)
    print("max abs err:", err.max(), "rms rel:", float(np.sqrt((err ** 2).mean() / (ref ** 2).mean())))


# version 24
# speedup vs baseline: 1.0710x; 1.0152x over previous
import contextlib
import numpy as np

import concourse.bass as bass
import concourse.mybir as mybir
from concourse.bass_utils import run_bass_kernel_spmd

F32 = mybir.dt.float32
I32 = mybir.dt.int32
BF16 = mybir.dt.bfloat16
OP = mybir.AluOpType
AF = mybir.ActivationFunctionType
AX = mybir.AxisListType

MAGIC = 12582912.0
D = 2048
KC = 16
H = 16
T = 1024
NS = 2
IC = 64
N_AU = 64
N_FU = 192
AU_PC = N_AU // 8
FU_PC = N_FU // 8


def build_nc():
    nc = bass.Bass("TRN2", target_bir_lowering=False)

    x_t = nc.dram_tensor("x_t", [D, T], F32, kind="ExternalInput").ap()
    w_attn = nc.dram_tensor("w_attn", [AU_PC, 128, 16, 128], F32, kind="ExternalInput").ap()
    w_ffn = nc.dram_tensor("w_ffn", [FU_PC, 128, 16, 128], F32, kind="ExternalInput").ap()
    bq_d = nc.dram_tensor("bq", [128, 16], F32, kind="ExternalInput").ap()
    bk_d = nc.dram_tensor("bk", [128, 16], F32, kind="ExternalInput").ap()
    bv_d = nc.dram_tensor("bv", [128, 16], F32, kind="ExternalInput").ap()
    bo_d = nc.dram_tensor("bo", [128, 16], F32, kind="ExternalInput").ap()
    wmq_d = nc.dram_tensor("wmq", [128, 64], F32, kind="ExternalInput").ap()
    wmk_d = nc.dram_tensor("wmk", [128, 64], F32, kind="ExternalInput").ap()
    y_t = nc.dram_tensor("y_t", [D, T], F32, kind="ExternalOutput").ap()

    ag_a_in = nc.dram_tensor("ag_a_in", [AU_PC, 128, 16, 128], BF16).ap()
    ag_a_out = nc.dram_tensor("ag_a_out", [N_AU, 128, 16, 128], BF16).ap()
    ag_f_in = nc.dram_tensor("ag_f_in", [FU_PC, 128, 16, 128], BF16).ap()
    ag_f_out1 = nc.dram_tensor("ag_f_out1", [64, 128, 16, 128], BF16).ap()
    ag_f_out2 = nc.dram_tensor("ag_f_out2", [64, 128, 16, 128], BF16).ap()
    ag_f_out3 = nc.dram_tensor("ag_f_out3", [64, 128, 16, 128], BF16).ap()
    cc_in = nc.dram_tensor("cc_in", [128, 1032], F32).ap()
    cc_out = nc.dram_tensor("cc_out", [128, 1032], F32).ap()

    es = contextlib.ExitStack()

    def sb(name, shape, dt):
        return es.enter_context(nc.sbuf_tensor(name, shape, dt))

    XF = sb("XF", [128, KC * T], F32)
    WK = sb("WK", [128, KC * T], BF16)
    ARENA = sb("ARENA", [128, 32768], BF16)
    QMF_O, KMT_O = 0, 8192
    WQB_O, WKB_O, WVB_O, WOB_O = 16384, 18432, 20480, 22528
    KF_O, QF_O, VF_O, VT_O, KVB_O, WST_O = 24576, 25088, 25600, 26112, 26368, 27400
    FUST = sb("FUST", [128, 5120], F32)
    FBB = FUST[:].bitcast(BF16)
    GW0, GW1, UW0, UW1, DW0 = 0, 2048, 4096, 6144, 8192
    YST = sb("YST", [128, 512], F32)
    XLN = sb("XLN", [128, 512], F32)
    HSB = sb("HSB", [128, 512], F32)
    KVS = sb("KVS", [128, 1032], F32)
    BC1 = sb("BC1", [128, 512], F32)
    BC2 = sb("BC2", [128, 512], F32)
    QA = sb("QA", [128, 512], F32)
    QB = sb("QB", [128, 512], F32)
    QE = sb("QE", [128, 512], F32)
    QC = sb("QC", [128, 512], I32)
    QD = sb("QD", [128, 512], I32)
    QG = sb("QG", [128, 512], BF16)
    QM = sb("QM", [128, 16], F32)
    QS = sb("QS", [128, 16], F32)
    QR = sb("QR", [128, 16], F32)
    BQ = sb("BQ", [128, 16], F32)
    BK = sb("BK", [128, 16], F32)
    BV = sb("BV", [128, 16], F32)
    BO = sb("BO", [128, 16], F32)
    WMQ = sb("WMQ", [128, 64], BF16)
    WMK = sb("WMK", [128, 64], BF16)
    ONESF = sb("ONESF", [128, 1], F32)
    ONESR = sb("ONESR", [128, 128], F32)
    ONESB = sb("ONESB", [128, 1], BF16)
    EPSB = sb("EPSB", [128, 1], F32)
    MBIAS = sb("MBIAS", [128, 3], F32)
    KMH = sb("KMH", [128, 128], F32)
    IDF = sb("IDF", [128, 128], F32)
    VFF = YST

    P1 = es.enter_context(nc.psum_tensor("P1", [128, 512], F32))
    P2 = es.enter_context(nc.psum_tensor("P2", [128, 512], F32))
    PG = es.enter_context(nc.psum_tensor("PG", [128, 512], F32))
    PU = es.enter_context(nc.psum_tensor("PU", [128, 512], F32))
    KVP = es.enter_context(nc.psum_tensor("KVP", [128, 1024], F32))
    S = es.enter_context(nc.psum_tensor("S", [128, 512], F32))
    PT = es.enter_context(nc.psum_tensor("PT", [128, 256], F32))

    semh = {}
    for s_ in ["a", "v", "t", "gc", "cc",
               "x", "bl", "wm", "aga", "agf0", "agf1", "agf2", "ccd",
               "ust0", "ust1",
               "lwq", "lwk", "lwv", "lwo", "lwo2", "gw0", "gw1", "uw0", "uw1",
               "dw", "yo"]:
        semh[s_] = es.enter_context(nc.semaphore("sem_" + s_))
    cnt = {k: 0 for k in semh}
    prog = []

    def emit(eng, fn, waits=(), inc=None, step=1):
        waits = [w for w in waits if w is not None]
        tok = None
        if inc is not None:
            cnt[inc] += step
            tok = (inc, cnt[inc])
        prog.append((eng, tuple(waits), fn, (inc, step)))
        return tok

    def vop(fn, waits=()):
        return emit("vector", fn, waits, "v")

    def aop(fn, waits=()):
        return emit("scalar", fn, waits, "a")

    def gop(fn, waits=()):
        return emit("gpsimd", fn, waits, "gc")

    def top(fn, waits=(), inc=None):
        return emit("tensor", fn, waits, inc)

    def dma(eng, sem, out, in_, waits=()):
        return emit(eng, lambda e, out=out, in_=in_: e.dma_start(out=out, in_=in_),
                    waits, sem, 16)

    res = {}

    def use(name, tok):
        res.setdefault(name, []).append(tok)

    def take(name):
        return res.pop(name, [])

    qprev = [None]

    def quant512(src, dst, extra_waits=(), pre_t=False):
        qa3 = (src if pre_t else QA[:]).rearrange("p (b k) -> p b k", k=32)
        qb3 = QB[:].rearrange("p (b k) -> p b k", k=32)
        qg3 = QG[:].rearrange("p (b k) -> p b k", k=32)
        s_b = QS[:].unsqueeze(-1).broadcast_to((128, 16, 32))
        r_b = QR[:].unsqueeze(-1).broadcast_to((128, 16, 32))
        if pre_t:
            first_w = list(extra_waits) + ([qprev[0]] if qprev[0] else [])
        else:
            w0 = vop(lambda e, src=src: e.transpose(out=QA[:], in_=src),
                     list(extra_waits) + ([qprev[0]] if qprev[0] else []))
            first_w = [w0]
        w1 = vop(lambda e, qa3=qa3: e.tensor_reduce(out=QM[:], in_=qa3, axis=AX.X,
                                                    op=OP.max, apply_absolute_value=True),
                 first_w)
        w2 = vop(lambda e: e.tensor_scalar(out=QS[:], in0=QM[:], scalar1=float(1.0 / 448.0),
                                           scalar2=1e-30, op0=OP.mult, op1=OP.max), [w1])
        w3 = vop(lambda e: e.reciprocal(out=QR[:], in_=QS[:]), [w2])
        w4 = vop(lambda e, qa3=qa3, qb3=qb3, r_b=r_b: e.tensor_tensor(
            out=qb3, in0=qa3, in1=r_b, op=OP.mult), [w3])
        w5 = vop(lambda e: e.tensor_scalar(out=QC[:], in0=QB[:].bitcast(I32), scalar1=0x7F800000,
                                           scalar2=0, op0=OP.bitwise_and, op1=OP.bypass), [w4])
        w6 = vop(lambda e: e.tensor_scalar(out=QD[:], in0=QC[:], scalar1=-25165824.0,
                                           scalar2=989855744.0, op0=OP.add, op1=OP.max), [w5])
        w7 = vop(lambda e: e.tensor_scalar(out=QE[:], in0=QC[:], scalar1=-1,
                                           scalar2=2155872256.0, op0=OP.mult, op1=OP.add), [w5])
        w8 = vop(lambda e: e.tensor_scalar(out=QC[:], in0=QE[:], scalar1=1140850688.0,
                                           scalar2=0, op0=OP.min, op1=OP.bypass), [w7, w6])
        w9 = vop(lambda e: e.tensor_tensor(out=QA[:], in0=QB[:], in1=QC[:].bitcast(F32),
                                           op=OP.mult), [w8])
        w13 = aop(lambda e: e.activation(out=QA[:], in_=QA[:], func=AF.Identity,
                                         bias=MBIAS[:, 0:1], scale=1.0), [w9, mb1, mb2, mb2b])
        w14 = aop(lambda e: e.activation(out=QA[:], in_=QA[:], func=AF.Identity,
                                         bias=MBIAS[:, 2:3], scale=1.0), [w13])
        w15 = vop(lambda e: e.tensor_tensor(out=QB[:], in0=QA[:], in1=QD[:].bitcast(F32),
                                            op=OP.mult), [w14, w6])
        w16 = vop(lambda e, qb3=qb3, qg3=qg3, s_b=s_b: e.tensor_tensor(
            out=qg3, in0=qb3, in1=s_b, op=OP.mult), [w15])
        w17 = vop(lambda e, dst=dst: e.transpose(out=dst, in_=QG[:]), [w16])
        qprev[0] = w17
        return w17

    for k in range(KC):
        dma("scalar", "x", XF[:, k * T:(k + 1) * T], x_t[k * 128:(k + 1) * 128, :])
    x_ld = ("x", cnt["x"])
    dma("scalar", "bl", BQ[:, 0:16], bq_d)
    dma("scalar", "bl", BK[:, 0:16], bk_d)
    dma("scalar", "bl", BV[:, 0:16], bv_d)
    dma("scalar", "bl", BO[:, 0:16], bo_d)
    b_ld = ("bl", cnt["bl"])
    dma("gpsimd", "wm", WMQ[:], wmq_d)
    dma("gpsimd", "wm", WMK[:], wmk_d)
    wm_ld = ("wm", cnt["wm"])
    on1 = gop(lambda e: e.memset(ONESF[:], 1.0))
    on1b = gop(lambda e: e.memset(ONESR[:], 1.0))
    on2 = gop(lambda e: e.memset(ONESB[:], 1.0))
    mb1 = gop(lambda e: e.memset(MBIAS[:, 0:1], MAGIC))
    mb2 = gop(lambda e: e.memset(MBIAS[:, 1:2], -MAGIC / 512.0))
    mb2b = gop(lambda e: e.memset(MBIAS[:, 2:3], -MAGIC))
    eps1 = gop(lambda e: e.memset(EPSB[:], 1e-5))
    io1 = gop(lambda e: e.iota(HSB[:, 0:128], [[1, 128]], channel_multiplier=-1,
                               allow_small_or_imprecise_dtypes=True))
    id1 = vop(lambda e: e.tensor_scalar(out=IDF[:], in0=HSB[:, 0:128], scalar1=0.0,
                                        scalar2=0, op0=OP.is_equal, op1=OP.bypass), [io1])
    use("HSB", id1)

    st_prev = None
    for u in range(AU_PC):
        ld = dma("sync", "ust0", FUST[:, 0:2048].rearrange("p (k m) -> p k m", k=16),
                 w_attn[u], waits=take("UST0"))
        cv = vop(lambda e: e.tensor_copy(out=ARENA[:, WST_O:WST_O + 2048],
                                         in_=FUST[:, 0:2048]),
                 [ld] + take("WST"))
        use("UST0", cv)
        st = dma("sync", "aga", ag_a_in[u],
                 ARENA[:, WST_O:WST_O + 2048].rearrange("p (k m) -> p k m", k=16),
                 waits=[cv])
        use("WST", st)
        st_prev = st
    cc_a = emit("gpsimd", lambda e: e.collective_compute(
        "AllGather", OP.bypass, replica_groups=[list(range(8))],
        ins=[ag_a_in.opt()], outs=[ag_a_out.opt()]), [st_prev], "cc")

    ffn_lds = {}
    cc_fs = []

    def load_ffn_unit(u):
        o = 2048 * (u % 2)
        ffn_lds[u] = dma("sync", "ust%d" % (u % 2),
                         FUST[:, o:o + 2048].rearrange("p (k m) -> p k m", k=16),
                         w_ffn[u], waits=take("UST%d" % (u % 2)))

    def prep_ffn_unit(u):
        if u not in ffn_lds:
            load_ffn_unit(u)
        ld = ffn_lds[u]
        o = 2048 * (u % 2)
        wq = None
        for j in range(4):
            wq = quant512(FUST[:, o + j * 512:o + (j + 1) * 512],
                          ARENA[:, WST_O + j * 512:WST_O + (j + 1) * 512],
                          extra_waits=([ld] + take("WST")) if j == 0 else (),
                          pre_t=True)
            use("UST%d" % (u % 2), wq)
        st = dma("sync", "agf%d" % (u // 8), ag_f_in[u],
                 ARENA[:, WST_O:WST_O + 2048].rearrange("p (k m) -> p k m", k=16),
                 waits=[wq])
        use("WST", st)
        if u % 8 == 7 and u // 8 < 2:
            cc_fs.append(emit("gpsimd", lambda e, u=u: e.collective_compute(
                "AllGather", OP.bypass, replica_groups=[list(range(8))],
                ins=[ag_f_in[u - 7:u + 1].opt()],
                outs=[[ag_f_out1, ag_f_out2][u // 8].opt()]),
                [st], "cc"))
        return st

    def ln_setup(s, first_waits, bc_ps=None, pre_stats=None):
        pa, pan, pb, pbn = bc_ps or (P1, "P1", P2, "P2")
        sl = slice(s * 512, (s + 1) * 512)
        if pre_stats is not None:
            sum_t, sq_t, sum_src, sq_src = pre_stats
            r2 = 0 if s == 0 else 64
            ts1 = aop(lambda e, sum_src=sum_src: e.copy(out=XLN[0:1, 0:512], in_=sum_src),
                      [sum_t] + list(first_waits) + take("XLN_q"))
            if s == 0:
                ts2 = aop(lambda e, sq_src=sq_src: e.copy(out=BC2[0:1, 0:512], in_=sq_src),
                          [sq_t] + take("BC2"))
                use("S_stat", ts1)
                use("PU", ts2)
            else:
                ts2 = vop(lambda e, sq_src=sq_src: e.tensor_copy(
                    out=BC2[64:65, 0:512], in_=sq_src), [sq_t] + take("BC2"))
                use("KVP", ts1)
                use("S_stat", ts2)
            return ln_bcast(ts1, ts2, pa, pan, pb, pbn, r2)
        for k in range(KC):
            top(lambda e, k=k, sl=sl: e.matmul(
                out=S[0:1, 0:512], lhsT=ONESF[:],
                rhs=XF[:, k * T:k * T + T][:, sl],
                start=(k == 0), stop=(k == KC - 1)),
                waits=(list(first_waits) + [x_ld, on1, on1b] + take("S_stat")) if k == 0 else ())
        for k in range(KC):
            sq = aop(lambda e, k=k, sl=sl: e.activation(
                out=HSB[:], in_=XF[:, k * T:k * T + T][:, sl], func=AF.Square),
                take("HSB") + ((take("PU") + list(first_waits) + [x_ld]) if k == 0 else []))
            tk = top(lambda e, k=k: e.matmul(
                out=PU[0:1, 0:512], lhsT=ONESF[:], rhs=HSB[:],
                start=(k == 0), stop=(k == KC - 1)), waits=[sq], inc="t")
            use("HSB", tk)
        ts1 = aop(lambda e: e.copy(out=XLN[0:1, 0:512], in_=S[0:1, 0:512]),
                  [tk] + take("XLN_q"))
        ts2 = aop(lambda e: e.copy(out=BC2[0:1, 0:512], in_=PU[0:1, 0:512]),
                  [tk] + take("BC2"))
        use("S_stat", ts1)
        use("PU", ts2)
        return ln_bcast(ts1, ts2, pa, pan, pb, pbn, 0)

    def ln_bcast(ts1, ts2, pa, pan, pb, pbn, r2):
        b1 = top(lambda e, pa=pa: e.matmul(out=pa[0:128, 0:512], lhsT=ONESR[0:1, :],
                                           rhs=XLN[0:1, 0:512],
                                    start=True, stop=True), waits=[ts1] + take(pan), inc="t")
        use("XLN_q", b1)
        b2 = top(lambda e, pb=pb, r2=r2: e.matmul(out=pb[0:128, 0:512],
                                           lhsT=ONESR[r2:r2 + 1, :],
                                           rhs=BC2[r2:r2 + 1, 0:512],
                                    start=True, stop=True), waits=[ts2] + take(pbn), inc="t")
        use("BC2_row", b2)
        m1 = vop(lambda e, pa=pa: e.tensor_scalar(out=BC1[:], in0=pa[0:128, 0:512],
                                           scalar1=float(1.0 / D),
                                           scalar2=0.0, op0=OP.mult, op1=OP.add),
                 [b1] + take("BC1"))
        use(pan, m1)
        use("S_stat", m1)
        m2 = vop(lambda e, pb=pb: e.tensor_scalar(out=BC2[:], in0=pb[0:128, 0:512],
                                           scalar1=float(1.0 / D),
                                           scalar2=0.0, op0=OP.mult, op1=OP.add),
                 [b2] + take("BC2") + take("BC2_row"))
        use(pbn, m2)
        m3 = vop(lambda e: e.tensor_tensor(out=HSB[:], in0=BC1[:], in1=BC1[:], op=OP.mult),
                 [m1] + take("HSB"))
        m4 = vop(lambda e: e.tensor_tensor(out=XLN[:], in0=BC2[:], in1=HSB[:], op=OP.subtract),
                 [m3, m2] + take("XLN_q"))
        use("HSB", m4)
        m6 = aop(lambda e: e.activation(out=BC2[:], in_=XLN[:], func=AF.Sqrt,
                                        bias=EPSB[:, 0:1], scale=1.0), [m4, eps1])
        m7 = vop(lambda e: e.reciprocal(out=XLN[:], in_=BC2[:]), [m6])
        m8 = vop(lambda e: e.tensor_tensor(out=BC2[:], in0=BC1[:], in1=XLN[:], op=OP.mult), [m7])
        m9 = aop(lambda e: e.copy(out=BC1[:], in_=XLN[:]), [m7, m8])
        use("XLN_q", m9)
        use("XLN_q", m8)
        m10 = vop(lambda e: e.engine_nop(), [m8, m9])
        return m10

    def ln_apply(s, k, dst, ln_tok, dst_waits=()):
        sl = slice(s * 512, (s + 1) * 512)
        h1 = vop(lambda e, k=k, sl=sl: e.tensor_tensor(
            out=HSB[:], in0=XF[:, k * T:k * T + T][:, sl], in1=BC1[:], op=OP.mult),
            [ln_tok] + take("HSB") + list(dst_waits))
        h2 = vop(lambda e, dst=dst: e.tensor_tensor(out=dst, in0=HSB[:], in1=BC2[:],
                                                    op=OP.subtract), [h1])
        use("HSB", h2)
        use("BC1", h1)
        use("BC2", h2)
        return h2

    ln1_last = None
    for s in range(NS):
        tok = ln_setup(s, [])
        use("BC", tok)
        for k in range(KC):
            ln1_last = ln_apply(s, k, WK[:, k * T + s * 512:k * T + (s + 1) * 512], tok)
    ln1_all = ln1_last

    def au(mat, m):
        return {"wq": 0, "wk": 16, "wv": 32, "wo": 48}[mat] + m

    blocks = [(m, s) for m in range(H) for s in range(NS)]
    ablk = {}

    def emit_A(m, s):
        d = {}
        lq, lk, lv = ablk[(m, "loads")]
        qfo = 29448 if s == 0 else 29960
        kfo = KF_O if s == 0 else KF_O + 512
        vff = YST if s == 0 else XLN
        sl = slice(s * 512, (s + 1) * 512)
        for k in range(KC):
            tq = top(lambda e, k=k, sl=sl: e.matmul(
                out=P1[:], lhsT=ARENA[:, WQB_O + k * 128:WQB_O + (k + 1) * 128],
                rhs=WK[:, k * T:k * T + T][:, sl], start=(k == 0), stop=(k == KC - 1)),
                waits=([lq, ln1_all, b_ld] + take("P1")) if k == 0 else (),
                inc="t" if k == KC - 1 else None)
        if s == NS - 1:
            use("WQB", tq)
        eq = aop(lambda e, m=m, qfo=qfo: e.activation(out=ARENA[:, qfo:qfo + 512], in_=P1[:],
                                             func=AF.Identity, bias=BQ[:, m:m + 1],
                                             scale=1.0), [tq] + take("QF%d" % s))
        use("P1", eq)
        for k in range(KC):
            tk = top(lambda e, k=k, sl=sl: e.matmul(
                out=P2[:], lhsT=ARENA[:, WKB_O + k * 128:WKB_O + (k + 1) * 128],
                rhs=WK[:, k * T:k * T + T][:, sl], start=(k == 0), stop=(k == KC - 1)),
                waits=([lk] + take("P2")) if k == 0 else (),
                inc="t" if k == KC - 1 else None)
        if s == NS - 1:
            use("WKB", tk)
        ek = aop(lambda e, m=m, kfo=kfo: e.activation(out=ARENA[:, kfo:kfo + 512], in_=P2[:],
                                             func=AF.Identity, bias=BK[:, m:m + 1],
                                             scale=1.0), [tk] + take("KF%d" % s))
        use("P2", ek)
        for k in range(KC):
            tv = top(lambda e, k=k, sl=sl: e.matmul(
                out=PG[:], lhsT=ARENA[:, WVB_O + k * 128:WVB_O + (k + 1) * 128],
                rhs=WK[:, k * T:k * T + T][:, sl], start=(k == 0), stop=(k == KC - 1)),
                waits=([lv] + take("PG")) if k == 0 else (),
                inc="t" if k == KC - 1 else None)
        if s == NS - 1:
            use("WVB", tv)
        ev = aop(lambda e, m=m, vff=vff: e.activation(out=vff[0:128, 0:512], in_=PG[:],
                                             func=AF.Identity, bias=BV[:, m:m + 1],
                                             scale=1.0),
                 [tv] + take("VF%d" % s) + (take("XLN_q") if s == 1 else []))
        use("PG", ev)
        d.update(eq=eq, ek=ek, ev=ev)
        return d

    def emit_B(m, s, d):
        eq, ek, ev = d["eq"], d["ek"], d["ev"]
        qfo = 29448 if s == 0 else 29960
        kfo = KF_O if s == 0 else KF_O + 512
        vff = YST if s == 0 else XLN
        qm = top(lambda e, qfo=qfo: e.matmul(out=PU[0:64, 0:512], lhsT=WMQ[:],
                                    rhs=ARENA[:, qfo:qfo + 512], start=True, stop=True),
                 waits=[eq, wm_ld] + take("PU"), inc="t")
        use("QF%d" % s, qm)
        e1 = aop(lambda e: e.activation(out=HSB[0:64, :], in_=PU[0:64, 0:512],
                                        func=AF.Relu, scale=-1.0),
                 [qm] + take("HSB"))
        e2 = aop(lambda e: e.activation(out=HSB[0:64, :], in_=HSB[0:64, :], func=AF.Exp,
                                        scale=-1.0), [e1])
        qmf_dst = ARENA[64 * (m % 2):64 * (m % 2) + 64,
                        QMF_O + (m // 2) * 1024 + s * 512:QMF_O + (m // 2) * 1024 + (s + 1) * 512]
        e3 = vop(lambda e, qmf_dst=qmf_dst: e.scalar_tensor_tensor(
            out=qmf_dst, in0=PU[0:64, 0:512], scalar=0.0, in1=HSB[0:64, :],
            op0=OP.max, op1=OP.add), [e2])
        use("HSB", e3)
        use("PU", e3)
        for j in range(4):
            tau = s * 4 + j
            pt_sl = slice(128 * (j % 2), 128 * (j % 2) + 128)
            vt_sl = slice(VT_O + 128 * (j % 2), VT_O + 128 * (j % 2) + 128)
            kmp = PU if (j % 2 == 0) else PG
            kmpn = "PU" if (j % 2 == 0) else "PG"
            hsl = slice(64 * (j % 2), 64 * (j % 2) + 64)
            km = top(lambda e, j=j, kmp=kmp, kfo=kfo: e.matmul(
                out=kmp[0:128, 0:64], lhsT=ARENA[:, kfo + j * 128:kfo + (j + 1) * 128],
                rhs=WMK[:], start=True, stop=True),
                waits=[ek, wm_ld] + take(kmpn) + ([e3] if j == 0 else []) +
                      ([ev] if j == 1 else []), inc="t")
            use("KF%d" % s, km)
            f1 = aop(lambda e, kmp=kmp, hsl=hsl: e.activation(
                out=KMH[:, hsl], in_=kmp[0:128, 0:64], func=AF.Relu, scale=-1.0),
                [km] + take("KMH%d" % (j % 2)))
            f2 = aop(lambda e, hsl=hsl: e.activation(out=KMH[:, hsl], in_=KMH[:, hsl],
                                                     func=AF.Exp, scale=-1.0), [f1])
            kmt_dst = ARENA[:, KMT_O + tau * 1024 + m * 64:KMT_O + tau * 1024 + (m + 1) * 64]
            f3 = vop(lambda e, kmt_dst=kmt_dst, kmp=kmp, hsl=hsl: e.scalar_tensor_tensor(
                out=kmt_dst, in0=kmp[0:128, 0:64], scalar=0.0, in1=KMH[:, hsl],
                op0=OP.max, op1=OP.add), [f2])
            use("KMH%d" % (j % 2), f3)
            use(kmpn, f3)
            vt = top(lambda e, j=j, pt_sl=pt_sl, vff=vff: e.transpose(
                out=PT[:, pt_sl], in_=vff[0:128, j * 128:(j + 1) * 128],
                identity=IDF[:]), waits=[ev, id1] + take("PT%d" % (j % 2)), inc="t")
            use("VF%d" % s, vt)
            if s == 1:
                use("XLN_q", vt)
            g1 = aop(lambda e, pt_sl=pt_sl, vt_sl=vt_sl: e.copy(
                out=ARENA[:, vt_sl], in_=PT[:, pt_sl]),
                [vt] + take("VT%d" % (j % 2)))
            use("PT%d" % (j % 2), g1)
            base = 64 * (m % 2)
            kv_start = (m % 8 in (0, 1)) and tau == 0
            kv_stop = (m % 8 in (6, 7)) and tau == NS * 4 - 1
            kv_last = top(lambda e, base=base, m=m, kmt_dst=kmt_dst, vt_sl=vt_sl,
                          kv_start=kv_start, kv_stop=kv_stop: e.matmul(
                out=KVP[base:base + 64, (m // 2) * 128:(m // 2) * 128 + 128],
                lhsT=kmt_dst, rhs=ARENA[:, vt_sl],
                start=kv_start, stop=kv_stop, skip_group_check=True),
                waits=[f3, g1] + (take("KVP") if (m == 0 and tau == 0) else []),
                inc="t")
            use("VT%d" % (j % 2), kv_last)
            ks_start = m in (0, 1) and tau == 0
            ks_stop = m in (H - 2, H - 1) and tau == NS * 4 - 1
            ks_tok = top(lambda e, base=base, m=m, kmt_dst=kmt_dst,
                         ks_start=ks_start, ks_stop=ks_stop: e.matmul(
                out=S[base:base + 64, (m // 2):(m // 2) + 1],
                lhsT=kmt_dst, rhs=ONESB[:],
                start=ks_start, stop=ks_stop, skip_group_check=True),
                waits=[on2] if (m in (0, 1) and tau == 0) else (),
                inc="t" if (m == H - 1 and tau == NS * 4 - 1) else None)
        return ks_tok

    pend = None
    for i, (m, s) in enumerate(blocks):
        if s == 0:
            if m > 0:
                us = list(range((m - 1) * FU_PC // H, m * FU_PC // H))
            else:
                us = []
            qh = [us[:(len(us) + 1) // 2], us[(len(us) + 1) // 2:]]
            ablk[(m, "qh")] = qh
            lq = dma("sync", "lwq", ARENA[:, WQB_O:WQB_O + 2048],
                     ag_a_out[au("wq", m)].rearrange("p k c -> p (k c)"),
                     waits=([cc_a] if m == 0 else []) + take("WQB"))
            lk = dma("sync", "lwk", ARENA[:, WKB_O:WKB_O + 2048],
                     ag_a_out[au("wk", m)].rearrange("p k c -> p (k c)"), waits=take("WKB"))
            lv = dma("sync", "lwv", ARENA[:, WVB_O:WVB_O + 2048],
                     ag_a_out[au("wv", m)].rearrange("p k c -> p (k c)"), waits=take("WVB"))
            ablk[(m, "loads")] = (lq, lk, lv)
        for u in ablk[(m, "qh")][s]:
            st_f = prep_ffn_unit(u)
        ablk[(m, s)] = emit_A(m, s)
        if pend is not None:
            ks_tok = emit_B(*pend, ablk[pend])
        pend = (m, s)
        if s == NS - 1:
            for u in range(m * FU_PC // H, (m + 1) * FU_PC // H):
                load_ffn_unit(u)
    ks_tok = emit_B(*pend, ablk[pend])
    for u in range((H - 1) * FU_PC // H, FU_PC):
        st_f = prep_ffn_unit(u)
    kv_done = ks_tok

    kvev = aop(lambda e: e.copy(out=KVS[:, 0:1024], in_=KVP[:]), [kv_done])
    ksev = vop(lambda e: e.tensor_scalar(out=KVS[:, 1024:1032], in0=S[:, 0:8], scalar1=1.0,
                                         scalar2=0.0, op0=OP.mult, op1=OP.add), [kv_done])
    use("KVP", kvev)
    use("S_stat", ksev)
    st1 = dma("sync", "ccd", cc_in, KVS[:], waits=[kvev, ksev])
    ar = emit("gpsimd", lambda e: e.collective_compute(
        "AllReduce", OP.add, replica_groups=[[0, 1, 2, 3], [4, 5, 6, 7]],
        ins=[cc_in.opt()], outs=[cc_out.opt()]), [st1], "cc")
    ld1 = dma("sync", "ccd", KVS[:], cc_out, waits=[ar])
    kvb = vop(lambda e: e.tensor_copy(out=ARENA[:, KVB_O:KVB_O + 1032], in_=KVS[:]), [ld1])
    cc_fs.append(emit("gpsimd", lambda e: e.collective_compute(
        "AllGather", OP.bypass, replica_groups=[list(range(8))],
        ins=[ag_f_in[16:24].opt()], outs=[ag_f_out3.opt()]), [st_f], "cc"))

    for m in range(H):
        base = 64 * (m % 2)
        for s in range(NS):
            qmf_src = ARENA[base:base + 64,
                            QMF_O + (m // 2) * 1024 + s * 512:QMF_O + (m // 2) * 1024 + (s + 1) * 512]
            nm = top(lambda e, base=base, m=m, qmf_src=qmf_src: e.matmul(
                out=PU[0:1, 0:512],
                lhsT=ARENA[base:base + 64, KVB_O + 1024 + (m // 2):KVB_O + 1024 + (m // 2) + 1],
                rhs=qmf_src, start=True, stop=True),
                waits=[kvb] + take("PU"), inc="t")
            n1 = vop(lambda e: e.tensor_scalar(out=XLN[0:1, 0:512], in0=PU[0:1, 0:512], scalar1=1e-6,
                                               scalar2=0.0, op0=OP.add, op1=OP.bypass),
                     [nm] + take("NR1") + take("XLN_q"))
            use("PU", n1)
            n2 = vop(lambda e: e.reciprocal(out=YST[0:1, 0:512], in_=XLN[0:1, 0:512]), [n1])
            use("XLN_q", n2)
            nb = top(lambda e: e.matmul(out=P2[:], lhsT=ONESR[0:1, :], rhs=YST[0:1, 0:512],
                                        start=True, stop=True),
                     waits=[n2] + take("P2"), inc="t")
            use("NR1", nb)
            n3 = aop(lambda e: e.copy(out=BC1[:], in_=P2[:]), [nb] + take("BC1") + take("BC"))
            use("P2", n3)
            oh = top(lambda e, base=base, m=m, qmf_src=qmf_src: e.matmul(
                out=P1[:],
                lhsT=ARENA[base:base + 64, KVB_O + (m // 2) * 128:KVB_O + (m // 2) * 128 + 128],
                rhs=qmf_src, start=True, stop=True),
                waits=take("P1"), inc="t")
            o1 = vop(lambda e, m=m, s=s: e.tensor_tensor(
                out=WK[:, m * T + s * 512:m * T + (s + 1) * 512],
                in0=P1[:], in1=BC1[:], op=OP.mult), [oh, n3])
            use("P1", o1)
            use("BC1", o1)
    attn_f = o1
    stats_done = {}

    def emit_stats(m, s, sl, a2):
        if s == 0:
            sum_out, sq_out = S[0:1, 0:512], PU[0:1, 0:512]
        else:
            sum_out, sq_out = KVP[0:1, 0:512], S[64:65, 0:512]
        sumw = top(lambda e, m=m, sl=sl, sum_out=sum_out: e.matmul(
            out=sum_out, lhsT=ONESF[:], rhs=XF[:, m * T:m * T + T][:, sl],
            start=(m == 0), stop=(m == H - 1), skip_group_check=True),
            waits=[a2] + (take("S_stat") if (m == 0 and s == 0) else []) +
                  (take("KVP") if (m == 0 and s == 1) else []),
            inc="t" if m == H - 1 else None)
        sq = aop(lambda e, m=m, sl=sl: e.activation(
            out=QB[:], in_=XF[:, m * T:m * T + T][:, sl], func=AF.Square),
            [a2] + take("QBD"))
        sq2 = top(lambda e, m=m, sq_out=sq_out: e.matmul(
            out=sq_out, lhsT=ONESF[:], rhs=QB[:],
            start=(m == 0), stop=(m == H - 1), skip_group_check=True),
            waits=[sq] + (take("PU") if (m == 0 and s == 0) else []), inc="t")
        use("QBD", sq2)
        if m == H - 1:
            stats_done[s] = (sumw, sq2)

    pend_stats = None
    wo_bufs = [(WOB_O, "WOB", "lwo"), (WQB_O, "WOB2", "lwo2")]
    for m in range(H):
        wo_o, wname, wsem = wo_bufs[m % 2]
        lo = dma("sync", wsem, ARENA[:, wo_o:wo_o + 2048],
                 ag_a_out[au("wo", m)].rearrange("p k c -> p (k c)"),
                 waits=take(wname) + (take("WQB") if m == 1 else []))
        for s in range(NS):
            sl = slice(s * 512, (s + 1) * 512)
            pd, pname = (P1, "P1") if (m * NS + s) % 2 == 0 else (P2, "P2")
            for k in range(KC):
                tw = top(lambda e, k=k, sl=sl, pd=pd, wo_o=wo_o: e.matmul(
                    out=pd[:], lhsT=ARENA[:, wo_o + k * 128:wo_o + (k + 1) * 128],
                    rhs=WK[:, k * T:k * T + T][:, sl], start=(k == 0), stop=(k == KC - 1)),
                    waits=([lo, attn_f] + take(pname)) if k == 0 else (),
                    inc="t" if k == KC - 1 else None)
            use(wname, tw)
            if pend_stats is not None:
                emit_stats(*pend_stats)
            a1 = aop(lambda e, m=m, pd=pd: e.activation(out=HSB[:], in_=pd[:],
                                                 func=AF.Identity,
                                                 bias=BO[:, m:m + 1], scale=1.0),
                     [tw] + take("HSB"))
            use(pname, a1)
            a2 = vop(lambda e, m=m, sl=sl: e.tensor_tensor(
                out=XF[:, m * T:m * T + T][:, sl], in0=XF[:, m * T:m * T + T][:, sl],
                in1=HSB[:], op=OP.add), [a1])
            use("HSB", a2)
            pend_stats = (m, s, sl, a2)
    attn_done = a2
    emit_stats(*pend_stats)

    gbuf = [GW0, GW1]
    ubuf = [UW0, UW1]

    def emit_deqx(s, k, tok):
        l2 = ln_apply(s, k, XLN[:], tok, take("XLN_q"))
        dq = quant512(XLN[:], WK[:, k * T + s * 512:k * T + (s + 1) * 512],
                      extra_waits=[l2] + ([attn_done, st_f] + take("QBD")
                                          if (s == 0 and k == 0) else []))
        use("XLN_q", dq)
        return dq

    tok0 = ln_setup(0, [attn_done],
                    pre_stats=stats_done[0] + (S[0:1, 0:512], PU[0:1, 0:512]))
    deqx_last = None
    for k in range(KC):
        deqx_last = emit_deqx(0, k, tok0)
    for s in range(NS):
        sl = slice(s * 512, (s + 1) * 512)
        for c in range(IC):
            gb, ub = gbuf[c % 2], ubuf[c % 2]
            lg = dma("sync", "gw%d" % (c % 2), FBB[:, gb:gb + 2048],
                     ag_f_out1[(c % 8) * 8 + c // 8].rearrange("p k c2 -> p (k c2)"),
                     waits=([cc_fs[0], deqx_last] if (s == 0 and c < 2) else []) + take("GB%d" % (c % 2)))
            lu = dma("sync", "uw%d" % (c % 2), FBB[:, ub:ub + 2048],
                     ag_f_out2[(c % 8) * 8 + c // 8].rearrange("p k c2 -> p (k c2)"),
                     waits=([cc_fs[1]] if (s == 0 and c < 2) else []) + take("UB%d" % (c % 2)))
            for k in range(KC):
                tg = top(lambda e, k=k, sl=sl, gb=gb: e.matmul(
                    out=PG[:], lhsT=FBB[:, gb + k * 128:gb + (k + 1) * 128],
                    rhs=WK[:, k * T:k * T + T][:, sl], start=(k == 0), stop=(k == KC - 1)),
                    waits=([lg, deqx_last] + take("PG")) if k == 0 else (),
                    inc="t" if k == KC - 1 else None)
            use("GB%d" % (c % 2), tg)
            for k in range(KC):
                tu = top(lambda e, k=k, sl=sl, ub=ub: e.matmul(
                    out=PU[:], lhsT=FBB[:, ub + k * 128:ub + (k + 1) * 128],
                    rhs=WK[:, k * T:k * T + T][:, sl], start=(k == 0), stop=(k == KC - 1)),
                    waits=([lu] + take("PU")) if k == 0 else (),
                    inc="t" if k == KC - 1 else None)
            use("UB%d" % (c % 2), tu)
            s1 = aop(lambda e: e.activation(out=HSB[:], in_=PG[:], func=AF.Silu),
                     [tg] + take("HSB"))
            use("PG", s1)
            s2 = vop(lambda e: e.tensor_tensor(out=XLN[:], in0=HSB[:], in1=PU[:], op=OP.mult),
                     [s1, tu] + take("XLN_q"))
            use("HSB", s2)
            use("PU", s2)
            hq = quant512(XLN[:], ARENA[:, c * 512:(c + 1) * 512],
                          extra_waits=[s2] + ([st_f] if (s == 0 and c == 0) else []) +
                          (take("DEQH") if c == 0 else []))
            use("XLN_q", hq)
        deqh_done = hq
        if s + 1 < NS:
            tok_n = ln_setup(s + 1, [], bc_ps=(PG, "PG", PU, "PU"),
                             pre_stats=stats_done[1] + (KVP[0:1, 0:512],
                                                        S[64:65, 0:512]))
        dwb_bufs = [DW0, GW0]
        for m in range(H):
            pd = P1 if (m % 2 == 0) else P2
            pname = "P1" if (m % 2 == 0) else "P2"
            for cg in range(4):
                dpar = (m * 4 + cg) % 2
                dwo = dwb_bufs[dpar]
                dsem = "dw" if dpar == 0 else "gw0"
                g_dw = m * 4 + cg
                ldw = dma("sync", dsem, FBB[:, dwo:dwo + 2048],
                          ag_f_out3[(g_dw % 8) * 8 + g_dw // 8].rearrange("p k c2 -> p (k c2)"),
                          waits=([deqh_done] if (m == 0 and cg == 0) else []) +
                                ([cc_fs[2]] if (s == 0 and m == 0 and cg < 2) else []) +
                                take("DWB%d" % dpar) + (take("GB0") if (m == 0 and cg == 1) else []))
                for cl in range(16):
                    c = cg * 16 + cl
                    td = top(lambda e, cl=cl, c=c, pd=pd, dwo=dwo: e.matmul(
                        out=pd[:], lhsT=FBB[:, dwo + cl * 128:dwo + (cl + 1) * 128],
                        rhs=ARENA[:, c * 512:(c + 1) * 512],
                        start=(c == 0), stop=(c == IC - 1)),
                        waits=([ldw] + (take(pname) if cg == 0 else [])) if cl == 0 else (),
                        inc="t" if (cl == 15) else None)
                use("DWB%d" % dpar, td)
            yv = vop(lambda e, m=m, sl=sl, pd=pd: e.tensor_tensor(
                out=YST[:], in0=pd[:], in1=XF[:, m * T:m * T + T][:, sl], op=OP.add),
                [td] + take("YST"))
            use(pname, yv)
            yst = dma("sync", "yo", y_t[m * 128:(m + 1) * 128, sl], YST[:], waits=[yv])
            use("YST", yst)
            if s + 1 < NS:
                deqx_last = emit_deqx(s + 1, m, tok_n)
        if s == 0:
            use("DEQH", td)

    final_tok = yst
    prog.append(("sync", (final_tok,), None, (None, 0)))

    eng_prog = {k: [] for k in ["sync", "scalar", "vector", "tensor", "gpsimd"]}
    for eng, waits, fn, inc in prog:
        eng_prog[eng].append((waits, fn, inc))

    with nc.Block() as block:
        def make_section(entries):
            def body(e):
                for waits, fn, (incname, step) in entries:
                    for (sname, val) in waits:
                        e.wait_ge(semh[sname], val)
                    if fn is not None:
                        inst = fn(e)
                        if incname is not None:
                            inst.then_inc(semh[incname], step)
            return body

        block.sync(make_section(eng_prog["sync"]))
        block.scalar(make_section(eng_prog["scalar"]))
        block.vector(make_section(eng_prog["vector"]))
        block.tensor(make_section(eng_prog["tensor"]))
        block.gpsimd(make_section(eng_prog["gpsimd"]))

    es.close()
    return nc


def _prep_inputs(ins):
    f32 = np.float32
    x = np.ascontiguousarray(ins["L0_attn_hidden_states"], dtype=f32).reshape(8192, D)
    wq = ins["L0_attn_wq"]
    wk = ins["L0_attn_wk"]
    wv = ins["L0_attn_wv"]
    wo = ins["L0_attn_wo"]

    def tile_w(w, rows):
        return np.ascontiguousarray(
            w.reshape(rows // 128, 128, w.shape[1] // 128, 128).transpose(0, 3, 2, 1),
            dtype=f32)

    wq_t = tile_w(wq, D)
    wk_t = tile_w(wk, D)
    wv_t = tile_w(wv, D)
    wo_t = tile_w(wo, D)
    attn_units = np.concatenate([wq_t, wk_t, wv_t, wo_t], axis=0)
    def bt32(a):
        n = a.shape[0]
        b = a.reshape(n, 4, 32, 16, 4, 32)
        return np.ascontiguousarray(b.transpose(0, 1, 5, 3, 4, 2)).reshape(a.shape)

    gw_t = tile_w(ins["L0_ffn_gate_w"], 8192)
    uw_t = tile_w(ins["L0_ffn_up_w"], 8192)
    dw_t = tile_w(ins["L0_ffn_down_w"], D)
    dw_u = np.ascontiguousarray(
        dw_t.reshape(16, 128, 4, 16, 128).transpose(0, 2, 1, 3, 4)).reshape(64, 128, 16, 128)
    ffn_units = bt32(np.concatenate([gw_t, uw_t, dw_u], axis=0))

    in_maps = []
    for c in range(8):
        xs = np.ascontiguousarray(x[c * T:(c + 1) * T].T)
        in_maps.append({
            "x_t": xs,
            "w_attn": np.ascontiguousarray(attn_units[c * AU_PC:(c + 1) * AU_PC]),
            "w_ffn": np.ascontiguousarray(ffn_units[c::8]),
            "bq": np.ascontiguousarray(ins["L0_attn_bq"], dtype=f32).reshape(16, 128).T.copy(),
            "bk": np.ascontiguousarray(ins["L0_attn_bk"], dtype=f32).reshape(16, 128).T.copy(),
            "bv": np.ascontiguousarray(ins["L0_attn_bv"], dtype=f32).reshape(16, 128).T.copy(),
            "bo": np.ascontiguousarray(ins["L0_attn_bo"], dtype=f32).reshape(16, 128).T.copy(),
            "wmq": np.ascontiguousarray(ins["L0_attn_wmap_q"].T, dtype=f32),
            "wmk": np.ascontiguousarray(ins["L0_attn_wmap_k"].T, dtype=f32),
        })
    return in_maps


def kernel(**inputs):
    in_maps = _prep_inputs(inputs)
    nc = build_nc()
    res_ = run_bass_kernel_spmd(nc, in_maps, core_ids=list(range(8)))
    outs = []
    for c in range(8):
        yt = np.asarray(res_.results[c]["y_t"])
        outs.append(np.ascontiguousarray(yt.T))
    y = np.concatenate(outs, axis=0).reshape(2, 4096, D).astype(np.float32)
    return y


if __name__ == "__main__":
    import reference as _r
    ins = _r.setup_inputs()
    out = kernel(**{k: np.asarray(v) for k, v in ins.items()})
    ref = np.asarray(_r.reference(**ins))
    err = np.abs(out - ref)
    print("max abs err:", err.max(), "rms rel:", float(np.sqrt((err ** 2).mean() / (ref ** 2).mean())))


# version 26 (final)
# speedup vs baseline: 1.0733x; 1.0021x over previous
import contextlib
import numpy as np

import concourse.bass as bass
import concourse.mybir as mybir
from concourse.bass_utils import run_bass_kernel_spmd

F32 = mybir.dt.float32
I32 = mybir.dt.int32
BF16 = mybir.dt.bfloat16
OP = mybir.AluOpType
AF = mybir.ActivationFunctionType
AX = mybir.AxisListType

MAGIC = 12582912.0
D = 2048
KC = 16
H = 16
T = 1024
NS = 2
IC = 64
N_AU = 64
N_FU = 192
AU_PC = N_AU // 8
FU_PC = N_FU // 8


def build_nc():
    nc = bass.Bass("TRN2", target_bir_lowering=False)

    x_t = nc.dram_tensor("x_t", [D, T], F32, kind="ExternalInput").ap()
    w_attn = nc.dram_tensor("w_attn", [AU_PC, 128, 16, 128], F32, kind="ExternalInput").ap()
    w_ffn = nc.dram_tensor("w_ffn", [FU_PC, 128, 16, 128], F32, kind="ExternalInput").ap()
    bq_d = nc.dram_tensor("bq", [128, 16], F32, kind="ExternalInput").ap()
    bk_d = nc.dram_tensor("bk", [128, 16], F32, kind="ExternalInput").ap()
    bv_d = nc.dram_tensor("bv", [128, 16], F32, kind="ExternalInput").ap()
    bo_d = nc.dram_tensor("bo", [128, 16], F32, kind="ExternalInput").ap()
    wmq_d = nc.dram_tensor("wmq", [128, 64], F32, kind="ExternalInput").ap()
    wmk_d = nc.dram_tensor("wmk", [128, 64], F32, kind="ExternalInput").ap()
    y_t = nc.dram_tensor("y_t", [D, T], F32, kind="ExternalOutput").ap()

    ag_a_in = nc.dram_tensor("ag_a_in", [AU_PC, 128, 16, 128], BF16).ap()
    ag_a_out = nc.dram_tensor("ag_a_out", [N_AU, 128, 16, 128], BF16).ap()
    ag_f_in = nc.dram_tensor("ag_f_in", [FU_PC, 128, 16, 128], BF16).ap()
    ag_f_out1 = nc.dram_tensor("ag_f_out1", [64, 128, 16, 128], BF16).ap()
    ag_f_out2 = nc.dram_tensor("ag_f_out2", [64, 128, 16, 128], BF16).ap()
    ag_f_out3 = nc.dram_tensor("ag_f_out3", [64, 128, 16, 128], BF16).ap()
    cc_in = nc.dram_tensor("cc_in", [128, 1032], F32).ap()
    cc_out = nc.dram_tensor("cc_out", [128, 1032], F32).ap()

    es = contextlib.ExitStack()

    def sb(name, shape, dt):
        return es.enter_context(nc.sbuf_tensor(name, shape, dt))

    XF = sb("XF", [128, KC * T], F32)
    WK = sb("WK", [128, KC * T], BF16)
    ARENA = sb("ARENA", [128, 32768], BF16)
    QMF_O, KMT_O = 0, 8192
    WQB_O, WKB_O, WVB_O, WOB_O = 16384, 18432, 20480, 22528
    KF_O, QF_O, VF_O, VT_O, KVB_O, WST_O = 24576, 25088, 25600, 26112, 26368, 27400
    FUST = sb("FUST", [128, 5120], F32)
    FBB = FUST[:].bitcast(BF16)
    GW0, GW1, UW0, UW1, DW0 = 0, 2048, 4096, 6144, 8192
    YST = sb("YST", [128, 512], F32)
    XLN = sb("XLN", [128, 512], F32)
    HSB = sb("HSB", [128, 512], F32)
    KVS = sb("KVS", [128, 1032], F32)
    BC1 = sb("BC1", [128, 512], F32)
    BC2 = sb("BC2", [128, 512], F32)
    QA = sb("QA", [128, 512], F32)
    QB = sb("QB", [128, 512], F32)
    QE = sb("QE", [128, 512], F32)
    QC = sb("QC", [128, 512], I32)
    QD = sb("QD", [128, 512], I32)
    QG = sb("QG", [128, 512], BF16)
    QM = sb("QM", [128, 16], F32)
    QS = sb("QS", [128, 16], F32)
    QR = sb("QR", [128, 16], F32)
    BQ = sb("BQ", [128, 16], F32)
    BK = sb("BK", [128, 16], F32)
    BV = sb("BV", [128, 16], F32)
    BO = sb("BO", [128, 16], F32)
    WMQ = sb("WMQ", [128, 64], BF16)
    WMK = sb("WMK", [128, 64], BF16)
    ONESF = sb("ONESF", [128, 1], F32)
    ONESR = sb("ONESR", [128, 128], F32)
    ONESB = sb("ONESB", [128, 1], BF16)
    EPSB = sb("EPSB", [128, 1], F32)
    MBIAS = sb("MBIAS", [128, 3], F32)
    KMH = sb("KMH", [128, 128], F32)
    IDF = sb("IDF", [128, 128], F32)
    VFF = YST

    P1 = es.enter_context(nc.psum_tensor("P1", [128, 512], F32))
    P2 = es.enter_context(nc.psum_tensor("P2", [128, 512], F32))
    PG = es.enter_context(nc.psum_tensor("PG", [128, 512], F32))
    PU = es.enter_context(nc.psum_tensor("PU", [128, 512], F32))
    KVP = es.enter_context(nc.psum_tensor("KVP", [128, 1024], F32))
    S = es.enter_context(nc.psum_tensor("S", [128, 512], F32))
    PT = es.enter_context(nc.psum_tensor("PT", [128, 256], F32))

    semh = {}
    for s_ in ["a", "v", "t", "gc", "cc",
               "x", "bl", "wm", "aga", "agf0", "agf1", "agf2", "ccd",
               "ust0", "ust1",
               "lwq", "lwk", "lwv", "lwo", "lwo2", "gw0", "gw1", "uw0", "uw1",
               "dw", "yo"]:
        semh[s_] = es.enter_context(nc.semaphore("sem_" + s_))
    cnt = {k: 0 for k in semh}
    prog = []

    def emit(eng, fn, waits=(), inc=None, step=1):
        waits = [w for w in waits if w is not None]
        tok = None
        if inc is not None:
            cnt[inc] += step
            tok = (inc, cnt[inc])
        prog.append((eng, tuple(waits), fn, (inc, step)))
        return tok

    def vop(fn, waits=()):
        return emit("vector", fn, waits, "v")

    def aop(fn, waits=()):
        return emit("scalar", fn, waits, "a")

    def gop(fn, waits=()):
        return emit("gpsimd", fn, waits, "gc")

    def top(fn, waits=(), inc=None):
        return emit("tensor", fn, waits, inc)

    def dma(eng, sem, out, in_, waits=()):
        return emit(eng, lambda e, out=out, in_=in_: e.dma_start(out=out, in_=in_),
                    waits, sem, 16)

    res = {}

    def use(name, tok):
        res.setdefault(name, []).append(tok)

    def take(name):
        return res.pop(name, [])

    qprev = [None]

    def quant512(src, dst, extra_waits=(), pre_t=False):
        qa3 = (src if pre_t else QA[:]).rearrange("p (b k) -> p b k", k=32)
        qb3 = QB[:].rearrange("p (b k) -> p b k", k=32)
        qg3 = QG[:].rearrange("p (b k) -> p b k", k=32)
        s_b = QS[:].unsqueeze(-1).broadcast_to((128, 16, 32))
        r_b = QR[:].unsqueeze(-1).broadcast_to((128, 16, 32))
        if pre_t:
            first_w = list(extra_waits) + ([qprev[0]] if qprev[0] else [])
        else:
            w0 = vop(lambda e, src=src: e.transpose(out=QA[:], in_=src),
                     list(extra_waits) + ([qprev[0]] if qprev[0] else []))
            first_w = [w0]
        w1 = vop(lambda e, qa3=qa3: e.tensor_reduce(out=QM[:], in_=qa3, axis=AX.X,
                                                    op=OP.max, apply_absolute_value=True),
                 first_w)
        w2 = vop(lambda e: e.tensor_scalar(out=QS[:], in0=QM[:], scalar1=float(1.0 / 448.0),
                                           scalar2=1e-30, op0=OP.mult, op1=OP.max), [w1])
        w3 = vop(lambda e: e.reciprocal(out=QR[:], in_=QS[:]), [w2])
        w4 = vop(lambda e, qa3=qa3, qb3=qb3, r_b=r_b: e.tensor_tensor(
            out=qb3, in0=qa3, in1=r_b, op=OP.mult), [w3])
        w5 = vop(lambda e: e.tensor_scalar(out=QC[:], in0=QB[:].bitcast(I32), scalar1=0x7F800000,
                                           scalar2=0, op0=OP.bitwise_and, op1=OP.bypass), [w4])
        w6 = vop(lambda e: e.tensor_scalar(out=QD[:], in0=QC[:], scalar1=-25165824.0,
                                           scalar2=989855744.0, op0=OP.add, op1=OP.max), [w5])
        w7 = vop(lambda e: e.tensor_scalar(out=QE[:], in0=QC[:], scalar1=-1,
                                           scalar2=2155872256.0, op0=OP.mult, op1=OP.add), [w5])
        w8 = vop(lambda e: e.tensor_scalar(out=QC[:], in0=QE[:], scalar1=1140850688.0,
                                           scalar2=0, op0=OP.min, op1=OP.bypass), [w7, w6])
        w9 = vop(lambda e: e.tensor_tensor(out=QA[:], in0=QB[:], in1=QC[:].bitcast(F32),
                                           op=OP.mult), [w8])
        w13 = aop(lambda e: e.activation(out=QA[:], in_=QA[:], func=AF.Identity,
                                         bias=MBIAS[:, 0:1], scale=1.0), [w9, mb1, mb2, mb2b])
        w14 = aop(lambda e: e.activation(out=QA[:], in_=QA[:], func=AF.Identity,
                                         bias=MBIAS[:, 2:3], scale=1.0), [w13])
        w15 = vop(lambda e: e.tensor_tensor(out=QB[:], in0=QA[:], in1=QD[:].bitcast(F32),
                                            op=OP.mult), [w14, w6])
        w16 = vop(lambda e, qb3=qb3, qg3=qg3, s_b=s_b: e.tensor_tensor(
            out=qg3, in0=qb3, in1=s_b, op=OP.mult), [w15])
        w17 = vop(lambda e, dst=dst: e.transpose(out=dst, in_=QG[:]), [w16])
        qprev[0] = w17
        return w17

    for k in range(KC):
        dma("scalar", "x", XF[:, k * T:(k + 1) * T], x_t[k * 128:(k + 1) * 128, :])
    x_ld = ("x", cnt["x"])
    dma("scalar", "bl", BQ[:, 0:16], bq_d)
    dma("scalar", "bl", BK[:, 0:16], bk_d)
    dma("scalar", "bl", BV[:, 0:16], bv_d)
    dma("scalar", "bl", BO[:, 0:16], bo_d)
    b_ld = ("bl", cnt["bl"])
    dma("gpsimd", "wm", WMQ[:], wmq_d)
    dma("gpsimd", "wm", WMK[:], wmk_d)
    wm_ld = ("wm", cnt["wm"])
    on1 = gop(lambda e: e.memset(ONESF[:], 1.0))
    on1b = gop(lambda e: e.memset(ONESR[:], 1.0))
    on2 = gop(lambda e: e.memset(ONESB[:], 1.0))
    mb1 = gop(lambda e: e.memset(MBIAS[:, 0:1], MAGIC))
    mb2 = gop(lambda e: e.memset(MBIAS[:, 1:2], -MAGIC / 512.0))
    mb2b = gop(lambda e: e.memset(MBIAS[:, 2:3], -MAGIC))
    eps1 = gop(lambda e: e.memset(EPSB[:], 1e-5))
    io1 = gop(lambda e: e.iota(HSB[:, 0:128], [[1, 128]], channel_multiplier=-1,
                               allow_small_or_imprecise_dtypes=True))
    id1 = vop(lambda e: e.tensor_scalar(out=IDF[:], in0=HSB[:, 0:128], scalar1=0.0,
                                        scalar2=0, op0=OP.is_equal, op1=OP.bypass), [io1])
    use("HSB", id1)

    st_prev = None
    for u in range(AU_PC):
        ld = dma("sync", "ust0", FUST[:, 0:2048].rearrange("p (k m) -> p k m", k=16),
                 w_attn[u], waits=take("UST0"))
        cv = vop(lambda e: e.tensor_copy(out=ARENA[:, WST_O:WST_O + 2048],
                                         in_=FUST[:, 0:2048]),
                 [ld] + take("WST"))
        use("UST0", cv)
        st = dma("sync", "aga", ag_a_in[u],
                 ARENA[:, WST_O:WST_O + 2048].rearrange("p (k m) -> p k m", k=16),
                 waits=[cv])
        use("WST", st)
        st_prev = st
    cc_a = emit("gpsimd", lambda e: e.collective_compute(
        "AllGather", OP.bypass, replica_groups=[list(range(8))],
        ins=[ag_a_in.opt()], outs=[ag_a_out.opt()]), [st_prev], "cc")

    ffn_lds = {}
    cc_fs = []

    def load_ffn_unit(u):
        o = 2048 * (u % 2)
        ffn_lds[u] = dma("sync", "ust%d" % (u % 2),
                         FUST[:, o:o + 2048].rearrange("p (k m) -> p k m", k=16),
                         w_ffn[u], waits=take("UST%d" % (u % 2)))

    def prep_ffn_unit(u):
        if u not in ffn_lds:
            load_ffn_unit(u)
        ld = ffn_lds[u]
        o = 2048 * (u % 2)
        wq = None
        for j in range(4):
            wq = quant512(FUST[:, o + j * 512:o + (j + 1) * 512],
                          ARENA[:, WST_O + j * 512:WST_O + (j + 1) * 512],
                          extra_waits=([ld] + take("WST")) if j == 0 else (),
                          pre_t=True)
            use("UST%d" % (u % 2), wq)
        st = dma("sync", "agf%d" % (u // 8), ag_f_in[u],
                 ARENA[:, WST_O:WST_O + 2048].rearrange("p (k m) -> p k m", k=16),
                 waits=[wq])
        use("WST", st)
        if u % 8 == 7 and u // 8 < 2:
            cc_fs.append(emit("gpsimd", lambda e, u=u: e.collective_compute(
                "AllGather", OP.bypass, replica_groups=[list(range(8))],
                ins=[ag_f_in[u - 7:u + 1].opt()],
                outs=[[ag_f_out1, ag_f_out2][u // 8].opt()]),
                [st], "cc"))
        return st

    def ln_setup(s, first_waits, bc_ps=None, pre_stats=None):
        pa, pan, pb, pbn = bc_ps or (P1, "P1", P2, "P2")
        sl = slice(s * 512, (s + 1) * 512)
        if pre_stats is not None:
            sum_t, sq_t, sum_src, sq_src = pre_stats
            r2 = 0 if s == 0 else 64
            ts1 = aop(lambda e, sum_src=sum_src: e.copy(out=XLN[0:1, 0:512], in_=sum_src),
                      [sum_t] + list(first_waits) + take("XLN_q"))
            if s == 0:
                ts2 = aop(lambda e, sq_src=sq_src: e.copy(out=BC2[0:1, 0:512], in_=sq_src),
                          [sq_t] + take("BC2"))
                use("S_stat", ts1)
                use("PU", ts2)
            else:
                ts2 = vop(lambda e, sq_src=sq_src: e.tensor_copy(
                    out=BC2[64:65, 0:512], in_=sq_src), [sq_t] + take("BC2"))
                use("KVP", ts1)
                use("S_stat", ts2)
            return ln_bcast(ts1, ts2, pa, pan, pb, pbn, r2)
        for k in range(KC):
            top(lambda e, k=k, sl=sl: e.matmul(
                out=S[0:1, 0:512], lhsT=ONESF[:],
                rhs=XF[:, k * T:k * T + T][:, sl],
                start=(k == 0), stop=(k == KC - 1)),
                waits=(list(first_waits) + [x_ld, on1, on1b] + take("S_stat")) if k == 0 else ())
        for k in range(KC):
            sq = aop(lambda e, k=k, sl=sl: e.activation(
                out=HSB[:], in_=XF[:, k * T:k * T + T][:, sl], func=AF.Square),
                take("HSB") + ((take("PU") + list(first_waits) + [x_ld]) if k == 0 else []))
            tk = top(lambda e, k=k: e.matmul(
                out=PU[0:1, 0:512], lhsT=ONESF[:], rhs=HSB[:],
                start=(k == 0), stop=(k == KC - 1)), waits=[sq], inc="t")
            use("HSB", tk)
        ts1 = aop(lambda e: e.copy(out=XLN[0:1, 0:512], in_=S[0:1, 0:512]),
                  [tk] + take("XLN_q"))
        ts2 = aop(lambda e: e.copy(out=BC2[0:1, 0:512], in_=PU[0:1, 0:512]),
                  [tk] + take("BC2"))
        use("S_stat", ts1)
        use("PU", ts2)
        return ln_bcast(ts1, ts2, pa, pan, pb, pbn, 0)

    def ln_bcast(ts1, ts2, pa, pan, pb, pbn, r2):
        b1 = top(lambda e, pa=pa: e.matmul(out=pa[0:128, 0:512], lhsT=ONESR[0:1, :],
                                           rhs=XLN[0:1, 0:512],
                                    start=True, stop=True), waits=[ts1] + take(pan), inc="t")
        use("XLN_q", b1)
        b2 = top(lambda e, pb=pb, r2=r2: e.matmul(out=pb[0:128, 0:512],
                                           lhsT=ONESR[r2:r2 + 1, :],
                                           rhs=BC2[r2:r2 + 1, 0:512],
                                    start=True, stop=True), waits=[ts2] + take(pbn), inc="t")
        use("BC2_row", b2)
        m1 = vop(lambda e, pa=pa: e.tensor_scalar(out=BC1[:], in0=pa[0:128, 0:512],
                                           scalar1=float(1.0 / D),
                                           scalar2=0.0, op0=OP.mult, op1=OP.add),
                 [b1] + take("BC1"))
        use(pan, m1)
        use("S_stat", m1)
        m2 = vop(lambda e, pb=pb: e.tensor_scalar(out=BC2[:], in0=pb[0:128, 0:512],
                                           scalar1=float(1.0 / D),
                                           scalar2=0.0, op0=OP.mult, op1=OP.add),
                 [b2] + take("BC2") + take("BC2_row"))
        use(pbn, m2)
        m3 = vop(lambda e: e.tensor_tensor(out=HSB[:], in0=BC1[:], in1=BC1[:], op=OP.mult),
                 [m1] + take("HSB"))
        m4 = vop(lambda e: e.tensor_tensor(out=XLN[:], in0=BC2[:], in1=HSB[:], op=OP.subtract),
                 [m3, m2] + take("XLN_q"))
        use("HSB", m4)
        m6 = aop(lambda e: e.activation(out=BC2[:], in_=XLN[:], func=AF.Sqrt,
                                        bias=EPSB[:, 0:1], scale=1.0), [m4, eps1])
        m7 = vop(lambda e: e.reciprocal(out=XLN[:], in_=BC2[:]), [m6])
        m8 = vop(lambda e: e.tensor_tensor(out=BC2[:], in0=BC1[:], in1=XLN[:], op=OP.mult), [m7])
        m9 = aop(lambda e: e.copy(out=BC1[:], in_=XLN[:]), [m7, m8])
        use("XLN_q", m9)
        use("XLN_q", m8)
        m10 = vop(lambda e: e.engine_nop(), [m8, m9])
        return m10

    def ln_apply(s, k, dst, ln_tok, dst_waits=()):
        sl = slice(s * 512, (s + 1) * 512)
        h1 = vop(lambda e, k=k, sl=sl: e.tensor_tensor(
            out=HSB[:], in0=XF[:, k * T:k * T + T][:, sl], in1=BC1[:], op=OP.mult),
            [ln_tok] + take("HSB") + list(dst_waits))
        h2 = vop(lambda e, dst=dst: e.tensor_tensor(out=dst, in0=HSB[:], in1=BC2[:],
                                                    op=OP.subtract), [h1])
        use("HSB", h2)
        use("BC1", h1)
        use("BC2", h2)
        return h2

    ln1_last = None
    for s in range(NS):
        tok = ln_setup(s, [])
        use("BC", tok)
        for k in range(KC):
            ln1_last = ln_apply(s, k, WK[:, k * T + s * 512:k * T + (s + 1) * 512], tok)
    ln1_all = ln1_last

    def au(mat, m):
        return {"wq": 0, "wk": 16, "wv": 32, "wo": 48}[mat] + m

    blocks = [(m, s) for m in range(H) for s in range(NS)]
    ablk = {}

    def emit_A(m, s):
        d = {}
        lq, lk, lv = ablk[(m, "loads")]
        qfo = 29448 if s == 0 else 29960
        kfo = KF_O if s == 0 else KF_O + 512
        vff = YST if s == 0 else XLN
        sl = slice(s * 512, (s + 1) * 512)
        for k in range(KC):
            tq = top(lambda e, k=k, sl=sl: e.matmul(
                out=P1[:], lhsT=ARENA[:, WQB_O + k * 128:WQB_O + (k + 1) * 128],
                rhs=WK[:, k * T:k * T + T][:, sl], start=(k == 0), stop=(k == KC - 1)),
                waits=([lq, ln1_all, b_ld] + take("P1")) if k == 0 else (),
                inc="t" if k == KC - 1 else None)
        if s == NS - 1:
            use("WQB", tq)
        eq = aop(lambda e, m=m, qfo=qfo: e.activation(out=ARENA[:, qfo:qfo + 512], in_=P1[:],
                                             func=AF.Identity, bias=BQ[:, m:m + 1],
                                             scale=1.0), [tq] + take("QF%d" % s))
        use("P1", eq)
        for k in range(KC):
            tk = top(lambda e, k=k, sl=sl: e.matmul(
                out=P2[:], lhsT=ARENA[:, WKB_O + k * 128:WKB_O + (k + 1) * 128],
                rhs=WK[:, k * T:k * T + T][:, sl], start=(k == 0), stop=(k == KC - 1)),
                waits=([lk] + take("P2")) if k == 0 else (),
                inc="t" if k == KC - 1 else None)
        if s == NS - 1:
            use("WKB", tk)
        ek = aop(lambda e, m=m, kfo=kfo: e.activation(out=ARENA[:, kfo:kfo + 512], in_=P2[:],
                                             func=AF.Identity, bias=BK[:, m:m + 1],
                                             scale=1.0), [tk] + take("KF%d" % s))
        use("P2", ek)
        for k in range(KC):
            tv = top(lambda e, k=k, sl=sl: e.matmul(
                out=PG[:], lhsT=ARENA[:, WVB_O + k * 128:WVB_O + (k + 1) * 128],
                rhs=WK[:, k * T:k * T + T][:, sl], start=(k == 0), stop=(k == KC - 1)),
                waits=([lv] + take("PG")) if k == 0 else (),
                inc="t" if k == KC - 1 else None)
        if s == NS - 1:
            use("WVB", tv)
        ev = aop(lambda e, m=m, vff=vff: e.activation(out=vff[0:128, 0:512], in_=PG[:],
                                             func=AF.Identity, bias=BV[:, m:m + 1],
                                             scale=1.0),
                 [tv] + take("VF%d" % s) + (take("XLN_q") if s == 1 else []))
        use("PG", ev)
        d.update(eq=eq, ek=ek, ev=ev)
        return d

    def emit_B(m, s, d):
        eq, ek, ev = d["eq"], d["ek"], d["ev"]
        qfo = 29448 if s == 0 else 29960
        kfo = KF_O if s == 0 else KF_O + 512
        vff = YST if s == 0 else XLN
        qm = top(lambda e, qfo=qfo: e.matmul(out=PU[0:64, 0:512], lhsT=WMQ[:],
                                    rhs=ARENA[:, qfo:qfo + 512], start=True, stop=True),
                 waits=[eq, wm_ld] + take("PU"), inc="t")
        use("QF%d" % s, qm)
        e1 = aop(lambda e: e.activation(out=HSB[0:64, :], in_=PU[0:64, 0:512],
                                        func=AF.Relu, scale=-1.0),
                 [qm] + take("HSB"))
        e2 = aop(lambda e: e.activation(out=HSB[0:64, :], in_=HSB[0:64, :], func=AF.Exp,
                                        scale=-1.0), [e1])
        qmf_dst = ARENA[64 * (m % 2):64 * (m % 2) + 64,
                        QMF_O + (m // 2) * 1024 + s * 512:QMF_O + (m // 2) * 1024 + (s + 1) * 512]
        e3 = vop(lambda e, qmf_dst=qmf_dst: e.scalar_tensor_tensor(
            out=qmf_dst, in0=PU[0:64, 0:512], scalar=0.0, in1=HSB[0:64, :],
            op0=OP.max, op1=OP.add), [e2])
        use("HSB", e3)
        use("PU", e3)
        for j in range(4):
            tau = s * 4 + j
            pt_sl = slice(128 * (j % 2), 128 * (j % 2) + 128)
            vt_sl = slice(VT_O + 128 * (j % 2), VT_O + 128 * (j % 2) + 128)
            kmp = PU if (j % 2 == 0) else PG
            kmpn = "PU" if (j % 2 == 0) else "PG"
            hsl = slice(64 * (j % 2), 64 * (j % 2) + 64)
            km = top(lambda e, j=j, kmp=kmp, kfo=kfo: e.matmul(
                out=kmp[0:128, 0:64], lhsT=ARENA[:, kfo + j * 128:kfo + (j + 1) * 128],
                rhs=WMK[:], start=True, stop=True),
                waits=[ek, wm_ld] + take(kmpn) + ([e3] if j == 0 else []) +
                      ([ev] if j == 1 else []), inc="t")
            use("KF%d" % s, km)
            f1 = aop(lambda e, kmp=kmp, hsl=hsl: e.activation(
                out=KMH[:, hsl], in_=kmp[0:128, 0:64], func=AF.Relu, scale=-1.0),
                [km] + take("KMH%d" % (j % 2)))
            f2 = aop(lambda e, hsl=hsl: e.activation(out=KMH[:, hsl], in_=KMH[:, hsl],
                                                     func=AF.Exp, scale=-1.0), [f1])
            kmt_dst = ARENA[:, KMT_O + tau * 1024 + m * 64:KMT_O + tau * 1024 + (m + 1) * 64]
            f3 = vop(lambda e, kmt_dst=kmt_dst, kmp=kmp, hsl=hsl: e.scalar_tensor_tensor(
                out=kmt_dst, in0=kmp[0:128, 0:64], scalar=0.0, in1=KMH[:, hsl],
                op0=OP.max, op1=OP.add), [f2])
            use("KMH%d" % (j % 2), f3)
            use(kmpn, f3)
            vt = top(lambda e, j=j, pt_sl=pt_sl, vff=vff: e.transpose(
                out=PT[:, pt_sl], in_=vff[0:128, j * 128:(j + 1) * 128],
                identity=IDF[:]), waits=[ev, id1] + take("PT%d" % (j % 2)), inc="t")
            use("VF%d" % s, vt)
            if s == 1:
                use("XLN_q", vt)
            g1 = aop(lambda e, pt_sl=pt_sl, vt_sl=vt_sl: e.copy(
                out=ARENA[:, vt_sl], in_=PT[:, pt_sl]),
                [vt] + take("VT%d" % (j % 2)))
            use("PT%d" % (j % 2), g1)
            base = 64 * (m % 2)
            kv_start = (m % 8 in (0, 1)) and tau == 0
            kv_stop = (m % 8 in (6, 7)) and tau == NS * 4 - 1
            kv_last = top(lambda e, base=base, m=m, kmt_dst=kmt_dst, vt_sl=vt_sl,
                          kv_start=kv_start, kv_stop=kv_stop: e.matmul(
                out=KVP[base:base + 64, (m // 2) * 128:(m // 2) * 128 + 128],
                lhsT=kmt_dst, rhs=ARENA[:, vt_sl],
                start=kv_start, stop=kv_stop, skip_group_check=True),
                waits=[f3, g1] + (take("KVP") if (m == 0 and tau == 0) else []),
                inc="t")
            use("VT%d" % (j % 2), kv_last)
            ks_start = m in (0, 1) and tau == 0
            ks_stop = m in (H - 2, H - 1) and tau == NS * 4 - 1
            ks_tok = top(lambda e, base=base, m=m, kmt_dst=kmt_dst,
                         ks_start=ks_start, ks_stop=ks_stop: e.matmul(
                out=S[base:base + 64, (m // 2):(m // 2) + 1],
                lhsT=kmt_dst, rhs=ONESB[:],
                start=ks_start, stop=ks_stop, skip_group_check=True),
                waits=[on2] if (m in (0, 1) and tau == 0) else (),
                inc="t" if (m == H - 1 and tau == NS * 4 - 1) else None)
        return ks_tok

    pend = None
    for i, (m, s) in enumerate(blocks):
        if s == 0:
            if m > 0:
                us = list(range((m - 1) * FU_PC // H, m * FU_PC // H))
            else:
                us = []
            qh = [us[:(len(us) + 1) // 2], us[(len(us) + 1) // 2:]]
            ablk[(m, "qh")] = qh
            lq = dma("sync", "lwq", ARENA[:, WQB_O:WQB_O + 2048],
                     ag_a_out[au("wq", m)].rearrange("p k c -> p (k c)"),
                     waits=([cc_a] if m == 0 else []) + take("WQB"))
            lk = dma("sync", "lwk", ARENA[:, WKB_O:WKB_O + 2048],
                     ag_a_out[au("wk", m)].rearrange("p k c -> p (k c)"), waits=take("WKB"))
            lv = dma("sync", "lwv", ARENA[:, WVB_O:WVB_O + 2048],
                     ag_a_out[au("wv", m)].rearrange("p k c -> p (k c)"), waits=take("WVB"))
            ablk[(m, "loads")] = (lq, lk, lv)
        for u in ablk[(m, "qh")][s]:
            st_f = prep_ffn_unit(u)
        ablk[(m, s)] = emit_A(m, s)
        if pend is not None:
            ks_tok = emit_B(*pend, ablk[pend])
        pend = (m, s)
        if s == NS - 1:
            for u in range(m * FU_PC // H, (m + 1) * FU_PC // H):
                load_ffn_unit(u)
    ks_tok = emit_B(*pend, ablk[pend])
    for u in range((H - 1) * FU_PC // H, FU_PC):
        st_f = prep_ffn_unit(u)
    kv_done = ks_tok

    kvev = aop(lambda e: e.copy(out=KVS[:, 0:1024], in_=KVP[:]), [kv_done])
    ksev = vop(lambda e: e.tensor_scalar(out=KVS[:, 1024:1032], in0=S[:, 0:8], scalar1=1.0,
                                         scalar2=0.0, op0=OP.mult, op1=OP.add), [kv_done])
    use("KVP", kvev)
    use("S_stat", ksev)
    st1 = dma("sync", "ccd", cc_in, KVS[:], waits=[kvev, ksev])
    ar = emit("gpsimd", lambda e: e.collective_compute(
        "AllReduce", OP.add, replica_groups=[[0, 1, 2, 3], [4, 5, 6, 7]],
        ins=[cc_in.opt()], outs=[cc_out.opt()]), [st1], "cc")
    ld1 = dma("sync", "ccd", KVS[:], cc_out, waits=[ar])
    kvb = vop(lambda e: e.tensor_copy(out=ARENA[:, KVB_O:KVB_O + 1032], in_=KVS[:]), [ld1])
    cc_fs.append(emit("gpsimd", lambda e: e.collective_compute(
        "AllGather", OP.bypass, replica_groups=[list(range(8))],
        ins=[ag_f_in[16:24].opt()], outs=[ag_f_out3.opt()]), [st_f], "cc"))

    dpar = [
        dict(prow=PU[0:1, 0:512], pkey="PU", xrow=XLN[0:1, 0:512], xkey="XLN_q",
             yrow=YST[0:1, 0:512], nrkey="NR1", onr=ONESR[0:1, :],
             pb=P2, pbkey="P2", bc=BC1, bckey="BC1", po=P1[:], pokey="P1"),
        dict(prow=S[64:65, 0:512], pkey="S64", xrow=XLN[64:65, 0:512], xkey="XLN64",
             yrow=YST[64:65, 0:512], nrkey="NR1b", onr=ONESR[64:65, :],
             pb=PG, pbkey="PG", bc=BC2, bckey="BC2", po=KVP[:, 0:512], pokey="KVP"),
    ]
    for m in range(H):
        base = 64 * (m % 2)
        for s in range(NS):
            it = m * NS + s
            dp = dpar[it % 2]
            qmf_src = ARENA[base:base + 64,
                            QMF_O + (m // 2) * 1024 + s * 512:QMF_O + (m // 2) * 1024 + (s + 1) * 512]
            nm = top(lambda e, base=base, m=m, qmf_src=qmf_src, dp=dp: e.matmul(
                out=dp["prow"],
                lhsT=ARENA[base:base + 64, KVB_O + 1024 + (m // 2):KVB_O + 1024 + (m // 2) + 1],
                rhs=qmf_src, start=True, stop=True, skip_group_check=True),
                waits=[kvb] + take(dp["pkey"]), inc="t")
            n1 = vop(lambda e, dp=dp: e.tensor_scalar(out=dp["xrow"], in0=dp["prow"],
                                               scalar1=1e-6,
                                               scalar2=0.0, op0=OP.add, op1=OP.bypass),
                     [nm] + take(dp["nrkey"]) + take(dp["xkey"]))
            use(dp["pkey"], n1)
            n2 = vop(lambda e, dp=dp: e.reciprocal(out=dp["yrow"], in_=dp["xrow"]), [n1])
            use(dp["xkey"], n2)
            nb = top(lambda e, dp=dp: e.matmul(out=dp["pb"][0:128, 0:512], lhsT=dp["onr"],
                                        rhs=dp["yrow"],
                                        start=True, stop=True),
                     waits=[n2] + take(dp["pbkey"]), inc="t")
            use(dp["nrkey"], nb)
            n3 = aop(lambda e, dp=dp: e.copy(out=dp["bc"][:], in_=dp["pb"][0:128, 0:512]),
                     [nb] + take(dp["bckey"]) + (take("BC") if it == 0 else []))
            use(dp["pbkey"], n3)
            oh = top(lambda e, base=base, m=m, qmf_src=qmf_src, dp=dp: e.matmul(
                out=dp["po"],
                lhsT=ARENA[base:base + 64, KVB_O + (m // 2) * 128:KVB_O + (m // 2) * 128 + 128],
                rhs=qmf_src, start=True, stop=True, skip_group_check=True),
                waits=take(dp["pokey"]), inc="t")
            o1 = vop(lambda e, m=m, s=s, dp=dp: e.tensor_tensor(
                out=WK[:, m * T + s * 512:m * T + (s + 1) * 512],
                in0=dp["po"], in1=dp["bc"][:], op=OP.mult), [oh, n3])
            use(dp["pokey"], o1)
            use(dp["bckey"], o1)
    attn_f = o1
    stats_done = {}

    def emit_stats(m, s, sl, a2):
        if s == 0:
            sum_out, sq_out = S[0:1, 0:512], PU[0:1, 0:512]
        else:
            sum_out, sq_out = KVP[0:1, 0:512], S[64:65, 0:512]
        sumw = top(lambda e, m=m, sl=sl, sum_out=sum_out: e.matmul(
            out=sum_out, lhsT=ONESF[:], rhs=XF[:, m * T:m * T + T][:, sl],
            start=(m == 0), stop=(m == H - 1), skip_group_check=True),
            waits=[a2] + (take("S_stat") if (m == 0 and s == 0) else []) +
                  (take("KVP") if (m == 0 and s == 1) else []),
            inc="t" if m == H - 1 else None)
        sq = aop(lambda e, m=m, sl=sl: e.activation(
            out=QB[:], in_=XF[:, m * T:m * T + T][:, sl], func=AF.Square),
            [a2] + take("QBD"))
        sq2 = top(lambda e, m=m, sq_out=sq_out: e.matmul(
            out=sq_out, lhsT=ONESF[:], rhs=QB[:],
            start=(m == 0), stop=(m == H - 1), skip_group_check=True),
            waits=[sq] + (take("PU") if (m == 0 and s == 0) else []), inc="t")
        use("QBD", sq2)
        if m == H - 1:
            stats_done[s] = (sumw, sq2)

    pend_stats = None
    wo_bufs = [(WOB_O, "WOB", "lwo"), (WQB_O, "WOB2", "lwo2")]
    for m in range(H):
        wo_o, wname, wsem = wo_bufs[m % 2]
        lo = dma("sync", wsem, ARENA[:, wo_o:wo_o + 2048],
                 ag_a_out[au("wo", m)].rearrange("p k c -> p (k c)"),
                 waits=take(wname) + (take("WQB") if m == 1 else []))
        for s in range(NS):
            sl = slice(s * 512, (s + 1) * 512)
            pd, pname = (P1, "P1") if (m * NS + s) % 2 == 0 else (P2, "P2")
            for k in range(KC):
                tw = top(lambda e, k=k, sl=sl, pd=pd, wo_o=wo_o: e.matmul(
                    out=pd[:], lhsT=ARENA[:, wo_o + k * 128:wo_o + (k + 1) * 128],
                    rhs=WK[:, k * T:k * T + T][:, sl], start=(k == 0), stop=(k == KC - 1)),
                    waits=([lo, attn_f] + take(pname)) if k == 0 else (),
                    inc="t" if k == KC - 1 else None)
            use(wname, tw)
            if pend_stats is not None:
                emit_stats(*pend_stats)
            a1 = aop(lambda e, m=m, pd=pd: e.activation(out=HSB[:], in_=pd[:],
                                                 func=AF.Identity,
                                                 bias=BO[:, m:m + 1], scale=1.0),
                     [tw] + take("HSB"))
            use(pname, a1)
            a2 = vop(lambda e, m=m, sl=sl: e.tensor_tensor(
                out=XF[:, m * T:m * T + T][:, sl], in0=XF[:, m * T:m * T + T][:, sl],
                in1=HSB[:], op=OP.add), [a1])
            use("HSB", a2)
            pend_stats = (m, s, sl, a2)
    attn_done = a2
    emit_stats(*pend_stats)

    gbuf = [GW0, GW1]
    ubuf = [UW0, UW1]

    def emit_deqx(s, k, tok):
        l2 = ln_apply(s, k, XLN[:], tok, take("XLN_q"))
        dq = quant512(XLN[:], WK[:, k * T + s * 512:k * T + (s + 1) * 512],
                      extra_waits=[l2] + ([attn_done, st_f] + take("QBD")
                                          if (s == 0 and k == 0) else []))
        use("XLN_q", dq)
        return dq

    tok0 = ln_setup(0, [attn_done],
                    pre_stats=stats_done[0] + (S[0:1, 0:512], PU[0:1, 0:512]))
    deqx_last = None
    for k in range(KC):
        deqx_last = emit_deqx(0, k, tok0)
    for s in range(NS):
        sl = slice(s * 512, (s + 1) * 512)
        for c in range(IC):
            gb, ub = gbuf[c % 2], ubuf[c % 2]
            lg = dma("sync", "gw%d" % (c % 2), FBB[:, gb:gb + 2048],
                     ag_f_out1[(c % 8) * 8 + c // 8].rearrange("p k c2 -> p (k c2)"),
                     waits=([cc_fs[0], deqx_last] if (s == 0 and c < 2) else []) + take("GB%d" % (c % 2)))
            lu = dma("sync", "uw%d" % (c % 2), FBB[:, ub:ub + 2048],
                     ag_f_out2[(c % 8) * 8 + c // 8].rearrange("p k c2 -> p (k c2)"),
                     waits=([cc_fs[1]] if (s == 0 and c < 2) else []) + take("UB%d" % (c % 2)))
            for k in range(KC):
                tg = top(lambda e, k=k, sl=sl, gb=gb: e.matmul(
                    out=PG[:], lhsT=FBB[:, gb + k * 128:gb + (k + 1) * 128],
                    rhs=WK[:, k * T:k * T + T][:, sl], start=(k == 0), stop=(k == KC - 1)),
                    waits=([lg, deqx_last] + take("PG")) if k == 0 else (),
                    inc="t" if k == KC - 1 else None)
            use("GB%d" % (c % 2), tg)
            for k in range(KC):
                tu = top(lambda e, k=k, sl=sl, ub=ub: e.matmul(
                    out=PU[:], lhsT=FBB[:, ub + k * 128:ub + (k + 1) * 128],
                    rhs=WK[:, k * T:k * T + T][:, sl], start=(k == 0), stop=(k == KC - 1)),
                    waits=([lu] + take("PU")) if k == 0 else (),
                    inc="t" if k == KC - 1 else None)
            use("UB%d" % (c % 2), tu)
            s1 = aop(lambda e: e.activation(out=HSB[:], in_=PG[:], func=AF.Silu),
                     [tg] + take("HSB"))
            use("PG", s1)
            s2 = vop(lambda e: e.tensor_tensor(out=XLN[:], in0=HSB[:], in1=PU[:], op=OP.mult),
                     [s1, tu] + take("XLN_q"))
            use("HSB", s2)
            use("PU", s2)
            hq = quant512(XLN[:], ARENA[:, c * 512:(c + 1) * 512],
                          extra_waits=[s2] + ([st_f] if (s == 0 and c == 0) else []) +
                          (take("DEQH") if c == 0 else []))
            use("XLN_q", hq)
        deqh_done = hq
        if s + 1 < NS:
            tok_n = ln_setup(s + 1, [], bc_ps=(PG, "PG", PU, "PU"),
                             pre_stats=stats_done[1] + (KVP[0:1, 0:512],
                                                        S[64:65, 0:512]))
        dwb_bufs = [DW0, GW0]
        for m in range(H):
            pd = P1 if (m % 2 == 0) else P2
            pname = "P1" if (m % 2 == 0) else "P2"
            for cg in range(4):
                dpar = (m * 4 + cg) % 2
                dwo = dwb_bufs[dpar]
                dsem = "dw" if dpar == 0 else "gw0"
                g_dw = m * 4 + cg
                ldw = dma("sync", dsem, FBB[:, dwo:dwo + 2048],
                          ag_f_out3[(g_dw % 8) * 8 + g_dw // 8].rearrange("p k c2 -> p (k c2)"),
                          waits=([deqh_done] if (m == 0 and cg == 0) else []) +
                                ([cc_fs[2]] if (s == 0 and m == 0 and cg < 2) else []) +
                                take("DWB%d" % dpar) + (take("GB0") if (m == 0 and cg == 1) else []))
                for cl in range(16):
                    c = cg * 16 + cl
                    td = top(lambda e, cl=cl, c=c, pd=pd, dwo=dwo: e.matmul(
                        out=pd[:], lhsT=FBB[:, dwo + cl * 128:dwo + (cl + 1) * 128],
                        rhs=ARENA[:, c * 512:(c + 1) * 512],
                        start=(c == 0), stop=(c == IC - 1)),
                        waits=([ldw] + (take(pname) if cg == 0 else [])) if cl == 0 else (),
                        inc="t" if (cl == 15) else None)
                use("DWB%d" % dpar, td)
            yv = vop(lambda e, m=m, sl=sl, pd=pd: e.tensor_tensor(
                out=YST[:], in0=pd[:], in1=XF[:, m * T:m * T + T][:, sl], op=OP.add),
                [td] + take("YST"))
            use(pname, yv)
            yst = dma("sync", "yo", y_t[m * 128:(m + 1) * 128, sl], YST[:], waits=[yv])
            use("YST", yst)
            if s + 1 < NS:
                deqx_last = emit_deqx(s + 1, m, tok_n)
        if s == 0:
            use("DEQH", td)

    final_tok = yst
    prog.append(("sync", (final_tok,), None, (None, 0)))

    eng_prog = {k: [] for k in ["sync", "scalar", "vector", "tensor", "gpsimd"]}
    for eng, waits, fn, inc in prog:
        eng_prog[eng].append((waits, fn, inc))

    with nc.Block() as block:
        def make_section(entries):
            def body(e):
                for waits, fn, (incname, step) in entries:
                    for (sname, val) in waits:
                        e.wait_ge(semh[sname], val)
                    if fn is not None:
                        inst = fn(e)
                        if incname is not None:
                            inst.then_inc(semh[incname], step)
            return body

        block.sync(make_section(eng_prog["sync"]))
        block.scalar(make_section(eng_prog["scalar"]))
        block.vector(make_section(eng_prog["vector"]))
        block.tensor(make_section(eng_prog["tensor"]))
        block.gpsimd(make_section(eng_prog["gpsimd"]))

    es.close()
    return nc


def _prep_inputs(ins):
    f32 = np.float32
    x = np.ascontiguousarray(ins["L0_attn_hidden_states"], dtype=f32).reshape(8192, D)
    wq = ins["L0_attn_wq"]
    wk = ins["L0_attn_wk"]
    wv = ins["L0_attn_wv"]
    wo = ins["L0_attn_wo"]

    def tile_w(w, rows):
        return np.ascontiguousarray(
            w.reshape(rows // 128, 128, w.shape[1] // 128, 128).transpose(0, 3, 2, 1),
            dtype=f32)

    wq_t = tile_w(wq, D)
    wk_t = tile_w(wk, D)
    wv_t = tile_w(wv, D)
    wo_t = tile_w(wo, D)
    attn_units = np.concatenate([wq_t, wk_t, wv_t, wo_t], axis=0)
    def bt32(a):
        n = a.shape[0]
        b = a.reshape(n, 4, 32, 16, 4, 32)
        return np.ascontiguousarray(b.transpose(0, 1, 5, 3, 4, 2)).reshape(a.shape)

    gw_t = tile_w(ins["L0_ffn_gate_w"], 8192)
    uw_t = tile_w(ins["L0_ffn_up_w"], 8192)
    dw_t = tile_w(ins["L0_ffn_down_w"], D)
    dw_u = np.ascontiguousarray(
        dw_t.reshape(16, 128, 4, 16, 128).transpose(0, 2, 1, 3, 4)).reshape(64, 128, 16, 128)
    ffn_units = bt32(np.concatenate([gw_t, uw_t, dw_u], axis=0))

    in_maps = []
    for c in range(8):
        xs = np.ascontiguousarray(x[c * T:(c + 1) * T].T)
        in_maps.append({
            "x_t": xs,
            "w_attn": np.ascontiguousarray(attn_units[c * AU_PC:(c + 1) * AU_PC]),
            "w_ffn": np.ascontiguousarray(ffn_units[c::8]),
            "bq": np.ascontiguousarray(ins["L0_attn_bq"], dtype=f32).reshape(16, 128).T.copy(),
            "bk": np.ascontiguousarray(ins["L0_attn_bk"], dtype=f32).reshape(16, 128).T.copy(),
            "bv": np.ascontiguousarray(ins["L0_attn_bv"], dtype=f32).reshape(16, 128).T.copy(),
            "bo": np.ascontiguousarray(ins["L0_attn_bo"], dtype=f32).reshape(16, 128).T.copy(),
            "wmq": np.ascontiguousarray(ins["L0_attn_wmap_q"].T, dtype=f32),
            "wmk": np.ascontiguousarray(ins["L0_attn_wmap_k"].T, dtype=f32),
        })
    return in_maps


def kernel(**inputs):
    in_maps = _prep_inputs(inputs)
    nc = build_nc()
    res_ = run_bass_kernel_spmd(nc, in_maps, core_ids=list(range(8)))
    outs = []
    for c in range(8):
        yt = np.asarray(res_.results[c]["y_t"])
        outs.append(np.ascontiguousarray(yt.T))
    y = np.concatenate(outs, axis=0).reshape(2, 4096, D).astype(np.float32)
    return y


if __name__ == "__main__":
    import reference as _r
    ins = _r.setup_inputs()
    out = kernel(**{k: np.asarray(v) for k, v in ins.items()})
    ref = np.asarray(_r.reference(**ins))
    err = np.abs(out - ref)
    print("max abs err:", err.max(), "rms rel:", float(np.sqrt((err ** 2).mean() / (ref ** 2).mean())))
